# version 06
# baseline (speedup 1.0000x reference)
import numpy as np
import ml_dtypes
import concourse.bass as bass
import concourse.mybir as mybir
from concourse.bass_utils import run_bass_kernel_spmd

F32 = mybir.dt.float32
BF16 = mybir.dt.bfloat16
FP8 = mybir.dt.float8e4
AF = mybir.ActivationFunctionType
ALU = mybir.AluOpType
AX = mybir.AxisListType.X

B, T, D = 8, 4096, 1024
H, Kd, Vd, CONV = 4, 256, 512, 4
KEY, VAL = 1024, 2048
TILE = 512
NT = T // TILE
C = 128
NCH = TILE // C
DCH = D // 128
VCH = VAL // 128
LN_EPS, EPS = 1e-5, 1e-6

ENGS = ("te", "ve", "se", "sy", "gp")

CW = 80
C_S1, C_SG2X, C_SGBX, C_SX2, C_SG2X2 = 0, 1, 2, 3, 4
C_SSQ, C_SSK, C_SSO = 5, 9, 13
C_SZ, C_SZ2, C_SY, C_SY2 = 17, 18, 19, 20
C_C, C_BETA = 21, 25
C_M, C_V, C_S, C_SP, C_R = 29, 30, 31, 32, 33
C_T1, C_T2, C_T3 = 34, 35, 36
SEL0 = 38
CS_S, CS_ROWA, CS_ROWB, CS_ROWD, CS_RC = 38, 39, 40, 41, 42
CS_B, CS_E = 43, 47
CS_X1, CS_X2 = 51, 52
C_LNRK, C_LNRQ, C_LNB, C_LA, C_EXPB, C_EXPD = 54, 58, 62, 66, 70, 74

P_S, P_ROWA, P_ROWB, P_ROWD, P_RC = 0, 1, 2, 3, 4
P_B, P_E = 5, 9
P_X1, P_X2 = 13, 14

RW_STAT = 0
RW_SSQ, RW_SSK, RW_SSO = 5, 9, 13
RW_SZ, RW_SZ2, RW_SY, RW_SY2 = 17, 18, 19, 20
RW_G, RW_BP = 32, 64


class Prog:
    def __init__(self):
        self.ops = []
        self.last_w = {}
        self.readers = {}

    def emit(self, eng, fn, reads=(), writes=(), dma=False, force_sig=False,
             inc=None):
        idx = len(self.ops)
        deps = set()
        for b in reads:
            w = self.last_w.get(b)
            if w is not None:
                deps.add(w)
        for b in writes:
            w = self.last_w.get(b)
            if w is not None:
                deps.add(w)
            for r in self.readers.get(b, ()):
                deps.add(r)
        deps.discard(idx)
        self.ops.append(dict(eng=eng, fn=fn, deps=sorted(deps),
                             sig=bool(force_sig),
                             inc=(inc if inc is not None else
                                  (16 if dma else 1)),
                             dma=dma,
                             skey=(("d_" + writes[0]) if (dma and writes)
                                   else eng)))
        for b in reads:
            self.readers.setdefault(b, []).append(idx)
        for b in writes:
            self.last_w[b] = idx
            self.readers[b] = []
        return idx

    def build(self, nc, sems, spacers=None, es=None):
        ops = self.ops
        for op in ops:
            for d in op["deps"]:
                if ops[d]["eng"] != op["eng"] or ops[d]["dma"]:
                    ops[d]["sig"] = True
        gfn = getattr(self, "groupfn", None)
        for op in ops:
            if op["dma"] and gfn is not None:
                op["sg"], op["tot"] = gfn(op["skey"])
            else:
                op["sg"], op["tot"] = op["skey"], False
        dkeys = sorted({op["sg"] for op in ops if op["dma"]})
        assert len(dkeys) + len(ENGS) <= 100, len(dkeys)
        import sys as _sys
        print(f"[kernel] dma sems: {len(dkeys)}", file=_sys.stderr)
        for k in dkeys:
            if k not in sems:
                sems[k] = es.enter_context(nc.semaphore(k))
        cum = {e: 0 for e in ENGS}
        cum.update({k: 0 for k in dkeys})
        val = [0] * len(ops)
        for i, op in enumerate(ops):
            if op["sig"]:
                cum[op["sg"]] += op["inc"]
            val[i] = cum[op["sg"]]
        for i, op in enumerate(ops):
            if op["tot"]:
                val[i] = cum[op["sg"]]
            for d in op["deps"]:
                assert not (op["tot"] and ops[d]["sg"] == op["sg"]), \
                    "total-group member depends on same-group member"
        per_eng = {e: [] for e in ENGS}
        for i, op in enumerate(ops):
            per_eng[op["eng"]].append(i)

        def run_engine(eng_name, eng_obj):
            waited = {}
            prev_i = None
            for i in per_eng[eng_name]:
                op = ops[i]
                need = {}
                for d in op["deps"]:
                    pe = ops[d]["eng"]
                    if (pe != eng_name or ops[d]["dma"]) and ops[d]["sig"]:
                        sk = ops[d]["sg"]
                        need[sk] = max(need.get(sk, 0), val[d])
                for sk, v in sorted(need.items()):
                    if waited.get(sk, 0) < v:
                        eng_obj.wait_ge(sems[sk], v)
                        waited[sk] = v
                if (spacers and eng_name in spacers and prev_i is not None
                        and prev_i in op["deps"] and not ops[prev_i]["dma"]):
                    spacers[eng_name](eng_obj)
                inst = op["fn"](eng_obj)
                if op["sig"]:
                    inst.then_inc(sems[op["sg"]], op["inc"])
                prev_i = i

        with nc.Block() as block:
            @block.tensor
            def _(e):
                run_engine("te", e)

            @block.vector
            def _(e):
                run_engine("ve", e)

            @block.scalar
            def _(e):
                run_engine("se", e)

            @block.sync
            def _(e):
                run_engine("sy", e)

            @block.gpsimd
            def _(e):
                run_engine("gp", e)
                for en in sorted(cum):
                    if cum[en] > 0:
                        e.wait_ge(sems[en], cum[en])
        return len(ops)


def build_program(upto="full", taps=()):
    taps = set(taps)
    nc = bass.Bass("TRN2", target_bir_lowering=False)
    p = Prog()

    _konst = {"d_" + n for n in (
        "idf", "idb", "onesr", "onescb", "negbig", "maskY", "maskM",
        "hotq", "hotk", "hotv", "selo", "e127", "onespad",
        "g1c", "b1c", "rmc", "g2c", "b2wc", "fbc", "b2c",
        "onwc", "dtbp", "wabp", "convq", "convk", "convv", "nega")}

    def _sgroup(k):
        if k in _konst:
            return "g_konst", True
        if k.startswith("d_x2w"):
            return "g_x2sp", False
        if k.startswith("d_lxw"):
            return "g_lxsp", False
        if k.startswith("d_outw"):
            return "g_outw", False
        if k.startswith("d_ln0w"):
            return "g_ln0sp", False
        if k.startswith("d_q8w"):
            return "g_q8sp", False
        if k.endswith("_tap"):
            return "g_tap", False
        return k, False
    p.groupfn = _sgroup

    def din(name, shape, dt=F32):
        return nc.dram_tensor(name, shape, dt, kind="ExternalInput").ap()

    x0 = din("x0", [D, T])
    wq = din("wq", [D, KEY]); wk = din("wk", [D, KEY])
    wv = din("wv", [D, VAL]); wg = din("wg", [D, VAL])
    wo = din("wo", [VAL, D]); w2 = din("w2", [D, D])
    wfT = din("wfT", [D, D])
    wabp = din("wabp", [128, 68 * DCH], BF16)
    cqw = din("cqw", [128, 4 * DCH])
    ckw = din("ckw", [128, 4 * DCH])
    cvw = din("cvw", [128, 4 * VCH])
    g1c = din("g1c", [128, DCH]); b1c = din("b1c", [128, DCH])
    rmc = din("rmc", [128, DCH]); g2c = din("g2c", [128, DCH])
    b2wc = din("b2wc", [128, DCH]); fbc = din("fbc", [128, DCH])
    b2c = din("b2c", [128, DCH])
    onwc = din("onwc", [128, 4])
    dtbp = din("dtbp", [128, 1]); alogp = din("alogp", [128, 1])
    idf = din("idf", [128, 128]); idb = din("idb", [128, 128], BF16)
    onesr = din("onesr", [1, 128])
    onescb = din("onescb", [128, 1], BF16)
    negbig = din("negbig", [128, 128], BF16)
    maskY4 = din("maskY4", [128, 512], BF16)
    maskM4 = din("maskM4", [128, 512], BF16)
    hotq = din("hotq", [128, 9 * DCH], BF16)
    hotk = din("hotk", [128, 13 * DCH], BF16)
    hotv = din("hotv", [128, 17 * VCH], BF16)
    selo_f = din("selo_f", [16, 16 * 128])
    e127 = din("e127", [128, 1])
    onespad = din("onespad", [128, 49])

    out_d = nc.dram_tensor("out", [D, T], F32, kind="ExternalOutput").ap()

    def dint(name, shape, dt=F32):
        return nc.dram_tensor(name, shape, dt, kind="Internal").ap()

    ln0sp = dint("ln0sp", [D, T])
    q8sp = dint("q8sp", [D, T], FP8)
    x2sp = dint("x2sp", [D, T])
    lnx2sp = dint("lnx2sp", [D, T])
    wq_b = dint("wq_b", [D, KEY], BF16); wk_b = dint("wk_b", [D, KEY], BF16)
    wv_b = dint("wv_b", [D, VAL], BF16); wg_b = dint("wg_b", [D, VAL], BF16)
    wo_b = dint("wo_b", [VAL, D], BF16); w2p_b = dint("w2p_b", [D, D], BF16)
    wf8sp = dint("wf8sp", [D, D], FP8)
    cc1i = nc.dram_tensor("cc1i", [1, 128], F32)
    cc1o = nc.dram_tensor("cc1o", [1, 128], F32)
    cc2i = nc.dram_tensor("cc2i", [1, 128], F32)
    cc2o = nc.dram_tensor("cc2o", [1, 128], F32)

    tap_outs = {}
    def mktap(name, shape, dt=F32):
        tap_outs[name] = nc.dram_tensor("tap_" + name, shape, dt,
                                        kind="ExternalOutput").ap()

    for nm in taps:
        if nm == "sca":
            mktap("sca", [1, 16])
        elif nm in ("x1", "h", "z", "x2f"):
            mktap(nm, [D, T], F32 if nm in ("x1", "z", "x2f") else F32)
        elif nm in ("q", "k"):
            mktap(nm, [KEY, T])
        elif nm in ("v", "o", "gt"):
            mktap(nm, [VAL, T])
        elif nm == "gb":
            mktap(nm, [128, T])
        elif nm == "enc":
            mktap(nm, [D, T])
        elif nm in ("dY0", "dX0", "dTT", "dMT", "dEhY", "dEhM"):
            mktap(nm, [128, 512], F32)
        elif nm in ("dRp", "dDl"):
            mktap(nm, [128, 768], F32)
        elif nm == "dKh":
            mktap(nm, [128, 1024], F32)
        elif nm == "dRblk":
            mktap(nm, [16, 128], F32)
        elif nm == "dCols":
            mktap(nm, [128, 80], F32)
        elif nm == "dS":
            mktap(nm, [128, 4096], F32)
        elif nm == "dWp":
            mktap(nm, [128, 512], F32)
        elif nm in ("dRst0", "dRst1", "dRs2"):
            mktap(nm, [128, 512], F32)
        elif nm == "dCar":
            mktap(nm, [128, 96], F32)
        elif nm in ("dPreK", "dPreQ"):
            mktap(nm, [128, 515], F32)
        elif nm in ("dCvK",):
            mktap(nm, [128, 512], F32)
        elif nm == "dLn0":
            mktap(nm, [D, T], F32)
        elif nm == "dQ8":
            mktap(nm, [D, T], FP8)

    from contextlib import ExitStack
    es = ExitStack()
    def S(name, shape, dt):
        return es.enter_context(nc.sbuf_tensor(name, shape, dt))
    def PS(name, shape, dt=F32):
        return es.enter_context(nc.psum_tensor(name, shape, dt))

    t_idf = S("t_idf", [128, 128], F32)
    t_idb = S("t_idb", [128, 128], BF16)
    t_onesr = S("t_onesr", [1, 128], F32)
    t_onescb = S("t_onescb", [128, 1], BF16)
    t_negbig = S("t_negbig", [128, 128], BF16)
    t_maskY = S("t_maskY", [128, 512], BF16)
    t_maskM = S("t_maskM", [128, 512], BF16)
    t_hotq = S("t_hotq", [128, 9 * DCH], BF16)
    t_hotk = S("t_hotk", [128, 13 * DCH], BF16)
    t_hotv = S("t_hotv", [128, 17 * VCH], BF16)
    t_selo = S("t_selo", [16, 16 * 128], F32)
    t_e127 = S("t_e127", [128, 1], F32)
    t_onespad = S("t_onespad", [128, 49], F32)
    t_g1c = S("t_g1c", [128, DCH], F32)
    t_b1c = S("t_b1c", [128, DCH], F32)
    t_rmc = S("t_rmc", [128, DCH], F32)
    t_g2c = S("t_g2c", [128, DCH], F32)
    t_b2wc = S("t_b2wc", [128, DCH], F32)
    t_b2c = S("t_b2c", [128, DCH], F32)
    t_gelb = S("t_gelb", [128, DCH], F32)
    t_fbc = S("t_fbc", [128, DCH], F32)
    t_onwc = S("t_onwc", [128, 4], F32)
    t_dtbp = S("t_dtbp", [128, 1], F32)
    t_nega = S("t_nega", [128, 1], F32)
    t_statq = S("t_statq", [128, 10 * DCH], BF16)
    t_statz = S("t_statz", [128, 44], BF16)
    t_selP = S("t_selP", [16, 128 * DCH], F32)
    t_selw2 = S("t_selw2", [16, 128 * DCH], F32)
    t_wabp = S("t_wabp", [128, 68 * DCH], BF16)
    t_convq = S("t_convq", [128, 4 * DCH], F32)
    t_convk = S("t_convk", [128, 4 * DCH], F32)
    t_convv = S("t_convv", [128, 4 * VCH], F32)
    t_eps5 = S("t_eps5", [128, 1], F32)
    t_eps6 = S("t_eps6", [128, 1], F32)
    t_one1 = S("t_one1", [128, 1], F32)
    t_ones36 = S("t_ones36", [36, TILE], F32)
    t_w2cs = S("t_w2cs", [1, D], F32)
    t_w2csc = S("t_w2csc", [128, DCH], F32)
    t_ucol = S("t_ucol", [128, DCH], F32)
    t_wcol = S("t_wcol", [128, DCH], F32)
    t_cst = S("t_cst", [1, 8], F32)
    t_cstb = S("t_cstb", [128, 4], F32)
    t_am1 = S("t_am1", [128, 32], F32)
    t_am2 = S("t_am2", [128, 56], F32)
    t_rowcc = S("t_rowcc", [1, 128], F32)
    t_s1x = S("t_s1x", [1, 16], F32)
    t_gcc = S("t_gcc", [1, 128], F32)
    t_sxh = S("t_sxh", [128, 1], F32)
    t_sx2h = S("t_sx2h", [128, 1], F32)
    t_f1 = S("t_f1", [128, 1], F32)
    t_f2 = S("t_f2", [128, 1], F32)
    t_swh = S("t_swh", [128, 1], F32)
    t_stg16 = S("t_stg16", [128, 16], F32)

    r_wsl = [S(f"r_wsl{i}", [128, 1024], BF16) for i in range(4)]
    r_wo2 = [S(f"r_wo2{i}", [128, 1024], BF16) for i in range(2)]
    r_f8 = [S(f"r_f8{i}", [128, 1024], FP8) for i in range(2)]
    r_wf = [w[:, :].bitcast(F32) for w in r_wo2]
    r_x0 = [S(f"r_x0{i}", [128, TILE], F32) for i in range(2)]
    r_32 = [S(f"r_32{i}", [128, TILE], F32) for i in range(2)]
    r_sq = [S(f"r_sq{i}", [128, TILE], BF16) for i in range(2)]
    r_ff = [S(f"r_ff{i}", [128, TILE], BF16) for i in range(2)]
    r_ga = [S(f"r_ga{i}", [128, TILE], BF16) for i in range(2)]
    r_dgv = [S(f"r_dgv{i}", [128, 128], BF16) for i in range(2)]
    r_pre = [S(f"r_pre{i}", [128, 64 + TILE], BF16) for i in range(2)]

    t_x1 = S("t_x1", [128, DCH * TILE], F32)
    t_z = S("t_z", [128, DCH * TILE], F32)
    t_zb = S("t_zb", [128, DCH * TILE], BF16)
    t_h = S("t_h", [128, DCH * TILE], BF16)
    t_q8 = S("t_q8", [128, DCH * TILE], FP8)
    t_q = S("t_q", [128, DCH * TILE], BF16)
    t_k = S("t_k", [128, DCH * TILE], BF16)
    t_v = S("t_v", [128, VCH * TILE], BF16)
    t_o = S("t_o", [128, VCH * TILE], BF16)
    t_carry = S("t_carry", [128, (2 * DCH + VCH) * 64], BF16)
    t_rstage = S("t_rstage", [128, TILE], F32)
    t_rstg2 = S("t_rstg2", [128, TILE], F32)
    r_cols = [S(f"r_cols{i}", [128, CW], F32) for i in range(2)]
    r_rblk = [S(f"r_rblk{i}", [16, 128], F32) for i in range(2)]
    t_sbc = S("t_sbc", [128, TILE], F32)
    t_eqb = S("t_eqb", [128, 128], F32)
    t_cc4 = S("t_cc4", [1, 4], F32)
    t_cCb = S("t_cCb", [128, 4], F32)
    t_gCc = S("t_gCc", [128, 4], F32)

    t_hrb = S("t_hrb", [16, TILE], F32)
    t_zrb = S("t_zrb", [16, TILE], F32)
    t_S = S("t_S", [128, H * 2 * Vd], BF16)
    r_X = [S(f"r_X{i}", [128, 4 * C], BF16) for i in range(2)]
    r_Y = [S(f"r_Y{i}", [128, 4 * C], BF16) for i in range(2)]
    r_Zs = [S(f"r_Zs{i}", [128, 4 * C], BF16) for i in range(2)]
    t_TT = S("t_TT", [128, 4 * C], BF16)
    t_MT = S("t_MT", [128, 4 * C], BF16)
    t_Eh = S("t_Eh", [128, 4 * C], BF16)
    r_Rp = [S(f"r_Rp{i}", [128, Vd + Kd], BF16) for i in range(2)]
    r_Dl = [S(f"r_Dl{i}", [128, Vd + Kd], BF16) for i in range(2)]
    t_Kh = S("t_Kh", [128, H * Kd], BF16)
    r_Wp = [S(f"r_Wp{i}", [128, 2 * Kd], BF16) for i in range(2)]
    t_g0 = S("t_g0", [128, 512], BF16)
    r_Qt = [S(f"r_Qt{i}", [128, C], BF16) for i in range(2)]

    pb0 = PS("pb0", [128, 512]); pb1 = PS("pb1", [128, 512])
    pb2 = PS("pb2", [128, 512]); pb3 = PS("pb3", [128, 512])
    pb4 = PS("pb4", [128, 512])
    pdel = PS("pdel", [128, 1024])
    pbt = PS("pbt", [128, 1024], BF16)

    sems = {e: es.enter_context(nc.semaphore("sem_" + e)) for e in ENGS}
    t_space = S("t_space", [1, 256], F32)

    def _sp(e):
        e.tensor_single_scalar(out=t_space[0:1, 0:128],
                               in_=t_space[0:1, 128:256],
                               scalar=1.0, op=ALU.mult)
        e.tensor_single_scalar(out=t_space[0:1, 128:256],
                               in_=t_space[0:1, 0:128],
                               scalar=1.0, op=ALU.mult)
    spacers = {"ve": _sp, "gp": _sp}

    def dma(eng, dst, src, reads, writes, force_sig=True):
        p.emit(eng, lambda e, d=dst, s=src: e.dma_start(out=d, in_=s),
               reads=reads, writes=writes, dma=True, force_sig=True)

    def mm(out, lhsT, rhs, start, stop, reads, writes):
        p.emit("te", lambda e, o=out, a=lhsT, b=rhs, s1=start, s2=stop:
               e.matmul(o, a, b, start=s1, stop=s2),
               reads=reads, writes=writes)

    def trans(out, in_, ident, reads, writes):
        p.emit("te", lambda e, o=out, i=in_, d=ident: e.transpose(o, i, d),
               reads=reads, writes=writes)

    def cp(eng, out, in_, reads, writes):
        p.emit(eng, lambda e, o=out, i=in_: e.tensor_copy(out=o, in_=i),
               reads=reads, writes=writes)

    def act(out, in_, fn, reads, writes, bias=0.0, scale=1.0):
        p.emit("se", lambda e, o=out, i=in_, f=fn, b=bias, s=scale:
               e.activation(out=o, in_=i, func=f, bias=b, scale=s),
               reads=reads, writes=writes)

    def tt(eng, out, a, b, op, reads, writes):
        p.emit(eng, lambda e, o=out, x=a, y=b, f=op:
               e.tensor_tensor(out=o, in0=x, in1=y, op=f),
               reads=reads, writes=writes)

    def ts(eng, out, a, s1, s2, op0, op1, reads, writes):
        if s2 is None:
            p.emit(eng, lambda e, o=out, x=a, u=s1, f0=op0:
                   e.tensor_single_scalar(out=o, in_=x, scalar=u, op=f0),
                   reads=reads, writes=writes)
        else:
            p.emit(eng, lambda e, o=out, x=a, u=s1, v=s2, f0=op0, f1=op1:
                   e.tensor_scalar(out=o, in0=x, scalar1=u, scalar2=v,
                                   op0=f0, op1=f1),
                   reads=reads, writes=writes)

    def stt(eng, out, in0, scalar, in1, op0, op1, reads, writes):
        p.emit(eng, lambda e, o=out, x=in0, s=scalar, y=in1, f0=op0, f1=op1:
               e.scalar_tensor_tensor(out=o, in0=x, scalar=s, in1=y,
                                      op0=f0, op1=f1),
               reads=reads, writes=writes)

    def red(out, in_, op, absv, reads, writes):
        p.emit("ve", lambda e, o=out, i=in_, f=op, a=absv:
               e.tensor_reduce(o, i, AX, f, apply_absolute_value=a),
               reads=reads, writes=writes)

    def recip(out, in_, reads, writes):
        p.emit("ve", lambda e, o=out, i=in_: e.reciprocal(out=o, in_=i),
               reads=reads, writes=writes)

    def mset(eng, ap_, v, writes):
        p.emit(eng, lambda e, a=ap_, x=v: e.memset(a, x), writes=writes)

    def scan(out, d0, d1, reads, writes):
        p.emit("ve", lambda e, o=out, a=d0, b=d1:
               e.tensor_tensor_scan(out=o, data0=a, data1=b, initial=0.0,
                                    op0=ALU.mult, op1=ALU.add),
               reads=reads, writes=writes)

    def pzero(pb, pname):
        p.emit("ve", lambda e, a=pb: e.memset(a[:, 0:512], 0.0),
               reads=(), writes=(pname,))

    def tap(name, src_ap, reads):
        if name in tap_outs:
            dma("gp", tap_outs[name], src_ap, reads=reads,
                writes=(name + "_tap",), force_sig=True)

    def selbc(out, pos, rblk, rname, start, stop, wname, extra_reads=()):
        mm(out, t_selo[:, 128 * pos:128 * (pos + 1)], rblk[0:16, :],
           start, stop, ("selo", rname) + tuple(extra_reads), (wname,))

    for dst, src, nm in [
        (t_idf, idf, "idf"), (t_idb, idb, "idb"), (t_onesr, onesr, "onesr"),
        (t_onescb, onescb, "onescb"), (t_negbig, negbig, "negbig"),
        (t_maskY, maskY4, "maskY"), (t_maskM, maskM4, "maskM"),
        (t_hotq, hotq, "hotq"), (t_hotk, hotk, "hotk"), (t_hotv, hotv, "hotv"),
        (t_selo, selo_f, "selo"), (t_e127, e127, "e127"),
        (t_onespad, onespad, "onespad"),
        (t_g1c, g1c, "g1c"), (t_b1c, b1c, "b1c"), (t_rmc, rmc, "rmc"),
        (t_g2c, g2c, "g2c"), (t_b2wc, b2wc, "b2wc"), (t_fbc, fbc, "fbc"),
        (t_b2c, b2c, "b2c"),
        (t_onwc, onwc, "onwc"), (t_dtbp, dtbp, "dtbp"),
        (t_wabp, wabp, "wabp"), (t_convq, cqw, "convq"),
        (t_convk, ckw, "convk"), (t_convv, cvw, "convv"),
    ]:
        dma("sy", dst[:, :], src, (), (nm,))
    dma("sy", t_nega[:, :], alogp, (), ("nega",))
    act(t_nega[32:36, :], t_nega[32:36, :], AF.Exp, ("nega",), ("nega",))
    ts("ve", t_nega[32:36, :], t_nega[32:36, :], -1.0, None, ALU.mult,
       ALU.mult, ("nega",), ("nega",))
    mset("ve", t_space[:, :], 0.0, ("space",))
    mset("gp", t_space[:, :], 0.0, ("space",))
    for tns, v, nm in [(t_eps5, LN_EPS, "eps5"), (t_eps6, EPS, "eps6"),
                       (t_one1, 1.0, "one1")]:
        mset("ve", tns[:, :], v, (nm,))
    mset("ve", t_ones36[:, :], 1.0, ("ones36",))
    mset("ve", t_S[:, :], 0.0, ("S0", "S1", "S2", "S3"))
    mset("ve", t_carry[:, :], 0.0, ("carry",))
    mset("ve", t_g0[:, :], 0.0, ("g0",))
    mset("ve", t_statq[:, :], 0.0, ("statq",))
    mset("ve", t_statz[:, :], 0.0, ("statz",))
    mset("ve", t_am1[:, :], 0.0, ("am1",))
    mset("ve", t_am2[:, :], 0.0, ("am2",))
    for i_ in range(2):
        mset("ve", r_cols[i_][:, :], 0.0, (f"rcols{i_}",))
        mset("ve", r_rblk[i_][:, :], 0.0, (f"rrblk{i_}",))
    mset("ve", t_hrb[:, :], 0.0, ("hrb",))
    mset("ve", t_zrb[:, :], 0.0, ("zrb",))
    mset("ve", t_rstage[:, :], 0.0, ("rstage",))
    mset("ve", t_rstg2[:, :], 0.0, ("rstg2",))
    mset("ve", t_sbc[:, :], 0.0, ("osbc", "sbc"))

    for c in range(DCH):
        b0 = 10 * c
        cp("ve", t_statq[:, b0:b0 + 1], t_onescb[:, :], ("onescb",), ("statq",))
        tt("ve", t_statq[:, b0 + 1:b0 + 2], t_g1c[:, c:c + 1],
           t_g1c[:, c:c + 1], ALU.mult, ("g1c",), ("statq",))
        tt("ve", t_statq[:, b0 + 2:b0 + 3], t_g1c[:, c:c + 1],
           t_b1c[:, c:c + 1], ALU.mult, ("g1c", "b1c"), ("statq",))
        cp("ve", t_statq[:, b0 + 8:b0 + 9], t_onescb[:, :], ("onescb",),
           ("statq",))
        tt("ve", t_statq[:, b0 + 9:b0 + 10], t_g1c[:, c:c + 1],
           t_g1c[:, c:c + 1], ALU.mult, ("g1c",), ("statq",))
    cp("ve", t_statz[:, 17:18], t_onescb[:, :], ("onescb",), ("statz",))
    cp("ve", t_statz[:, 18 + 18:18 + 19], t_onescb[:, :], ("onescb",),
       ("statz",))
    t_staty = S("t_staty", [128, 42], BF16)
    mset("ve", t_staty[:, :], 0.0, ("staty",))
    cp("ve", t_staty[:, 19:20], t_onescb[:, :], ("onescb",), ("staty",))
    cp("ve", t_staty[:, 20 + 20:20 + 21], t_onescb[:, :], ("onescb",),
       ("staty",))

    tt("ve", t_ucol[:, :], t_g1c[:, :], t_rmc[:, :], ALU.mult,
       ("g1c", "rmc"), ("ucol",))
    tt("ve", t_wcol[:, :], t_b1c[:, :], t_rmc[:, :], ALU.mult,
       ("b1c", "rmc"), ("wcol",))
    t_bcolb = S("t_bcolb", [128, 3 * DCH], BF16)
    for c in range(DCH):
        tt("ve", t_bcolb[:, 3 * c:3 * c + 1], t_g1c[:, c:c + 1],
           t_g1c[:, c:c + 1], ALU.mult, ("g1c",), ("bcolb",))
        tt("ve", t_bcolb[:, 3 * c + 1:3 * c + 2], t_g1c[:, c:c + 1],
           t_b1c[:, c:c + 1], ALU.mult, ("g1c", "b1c"), ("bcolb",))
        tt("ve", t_bcolb[:, 3 * c + 2:3 * c + 3], t_b1c[:, c:c + 1],
           t_b1c[:, c:c + 1], ALU.mult, ("b1c",), ("bcolb",))
    for j in range(3):
        for c in range(DCH):
            mm(pb3[0:1, 128 * j:128 * j + 1], t_bcolb[:, 3 * c + j:3 * c + j + 1],
               t_onescb[:, :], c == 0, c == DCH - 1, ("bcolb", "onescb"),
               ("pb3",))
    cp("ve", t_cst[:, 0:1], pb3[0:1, 0:1], ("pb3",), ("cst",))
    cp("ve", t_cst[:, 1:2], pb3[0:1, 128:129], ("pb3",), ("cst",))
    cp("ve", t_cst[:, 2:3], pb3[0:1, 256:257], ("pb3",), ("cst",))
    ts("ve", t_cst[:, 0:3], t_cst[:, 0:3], 1.0 / D, None, ALU.mult, ALU.mult,
       ("cst",), ("cst",))
    for j_ in range(3):
        mm(pb4[:, 132 + j_:133 + j_], t_onesr[:, :], t_cst[0:1, j_:j_ + 1],
           True, True, ("onesr", "cst"), ("pb4",))
    cp("ve", t_cstb[:, 0:3], pb4[:, 132:135], ("pb4",), ("cstb",))


    p.emit("ve", lambda e: e.memset(pdel[:, 0:1024], 0.0), reads=(),
           writes=("pdelw",))
    specs = [(wq, wq_b, KEY, False), (wk, wk_b, KEY, False),
             (wv, wv_b, VAL, False), (wg, wg_b, VAL, False),
             (wo, wo_b, D, False), (w2, w2p_b, D, True)]
    for si, (src, dst, width, isw2) in enumerate(specs):
        nrow = src.shape[0] // 128
        for rc in range(nrow):
            for hf in range(width // 512):
                ii = (rc * (width // 512) + hf) % 2
                sl, bn = r_wf[ii], f"rwo2{ii}"
                bsl, bbn = r_wsl[ii], f"rwsl{ii}"
                dma("sy", sl[:, 0:512], src[128 * rc:128 * (rc + 1),
                                            512 * hf:512 * (hf + 1)],
                    (bn,), (bn,))
                if isw2:
                    mm(pdel[0:1, 512 * (hf % 2):512 * (hf % 2) + 512],
                       t_b2c[:, rc:rc + 1], sl[:, 0:512],
                       False, False, ("b2c", bn), ("pdelw",))
                    ts("ve", bsl[:, 0:512], sl[:, 0:512],
                       t_g2c[:, rc:rc + 1], None,
                       ALU.mult, ALU.mult, (bn, "g2c"), (bbn,))
                else:
                    cp("ve", bsl[:, 0:512], sl[:, 0:512], (bn,), (bbn,))
                dma("gp", dst[128 * rc:128 * (rc + 1),
                              512 * hf:512 * (hf + 1)], bsl[:, 0:512],
                    (bbn,), (f"wh{si}",))
    cp("ve", t_w2cs[:, :], pdel[0:1, 0:1024], ("pdelw",), ("w2cs",))
    for c in range(DCH):
        trans(pb4[:, 0:1], t_w2cs[0:1, 128 * c:128 * (c + 1)],
              t_idf[0:1, 0:1], ("w2cs", "idf"), ("pb4",))
        cp("ve", t_gelb[:, c:c + 1], pb4[:, 0:1], ("pb4",), ("gelb",))
    for c in range(DCH):
        tt("ve", t_gelb[:, c:c + 1], t_gelb[:, c:c + 1], t_b2wc[:, c:c + 1],
           ALU.add, ("gelb", "b2wc"), ("gelb",))
    for rc in range(DCH):
        sl, bn = r_wsl[rc % 2], f"rwsl{rc % 2}"
        dma("sy", sl[:, :], w2p_b[128 * rc:128 * (rc + 1), 0:1024],
            (bn, "wh5"), (bn,))
        mm(pdel[0:1, 0:512], t_onescb[:, :], sl[:, 0:512], rc == 0, False,
           ("onescb", bn), ("pdel",))
        mm(pdel[0:1, 512:1024], t_onescb[:, :], sl[:, 512:1024], False,
           rc == DCH - 1, ("onescb", bn), ("pdel",))
    cp("ve", t_w2cs[:, :], pdel[0:1, 0:1024], ("pdel",), ("w2cs",))
    for c in range(DCH):
        trans(pb4[:, 0:1], t_w2cs[0:1, 128 * c:128 * (c + 1)],
              t_idf[0:1, 0:1], ("w2cs", "idf"), ("pb4",))
        cp("ve", t_w2csc[:, c:c + 1], pb4[:, 0:1], ("pb4",), ("w2csc",))

    for c in range(DCH):
        mset("ve", t_stg16[:, :], 0.0, ("stg16",))
        cp("ve", t_stg16[:, P_ROWB:P_ROWB + 1], t_ucol[:, c:c + 1],
           ("ucol",), ("stg16",))
        cp("ve", t_stg16[:, P_RC:P_RC + 1], t_wcol[:, c:c + 1],
           ("wcol",), ("stg16",))
        trans(pb4[0:16, 0:128], t_stg16[:, :], t_idf[:, :],
              ("stg16", "idf"), ("pb4",))
        cp("ve", t_selP[:, 128 * c:128 * (c + 1)], pb4[0:16, 0:128],
           ("pb4",), ("selP",))
        mset("ve", t_stg16[:, :], 0.0, ("stg16",))
        cp("ve", t_stg16[:, P_X2:P_X2 + 1], t_w2csc[:, c:c + 1],
           ("w2csc",), ("stg16",))
        trans(pb4[0:16, 0:128], t_stg16[:, :], t_idf[:, :],
              ("stg16", "idf"), ("pb4",))
        cp("ve", t_selw2[:, 128 * c:128 * (c + 1)], pb4[0:16, 0:128],
           ("pb4",), ("selw2",))

    for rc in range(DCH):
        for hf in range(2):
            sl, bn = r_wf[(2 * rc + hf) % 2], f"rwo2{(2 * rc + hf) % 2}"
            dma("sy", sl[:, 0:512], wfT[128 * rc:128 * (rc + 1),
                                        512 * hf:512 * (hf + 1)],
                (bn,), (bn,))
            red(t_am1[:, 1 + 2 * rc + hf:2 + 2 * rc + hf], sl[:, 0:512],
                ALU.max, True, (bn,), ("am1",))
    red(t_am1[:, 0:1], t_am1[:, 1:1 + 2 * DCH], ALU.max, False, ("am1",),
        ("am1c",))
    trans(pb4[0:32, 0:128], t_am1[:, 0:32], t_idf[:, :], ("am1c", "idf"),
          ("pb4",))
    cp("ve", t_rowcc[:, :], pb4[0:1, 0:128], ("pb4",), ("rowcc",))
    red(t_s1x[:, 0:1], t_rowcc[:, :], ALU.max, False, ("rowcc",), ("s1x",))
    ts("ve", t_s1x[:, 1:2], t_s1x[:, 0:1], 1e-12, None, ALU.max, ALU.max,
       ("s1x",), ("s1x",))
    recip(t_s1x[:, 2:3], t_s1x[:, 1:2], ("s1x",), ("s1x",))
    ts("ve", t_s1x[:, 3:4], t_s1x[:, 2:3], 448.0, 448.0, ALU.mult, ALU.min,
       ("s1x",), ("s1x",))
    ts("ve", t_s1x[:, 4:5], t_s1x[:, 3:4], 0.5, None, ALU.mult, ALU.mult,
       ("s1x",), ("s1x",))
    recip(t_s1x[:, 5:6], t_s1x[:, 3:4], ("s1x",), ("s1x",))
    mm(pb4[:, 128:129], t_onesr[:, :], t_s1x[:, 4:5], True, True,
       ("onesr", "s1x"), ("pb4",))
    cp("ve", t_swh[:, :], pb4[:, 128:129], ("pb4",), ("swh",))
    for rc in range(DCH):
        for hf in range(2):
            sl, bn = r_wf[(2 * rc + hf) % 2], f"rwo2{(2 * rc + hf) % 2}"
            q8b, qn = r_f8[(2 * rc + hf) % 2], f"rf8{(2 * rc + hf) % 2}"
            dma("sy", sl[:, 0:512], wfT[128 * rc:128 * (rc + 1),
                                        512 * hf:512 * (hf + 1)],
                (bn,), (bn,))
            act(q8b[:, 0:512], sl[:, 0:512], AF.Copy, (bn, "swh"), (qn,),
                scale=t_swh[:, :])
            dma("gp", wf8sp[128 * rc:128 * (rc + 1),
                            512 * hf:512 * (hf + 1)], q8b[:, 0:512], (qn,),
                ("wf8",))

    if upto == "setup":
        for oc in range(DCH):
            dma("gp", out_d[128 * oc:128 * (oc + 1), 0:TILE],
                t_rstage[:, 0:TILE], ("rstage",), ("outdummy",))
        n = p.build(nc, sems, spacers, es)
        import sys
        print(f"[kernel] built {n} ops (setup)", file=sys.stderr)
        es.close()
        return nc, tap_outs, p

    def x_3d(hbm, t0):
        return hbm.rearrange("(c p) t -> p c t", p=128)[:, :, t0:t0 + TILE]

    def sb_3d(tns):
        return tns[:, :].rearrange("p (c t) -> p c t", c=DCH)

    def phaseA_tile(ti):
        t0 = ti * TILE
        dma("sy", sb_3d(t_x1), x_3d(x0, t0), ("x1",), ("x1",))
        pzero(pb3, "pb3")
        for c in range(DCH):
            cs = slice(TILE * c, TILE * (c + 1))
            sq, sqn = r_32[c % 2], f"r32{c % 2}"
            tt("ve", sq[:, :], t_x1[:, cs], t_x1[:, cs], ALU.mult, ("x1",),
               (sqn,))
            mm(pb3[0:1, 0:TILE], t_onespad[:, 0:1], t_x1[:, cs],
               False, False, ("onespad", "x1"), ("pb3",))
            mm(pb3[0:4, 0:TILE], t_onespad[:, 4:8], sq[:, :],
               False, c == DCH - 1, ("onespad", sqn), ("pb3",))
        cp("ve", t_rstage[0:5, :], pb3[0:5, 0:TILE], ("pb3",), ("rstage",))
        for ci in range(NCH):
            cur = ci % 2
            col = r_cols[cur]
            coln = f"rcols{cur}"
            rblk = r_rblk[cur]
            rbn = f"rrblk{cur}"
            ccs = slice(ci * C, (ci + 1) * C)
            trans(pb4[:, 0:128], t_rstage[:, ccs], t_idf[:, :],
                  ("rstage", "idf"), ("pb4",))
            cp("ve", col[:, 0:21], pb4[:, 0:21], ("pb4",), (coln,))
            ts("ve", col[:, C_M:C_M + 1], col[:, C_S1:C_S1 + 1], 1.0 / D,
               None, ALU.mult, ALU.mult, (coln,), (coln,))
            ts("ve", col[:, C_T1:C_T1 + 1], col[:, C_SX2:C_SX2 + 1], 1.0 / D,
               None, ALU.mult, ALU.mult, (coln,), (coln,))
            tt("ve", col[:, C_T2:C_T2 + 1], col[:, C_M:C_M + 1],
               col[:, C_M:C_M + 1], ALU.mult, (coln,), (coln,))
            tt("ve", col[:, C_V:C_V + 1], col[:, C_T1:C_T1 + 1],
               col[:, C_T2:C_T2 + 1], ALU.subtract, (coln,), (coln,))
            act(col[:, C_T3:C_T3 + 1], col[:, C_V:C_V + 1], AF.Ln,
                (coln, "eps5"), (coln,), bias=t_eps5[:, :])
            act(col[:, CS_S:CS_S + 1], col[:, C_T3:C_T3 + 1], AF.Exp,
                (coln,), (coln,), scale=-0.5)
            tt("ve", col[:, CS_ROWD:CS_ROWD + 1], col[:, C_M:C_M + 1],
               col[:, CS_S:CS_S + 1], ALU.mult, (coln,), (coln,))
            ts("ve", col[:, CS_ROWD:CS_ROWD + 1], col[:, CS_ROWD:CS_ROWD + 1],
               -1.0, None, ALU.mult, ALU.mult, (coln,), (coln,))
            trans(pb4[0:16, 128:256], col[:, SEL0:SEL0 + 16], t_idf[:, :],
                  (coln, "idf"), ("pb4",))
            cp("ve", rblk[0:16, :], pb4[0:16, 128:256], ("pb4",), (rbn,))
            selbc(pb0[:, 0:C], P_S, rblk, rbn, True, True, "pb0")
            selbc(pb1[:, 0:C], P_ROWD, rblk, rbn, True, True, "pb1")
            for c in range(DCH):
                cs2 = slice(TILE * c + ci * C, TILE * c + (ci + 1) * C)
                tt("ve", t_z[:, cs2], t_x1[:, cs2], pb0[:, 0:C], ALU.mult,
                   ("x1", "pb0"), ("z",))
                tt("ve", t_z[:, cs2], t_z[:, cs2], pb1[:, 0:C], ALU.add,
                   ("z", "pb1"), ("z",))
        red(t_am1[:, 17 + ti:18 + ti], t_z[:, :], ALU.max, True, ("z",),
            (f"amT{ti}",))
        dma("gp", x_3d(ln0sp, t0), sb_3d(t_z), ("z",), (f"ln0w{ti}",),
            force_sig=True)

    def amax_collective(which):
        tam, amn = (t_am1, "ax1") if which == 1 else (t_am2, "ax2")
        cci, cco = (cc1i, cc1o) if which == 1 else (cc2i, cc2o)
        sx, tf = (t_sxh, t_f1) if which == 1 else (t_sx2h, t_f2)
        rd = tuple(f"amT{ti}" if which == 1 else f"bmT{ti}"
                   for ti in range(NT))
        red(tam[:, 0:1], tam[:, 17:17 + NT] if which == 1
            else tam[:, 8:8 + NT], ALU.max, False,
            rd + (("am1",) if which == 1 else ("am2",)), (amn + "c",))
        trans(pb4[0:32, 0:128], tam[:, 0:32], t_idf[:, :],
              (amn + "c", "idf", "am1" if which == 1 else "am2"),
              ("pb4",))
        cp("ve", t_rowcc[:, :], pb4[0:1, 0:128], ("pb4",), (amn + "r",))
        red(t_s1x[:, 6:7], t_rowcc[:, :], ALU.max, False, (amn + "r",),
            (amn + "v",))
        mset("gp", t_gcc[:, :], 0.0, (amn + "g",))
        cp("gp", t_gcc[:, 0:1], t_s1x[:, 6:7], (amn + "v",), (amn + "g",))
        dma("gp", cci.ap()[:, :], t_gcc[:, :], (amn + "g",), (amn + "i",))
        p.emit("gp", lambda e, ci_=cci, co_=cco: e.collective_compute(
            "AllReduce", ALU.max, replica_groups=[list(range(8))],
            ins=[ci_.ap().opt()], outs=[co_.ap().opt()]),
            reads=(amn + "i",), writes=(amn + "o",), dma=True, force_sig=True,
            inc=1)
        dma("gp", t_gcc[:, :], cco.ap()[:, :], (amn + "o",), (amn + "gg",))
        ts("ve", t_s1x[:, 7:8], t_gcc[0:1, 0:1], 1e-12, None, ALU.max,
           ALU.max, (amn + "gg",), (amn + "a",))
        recip(t_s1x[:, 8:9], t_s1x[:, 7:8], (amn + "a",), (amn + "b",))
        ts("ve", t_s1x[:, 9:10], t_s1x[:, 8:9], 448.0, 448.0, ALU.mult,
           ALU.min, (amn + "b",), (amn + "sx",))
        ts("ve", t_s1x[:, 10:11], t_s1x[:, 9:10], 0.5, None, ALU.mult,
           ALU.mult, (amn + "sx",), (amn + "sxh",))
        recip(t_s1x[:, 11:12], t_s1x[:, 9:10], (amn + "sx",), (amn + "rx",))
        tt("ve", t_s1x[:, 12:13], t_s1x[:, 11:12], t_s1x[:, 5:6], ALU.mult,
           (amn + "rx", "s1x"), (amn + "ff",))
        ts("ve", t_s1x[:, 12:13], t_s1x[:, 12:13], 4.0, None, ALU.mult,
           ALU.mult, (amn + "ff",), (amn + "ff",))
        mm(pb4[:, 129:130], t_onesr[:, :], t_s1x[:, 10:11], True, True,
           ("onesr", amn + "sxh"), ("pb4",))
        cp("ve", sx[:, :], pb4[:, 129:130], ("pb4",), ("sxh%d" % which,))
        mm(pb4[:, 130:131], t_onesr[:, :], t_s1x[:, 12:13], True, True,
           ("onesr", amn + "ff"), ("pb4",))
        cp("ve", tf[:, :], pb4[:, 130:131], ("pb4",), ("fd%d" % which,))

    def phaseA2_tile(ti):
        t0 = ti * TILE
        dma("sy", sb_3d(t_z), x_3d(ln0sp, t0),
            ("z",) + tuple(f"ln0w{t}" for t in range(NT)),
            ("z",))
        act(t_q8[:, :], t_z[:, :], AF.Copy, ("z", "sxh1"), ("q8t",),
            scale=t_sxh[:, :])
        dma("gp", x_3d(q8sp, t0).bitcast(FP8), sb_3d(t_q8), ("q8t",),
            (f"q8w{ti}",), force_sig=True)

    for ti in range(NT):
        phaseA_tile(ti)
    if upto == "A1":
        for oc in range(DCH):
            dma("gp", out_d[128 * oc:128 * (oc + 1), 0:TILE],
                t_rstage[:, 0:TILE], ("rstage",), ("outdummy",))
        n = p.build(nc, sems, spacers, es)
        import sys
        print(f"[kernel] built {n} ops (A1)", file=sys.stderr)
        es.close()
        return nc, tap_outs, p
    if upto != "A2nc":
        amax_collective(1)
    else:
        mset("ve", t_sxh[:, :], 0.001, ("sxh1",))
    for ti in range(NT):
        phaseA2_tile(ti)
    if upto == "A2nc":
        for oc in range(DCH):
            dma("gp", out_d[128 * oc:128 * (oc + 1), 0:TILE],
                t_rstage[:, 0:TILE], ("rstage",), ("outdummy",))
        n = p.build(nc, sems, spacers, es)
        import sys
        print(f"[kernel] built {n} ops (A2nc)", file=sys.stderr)
        es.close()
        return nc, tap_outs, p

    if "sca" in taps:
        tap("sca", t_s1x[:, :], ("fd1", "sxh1", "s1x"))

    if upto == "A":
        for oc in range(DCH):
            dma("gp", out_d[128 * oc:128 * (oc + 1), 0:TILE], t_z[:, 0:TILE],
                ("z",), ("outdummy",))
        n = p.build(nc, sems, spacers, es)
        import sys
        print(f"[kernel] built {n} ops (upto=A)", file=sys.stderr)
        es.close()
        return nc, tap_outs, p

    def w_slab(dst, dstname, whbm, whname, oc, nic, wid):
        src = whbm.rearrange("(ic p) o -> p ic o", p=128)[
            :, :, 128 * oc:128 * (oc + 1)]
        dma("sy" if oc % 2 == 0 else "gp",
            dst[:, 0:nic * 128].rearrange("p (ic j) -> p ic j", ic=nic),
            src, (dstname, whname), (dstname,))

    def ffn_stage(ti, q8tile, q8name, resid_ring, residname, dq_col, dqn,
                  xout, xoutname, wsp, wspname):
        for oc in range(DCH):
            sl, sn = r_f8[oc % 2], f"rf8{oc % 2}"
            w_slab(sl, sn, wsp, wspname, oc, DCH, D)
            pb, pn = (pb0, "pb0") if oc % 2 == 0 else (pb1, "pb1")
            for ic in range(DCH):
                mm(pb[:, 0:TILE], sl[:, 128 * ic:128 * (ic + 1)],
                   q8tile[:, TILE * ic:TILE * (ic + 1)], ic == 0,
                   ic == DCH - 1, (sn, q8name), (pn,))
            ff, fn_ = r_ff[oc % 2], f"rff{oc % 2}"
            ts("ve", ff[:, :], pb[:, 0:TILE], dq_col[:, :],
               t_fbc[:, oc:oc + 1], ALU.mult, ALU.add,
               (pn, dqn, "fbc"), (fn_,))
            rx = resid_ring(oc)
            stt("ve", xout[:, TILE * oc:TILE * (oc + 1)], ff[:, :], 0.5,
                rx[0], ALU.mult, ALU.add, (fn_, rx[1]),
                (xoutname,))

    def stats_group(rhs_t, rhsname, kind):
        for c in range(DCH):
            cs = slice(TILE * c, TILE * (c + 1))
            xb, xbn = r_ff[c % 2], f"rff{c % 2}"
            sq, sqn = r_sq[c % 2], f"rsq{c % 2}"
            cp("ve", xb[:, :], rhs_t[:, cs], (rhsname,), (xbn,))
            tt("ve", sq[:, :], xb[:, :], xb[:, :], ALU.mult, (xbn,), (sqn,))
            if kind == "x1":
                mm(pb3[0:5, 0:TILE], t_statq[:, 10 * c:10 * c + 5], xb[:, :],
                   False, False, ("statq", xbn), ("pb3",))
                mm(pb3[0:5, 0:TILE], t_statq[:, 10 * c + 5:10 * c + 10],
                   sq[:, :], False, False, ("statq", sqn), ("pb3",))
            elif kind == "z":
                mm(pb3[0:18, 0:TILE], t_statz[:, 0:18], xb[:, :],
                   False, False, ("statz", xbn), ("pb3",))
                mm(pb3[0:19, 0:TILE], t_statz[:, 18:37], sq[:, :],
                   False, False, ("statz", sqn), ("pb3",))
            else:
                mm(pb3[0:20, 0:TILE], t_staty[:, 0:20], xb[:, :],
                   False, False, ("staty", xbn), ("pb3",))
                mm(pb3[0:21, 0:TILE], t_staty[:, 20:41], sq[:, :],
                   False, False, ("staty", sqn), ("pb3",))

    def stats_group_bf(rhs_t, rhsname, kind):
        for c in range(DCH):
            cs = slice(TILE * c, TILE * (c + 1))
            sq, sqn = r_sq[c % 2], f"rsq{c % 2}"
            tt("ve", sq[:, :], rhs_t[:, cs], rhs_t[:, cs], ALU.mult,
               (rhsname,), (sqn,))
            mm(pb3[0:18, 0:TILE], t_statz[:, 0:18], rhs_t[:, cs],
               False, False, ("statz", rhsname), ("pb3",))
            mm(pb3[0:19, 0:TILE], t_statz[:, 18:37], sq[:, :],
               False, False, ("statz", sqn), ("pb3",))

    def sweep1_tile(ti):
        t0 = ti * TILE
        TL = f"t{ti}_"
        dma("sy", sb_3d(t_q8), x_3d(q8sp, t0).bitcast(FP8),
            ("q8t",) + tuple(f"q8w{t}" for t in range(NT)), ("q8t",))
        def resid_x0(oc):
            rx, rxn = r_x0[oc % 2], f"rx0{oc % 2}"
            dma("sy", rx[:, :], x0[128 * oc:128 * (oc + 1), t0:t0 + TILE],
                (rxn,), (rxn,))
            return rx[:, :], rxn
        ffn_stage(ti, t_q8, "q8t", resid_x0, None, t_f1, "fd1", t_x1, "x1",
                  wf8sp, "wf8")
        if upto == "S1f":
            return
        if "x1" in tap_outs:
            dma("gp", x_3d(tap_outs["x1"], t0), sb_3d(t_x1), ("x1",),
                ("x1tap",), force_sig=True)
        pzero(pb3, "pb3")
        stats_group(t_x1, "x1", "x1")
        cp("ve", t_rstage[0:5, :], pb3[0:5, 0:TILE], ("pb3",), ("rstage",))
        hrblks = []
        for ci in range(NCH):
            cur = ci % 2
            col, coln = r_cols[cur], f"rcols{cur}"
            rblk, rbn = r_rblk[cur], f"rrblk{cur}"
            ccs = slice(ci * C, (ci + 1) * C)
            trans(pb4[:, 0:128], t_rstage[:, ccs], t_idf[:, :],
                  ("rstage", "idf"), ("pb4",))
            cp("ve", col[:, 0:21], pb4[:, 0:21], ("pb4",), (coln,))
            ts("ve", col[:, C_M:C_M + 1], col[:, C_S1:C_S1 + 1], 1.0 / D,
               None, ALU.mult, ALU.mult, (coln,), (coln,))
            ts("ve", col[:, C_T1:C_T1 + 1], col[:, C_SX2:C_SX2 + 1], 1.0 / D,
               None, ALU.mult, ALU.mult, (coln,), (coln,))
            tt("ve", col[:, C_T2:C_T2 + 1], col[:, C_M:C_M + 1],
               col[:, C_M:C_M + 1], ALU.mult, (coln,), (coln,))
            tt("ve", col[:, C_V:C_V + 1], col[:, C_T1:C_T1 + 1],
               col[:, C_T2:C_T2 + 1], ALU.subtract, (coln,), (coln,))
            act(col[:, C_T3:C_T3 + 1], col[:, C_V:C_V + 1], AF.Ln,
                (coln, "eps5"), (coln,), bias=t_eps5[:, :])
            act(col[:, CS_S:CS_S + 1], col[:, C_T3:C_T3 + 1], AF.Exp,
                (coln,), (coln,), scale=-0.5)
            tt("ve", col[:, C_T1:C_T1 + 1], col[:, CS_S:CS_S + 1],
               col[:, CS_S:CS_S + 1], ALU.mult, (coln,), (coln,))
            tt("ve", col[:, C_T2:C_T2 + 1], col[:, C_V:C_V + 1],
               col[:, C_T1:C_T1 + 1], ALU.mult, (coln,), (coln,))
            act(col[:, C_T3:C_T3 + 1], col[:, C_T2:C_T2 + 1], AF.Ln,
                (coln, "eps5"), (coln,), bias=t_eps5[:, :])
            act(col[:, C_SP:C_SP + 1], col[:, C_T3:C_T3 + 1], AF.Exp,
                (coln,), (coln,), scale=-0.5)
            ts("ve", col[:, C_T1:C_T1 + 1], col[:, C_SG2X:C_SG2X + 1],
               1.0 / D, None, ALU.mult, ALU.mult, (coln,), (coln,))
            tt("ve", col[:, C_T2:C_T2 + 1], col[:, C_M:C_M + 1],
               col[:, C_T1:C_T1 + 1], ALU.mult, (coln,), (coln,))
            ts("ve", col[:, C_T1:C_T1 + 1], col[:, C_SG2X2:C_SG2X2 + 1],
               1.0 / D, None, ALU.mult, ALU.mult, (coln,), (coln,))
            stt("ve", col[:, C_T2:C_T2 + 1], col[:, C_T2:C_T2 + 1], -2.0,
                col[:, C_T1:C_T1 + 1], ALU.mult, ALU.add, (coln,), (coln,))
            tt("ve", col[:, C_T1:C_T1 + 1], col[:, C_M:C_M + 1],
               col[:, C_M:C_M + 1], ALU.mult, (coln,), (coln,))
            stt("ve", col[:, C_T2:C_T2 + 1], col[:, C_T1:C_T1 + 1],
                t_cstb[:, 0:1], col[:, C_T2:C_T2 + 1], ALU.mult, ALU.add,
                (coln, "cstb"), (coln,))
            tt("ve", col[:, C_T1:C_T1 + 1], col[:, CS_S:CS_S + 1],
               col[:, CS_S:CS_S + 1], ALU.mult, (coln,), (coln,))
            tt("ve", col[:, C_T2:C_T2 + 1], col[:, C_T2:C_T2 + 1],
               col[:, C_T1:C_T1 + 1], ALU.mult, (coln,), (coln,))
            ts("ve", col[:, C_T1:C_T1 + 1], col[:, C_SGBX:C_SGBX + 1],
               1.0 / D, None, ALU.mult, ALU.mult, (coln,), (coln,))
            stt("ve", col[:, C_T3:C_T3 + 1], col[:, C_M:C_M + 1],
                t_cstb[:, 1:2], col[:, C_T1:C_T1 + 1], ALU.mult,
                ALU.subtract, (coln, "cstb"), (coln,))
            tt("ve", col[:, C_T3:C_T3 + 1], col[:, C_T3:C_T3 + 1],
               col[:, CS_S:CS_S + 1], ALU.mult, (coln,), (coln,))
            tt("ve", col[:, C_T1:C_T1 + 1], col[:, C_SP:C_SP + 1],
               col[:, C_SP:C_SP + 1], ALU.mult, (coln,), (coln,))
            tt("ve", col[:, C_T2:C_T2 + 1], col[:, C_T2:C_T2 + 1],
               col[:, C_T1:C_T1 + 1], ALU.mult, (coln,), (coln,))
            tt("ve", col[:, C_T3:C_T3 + 1], col[:, C_T3:C_T3 + 1],
               col[:, C_SP:C_SP + 1], ALU.mult, (coln,), (coln,))
            stt("ve", col[:, C_T2:C_T2 + 1], col[:, C_T3:C_T3 + 1], -2.0,
                col[:, C_T2:C_T2 + 1], ALU.mult, ALU.add, (coln,), (coln,))
            ts("ve", col[:, C_T2:C_T2 + 1], col[:, C_T2:C_T2 + 1], 1.0,
               t_cstb[:, 2:3], ALU.mult, ALU.add, (coln, "cstb"), (coln,))
            act(col[:, C_T3:C_T3 + 1], col[:, C_T2:C_T2 + 1], AF.Ln,
                (coln, "eps6"), (coln,), bias=t_eps6[:, :])
            act(col[:, C_R:C_R + 1], col[:, C_T3:C_T3 + 1], AF.Exp,
                (coln,), (coln,), scale=-0.5)
            cp("ve", col[:, CS_RC:CS_RC + 1], col[:, C_R:C_R + 1], (coln,),
               (coln,))
            tt("ve", col[:, C_T1:C_T1 + 1], col[:, CS_S:CS_S + 1],
               col[:, C_SP:C_SP + 1], ALU.mult, (coln,), (coln,))
            tt("ve", col[:, CS_ROWA:CS_ROWA + 1], col[:, C_T1:C_T1 + 1],
               col[:, C_R:C_R + 1], ALU.mult, (coln,), (coln,))
            tt("ve", col[:, CS_ROWB:CS_ROWB + 1], col[:, C_M:C_M + 1],
               col[:, CS_ROWA:CS_ROWA + 1], ALU.mult, (coln,), (coln,))
            ts("ve", col[:, CS_ROWB:CS_ROWB + 1], col[:, CS_ROWB:CS_ROWB + 1],
               -1.0, None, ALU.mult, ALU.mult, (coln,), (coln,))
            tt("ve", col[:, CS_ROWD:CS_ROWD + 1], col[:, C_M:C_M + 1],
               col[:, CS_S:CS_S + 1], ALU.mult, (coln,), (coln,))
            ts("ve", col[:, CS_ROWD:CS_ROWD + 1], col[:, CS_ROWD:CS_ROWD + 1],
               -1.0, None, ALU.mult, ALU.mult, (coln,), (coln,))
            trans(pb4[0:16, 128:256], col[:, SEL0:SEL0 + 16], t_idf[:, :],
                  (coln, "idf"), ("pb4",))
            cp("ve", rblk[0:16, :], pb4[0:16, 128:256], ("pb4",), (rbn,))
            cp("ve", t_hrb[0:16, 128 * ci:128 * (ci + 1)], rblk[0:16, :],
               (rbn,), ("hrb",))
            selbc(pb0[:, 0:C], P_ROWA, rblk, rbn, True, True, "pb0")
            for dc in range(DCH):
                mm(pb1[:, 0:C], t_selP[:, 128 * dc:128 * (dc + 1)],
                   rblk[0:16, :], True, True, ("selP", rbn), ("pb1",))
                cs2 = slice(TILE * dc + ci * C, TILE * dc + (ci + 1) * C)
                sqq, sqqn = r_sq[dc % 2], f"rsq{dc % 2}"
                tt("ve", sqq[:, 0:C], t_x1[:, cs2], pb0[:, 0:C], ALU.mult,
                   ("x1", "pb0"), (sqqn,))
                ts("ve", sqq[:, 0:C], sqq[:, 0:C], t_ucol[:, dc:dc + 1],
                   None, ALU.mult, ALU.mult, (sqqn, "ucol"), (sqqn,))
                tt("ve", t_h[:, cs2], sqq[:, 0:C], pb1[:, 0:C], ALU.add,
                   (sqqn, "pb1"), ("h",))
        if "h" in tap_outs:
            dma("gp", x_3d(tap_outs["h"], t0), sb_3d(t_h), ("h",),
                ("htap",), force_sig=True)

        if upto == "S1h":
            return
        def proj(dst, dstname, whbm, whname, dgt, dgname, convt, convname,
                 nout, carry_base, rebuild_v=False, ti=ti):
            for oc in range(nout):
                sl, sn = r_wsl[oc % 4], f"rwsl{oc % 4}"
                w_slab(sl, sn, whbm, whname, oc, DCH, nout * 128)
                pb, pn = (pb0, "pb0") if oc % 2 == 0 else (pb1, "pb1")
                for ic in range(DCH):
                    mm(pb[:, 0:TILE], sl[:, 128 * ic:128 * (ic + 1)],
                       t_h[:, TILE * ic:TILE * (ic + 1)], ic == 0,
                       ic == DCH - 1, (sn, "h"), (pn,))
                pre, pren = r_pre[oc % 2], f"rpre{oc % 2}"
                cb = 64 * (carry_base + oc)
                cp("ve", pre[:, 0:64], t_carry[:, cb:cb + 64], ("carry",),
                   (pren,))
                cp("ve", pre[:, 64:64 + TILE], pb[:, 0:TILE], (pn,), (pren,))
                cp("ve", t_carry[:, cb:cb + 64], pre[:, TILE:TILE + 64],
                   (pren,), ("carry",))
                for tp in range(4):
                    if rebuild_v:
                        dgt_, dgn = r_dgv[tp % 2], f"rdgv{tp % 2}"
                        ts("ve", dgt_[:, :], t_idb[:, :],
                           convt[:, 4 * oc + tp:4 * oc + tp + 1], None,
                           ALU.mult, ALU.mult, ("idb", convname), (dgn,))
                        dgap = dgt_[:, :]
                    else:
                        j = 4 * oc + tp
                        dgap, dgn = dgt[:, 128 * j:128 * (j + 1)], dgname
                    mm(pb2[:, 0:TILE], dgap,
                       pre[:, 61 + tp:61 + tp + TILE], tp == 0, tp == 3,
                       (dgn, pren), ("pb2",))
                act(dst[:, TILE * oc:TILE * (oc + 1)], pb2[:, 0:TILE],
                    AF.Silu, ("pb2",), (dstname,))

        if ti == 1 and "dCar" in tap_outs:
            cp("ve", t_sbc[:, 0:96], t_carry[:, :], ("carry",), ("sbc",))
            dma("gp", tap_outs["dCar"], t_sbc[:, 0:96], ("sbc",),
                ("dCar_tap",))
        proj(t_q, "q", wq_b, "wh0", None, "dgq", t_convq, "convq", DCH,
             0, rebuild_v=True)
        proj(t_k, "k", wk_b, "wh1", None, "dgk", t_convk, "convk", DCH,
             DCH, rebuild_v=True)
        proj(t_v, "v", wv_b, "wh2", None, "dgv", t_convv, "convv", VCH,
             2 * DCH, rebuild_v=True)
        if "q" in tap_outs:
            dma("gp", tap_outs["q"].rearrange("(c p) t -> p c t", p=128)
                [:, :, t0:t0 + TILE],
                t_q[:, :].rearrange("p (c t) -> p c t", c=DCH), ("q",),
                ("qtap",), force_sig=True)
        if "k" in tap_outs:
            dma("gp", tap_outs["k"].rearrange("(c p) t -> p c t", p=128)
                [:, :, t0:t0 + TILE],
                t_k[:, :].rearrange("p (c t) -> p c t", c=DCH), ("k",),
                ("ktap",), force_sig=True)
        if "v" in tap_outs:
            dma("gp", tap_outs["v"].rearrange("(c p) t -> p c t", p=128)
                [:, :, t0:t0 + TILE],
                t_v[:, :].rearrange("p (c t) -> p c t", c=VCH), ("v",),
                ("vtap",), force_sig=True)

        if upto == "S1q":
            return
        for c in range(DCH):
            mm(pb3[0:68, 0:TILE], t_wabp[:, 68 * c:68 * (c + 1)],
               t_h[:, TILE * c:TILE * (c + 1)], False, False,
               ("wabp", "h"), ("pb3",))
        for c in range(DCH):
            cs = slice(TILE * c, TILE * (c + 1))
            sq, sqn = r_sq[c % 2], f"rsq{c % 2}"
            tt("ve", sq[:, :], t_q[:, cs], t_q[:, cs], ALU.mult, ("q",),
               (sqn,))
            mm(pb3[0:9, 0:TILE], t_hotq[:, 9 * c:9 * (c + 1)], sq[:, :],
               False, False, ("hotq", sqn), ("pb3",))
        for c in range(DCH):
            cs = slice(TILE * c, TILE * (c + 1))
            sq, sqn = r_sq[c % 2], f"rsq{c % 2}"
            tt("ve", sq[:, :], t_k[:, cs], t_k[:, cs], ALU.mult, ("k",),
               (sqn,))
            mm(pb3[0:13, 0:TILE], t_hotk[:, 13 * c:13 * (c + 1)], sq[:, :],
               False, False, ("hotk", sqn), ("pb3",))
        cp("ve", t_rstage[0:32, :], pb3[0:32, 0:TILE], ("pb3",), ("rstage",))
        cp("ve", t_rstage[32:64, :], pb3[32:64, 0:TILE], ("pb3",),
           ("rstage",))
        cp("ve", t_rstage[64:96, :], pb3[64:96, 0:TILE], ("pb3",),
           ("rstage",))
        if ti == 0 and "dRst0" in tap_outs:
            dma("gp", tap_outs["dRst0"], t_rstage[:, :], ("rstage",),
                ("dRst0_tap",))
        act(t_rstg2[32:36, :], t_rstage[32:36, :], AF.Exp,
            ("rstage", "dtbp"), ("rstg2",), bias=t_dtbp[32:36, :])
        act(t_rstage[32:36, :], t_rstg2[32:36, :], AF.Ln,
            ("rstg2", "one1"), ("rstage",), bias=t_one1[32:36, :])
        ts("ve", t_rstg2[32:36, :], t_rstage[32:36, :],
           t_nega[32:36, :], None, ALU.mult, ALU.mult, ("rstage", "nega"),
           ("rstg2",))
        act(t_rstg2[64:68, :], t_rstage[64:68, :], AF.Exp, ("rstage",),
            ("rstg2",), scale=-1.0)
        ts("ve", t_rstg2[64:68, :], t_rstg2[64:68, :], 1.0, None, ALU.add,
           ALU.add, ("rstg2",), ("rstg2",))
        recip(t_rstage[64:68, :], t_rstg2[64:68, :], ("rstg2",), ("rstage",))
        for ci in range(NCH):
            ccs = slice(ci * C, (ci + 1) * C)
            scan(t_rstage[32:36, ccs], t_ones36[32:36, ccs],
                 t_rstg2[32:36, ccs], ("ones36", "rstg2"), ("rstage",))

        if ti == 0 and "dRst1" in tap_outs:
            dma("gp", tap_outs["dRst1"], t_rstage[:, :], ("rstage",),
                ("dRst1_tap",))
        if ti == 0 and "dRs2" in tap_outs:
            dma("gp", tap_outs["dRs2"], t_rstg2[:, :], ("rstg2",),
                ("dRs2_tap",))
        if upto == "S1g":
            return
        for ci in range(NCH):
            delta_chunk(ti, ci)
        if upto == "S1d":
            return

        if "o" in tap_outs:
            dma("gp", tap_outs["o"].rearrange("(c p) t -> p c t", p=128)
                [:, :, t0:t0 + TILE],
                t_o[:, :].rearrange("p (c t) -> p c t", c=VCH), ("o",),
                ("otap",), force_sig=True)

        o_gate_enc_z(ti)
        if upto == "S1o":
            return
        w2_stage(ti)

    def delta_chunk(ti, ci):
        cur = ci % 2
        col, coln = r_cols[cur], f"rcols{cur}"
        rblk, rbn = r_rblk[cur], f"rrblk{cur}"
        ccs = slice(ci * C, (ci + 1) * C)
        trans(pb4[:, 0:128], t_rstage[:, ccs], t_idf[:, :],
              ("rstage", "idf"), ("pb4",))
        cp("ve", col[:, 0:21], pb4[:, 0:21], ("pb4",), (coln,))
        cp("ve", col[:, C_C:C_C + 4], pb4[:, 32:36], ("pb4",), (coln,))
        cp("ve", col[:, C_BETA:C_BETA + 4], pb4[:, 64:68], ("pb4",), (coln,))
        act(col[:, C_T1:C_T1 + 4], col[:, C_SSK:C_SSK + 4], AF.Ln,
            (coln, "eps6"), (coln,), bias=t_eps6[:, :])
        ts("ve", col[:, C_LNRK:C_LNRK + 4], col[:, C_T1:C_T1 + 4], -0.5,
           None, ALU.mult, ALU.mult, (coln,), (coln,))
        act(col[:, C_T1:C_T1 + 4], col[:, C_SSQ:C_SSQ + 4], AF.Ln,
            (coln, "eps6"), (coln,), bias=t_eps6[:, :])
        ts("ve", col[:, C_LNRQ:C_LNRQ + 4], col[:, C_T1:C_T1 + 4], -0.5,
           -0.5 * float(np.log(Kd)), ALU.mult, ALU.add, (coln,), (coln,))
        act(col[:, C_LNB:C_LNB + 4], col[:, C_BETA:C_BETA + 4], AF.Ln,
            (coln,), (coln,))
        tt("ve", col[:, C_LA:C_LA + 4], col[:, C_LNRK:C_LNRK + 4],
           col[:, C_C:C_C + 4], ALU.subtract, (coln,), (coln,))
        tt("ve", col[:, C_T1:C_T1 + 4], col[:, C_C:C_C + 4],
           col[:, C_LNRK:C_LNRK + 4], ALU.add, (coln,), (coln,))
        tt("ve", col[:, CS_B:CS_B + 4], col[:, C_T1:C_T1 + 4],
           col[:, C_LNB:C_LNB + 4], ALU.add, (coln,), (coln,))
        tt("ve", col[:, CS_E:CS_E + 4], col[:, C_C:C_C + 4],
           col[:, C_LNRQ:C_LNRQ + 4], ALU.add, (coln,), (coln,))
        act(col[:, C_EXPB:C_EXPB + 4], col[:, CS_B:CS_B + 4], AF.Exp,
            (coln,), (coln,))
        mm(pb4[0:1, 200:204], t_e127[:, :], col[:, C_C:C_C + 4], True, True,
           ("e127", coln), ("pb4",))
        cp("ve", t_cc4[:, :], pb4[0:1, 200:204], ("pb4",), ("cc4",))
        for hh in range(H):
            mm(pb4[:, 204 + hh:205 + hh], t_onesr[:, :],
               t_cc4[0:1, hh:hh + 1], True, True, ("onesr", "cc4"), ("pb4",))
        cp("ve", t_cCb[:, :], pb4[:, 204:208], ("pb4",), ("cCb",))
        act(t_gCc[:, :], t_cCb[:, :], AF.Exp, ("cCb",), ("gCc",))
        for hh in range(H):
            act(col[:, C_EXPD + hh:C_EXPD + hh + 1],
                col[:, C_LA + hh:C_LA + hh + 1], AF.Exp, (coln, "cCb"),
                (coln,), bias=t_cCb[:, hh:hh + 1])
        trans(pb4[0:16, 128:256], col[:, SEL0:SEL0 + 16], t_idf[:, :],
              (coln, "idf"), ("pb4",))
        cp("ve", rblk[0:16, :], pb4[0:16, 128:256], ("pb4",), (rbn,))

        for hh in range(H):
            for j2 in range(2):
                ks = slice(TILE * (2 * hh + j2) + ci * C,
                           TILE * (2 * hh + j2) + (ci + 1) * C)
                mm(pb0[:, 128 * hh:128 * (hh + 1)], t_k[:, ks], t_k[:, ks],
                   j2 == 0, j2 == 1, ("k",), ("pb0",))
                mm(pb1[:, 128 * hh:128 * (hh + 1)], t_k[:, ks], t_q[:, ks],
                   j2 == 0, j2 == 1, ("k", "q"), ("pb1",))
        mm(pb3[:, 0:512], t_negbig[:, :], t_maskY[:, :], True, True,
           ("negbig", "maskY"), ("pb3",))
        for hh in range(H):
            mm(pb3[:, 128 * hh:128 * (hh + 1)],
               t_selo[:, 128 * (P_B + hh):128 * (P_B + hh + 1)],
               rblk[0:16, :], False, True, ("selo", rbn), ("pb3",))
        for hh in range(H):
            act(t_Eh[:, 128 * hh:128 * (hh + 1)],
                pb3[:, 128 * hh:128 * (hh + 1)], AF.Exp, ("pb3", coln),
                ("Eh",), bias=col[:, C_LA + hh:C_LA + hh + 1])
        if ti == 0 and ci == 0 and "dEhY" in tap_outs:
            cp("ve", t_sbc[:, 0:512], t_Eh[:, :], ("Eh",), ("sbc",))
            dma("gp", tap_outs["dEhY"], t_sbc[:, 0:512], ("sbc",),
                ("dEhY_tap",))
            cp("ve", t_sbc[:, 0:512], pb3[:, 0:512], ("pb3",), ("sbc",))
            dma("gp", tap_outs["dEhM"], t_sbc[:, 0:512], ("sbc",),
                ("dEhM_tap",))
        stt("ve", r_Y[0][:, :], t_Eh[:, :], -1.0, pb0[:, 0:512], ALU.mult,
            ALU.mult, ("Eh", "pb0"), ("rY0",))
        mm(pb3[:, 0:512], t_negbig[:, :], t_maskM[:, :], True, True,
           ("negbig", "maskM"), ("pb3",))
        for hh in range(H):
            mm(pb3[:, 128 * hh:128 * (hh + 1)],
               t_selo[:, 128 * (P_E + hh):128 * (P_E + hh + 1)],
               rblk[0:16, :], False, True, ("selo", rbn), ("pb3",))
        for hh in range(H):
            act(t_Eh[:, 128 * hh:128 * (hh + 1)],
                pb3[:, 128 * hh:128 * (hh + 1)], AF.Exp, ("pb3", coln),
                ("Eh",), bias=col[:, C_LA + hh:C_LA + hh + 1])
        tt("ve", t_MT[:, :], t_Eh[:, :], pb1[:, 0:512], ALU.mult,
           ("Eh", "pb1"), ("MT",))
        for hh in range(H):
            trans(pbt[:, 128 * hh:128 * (hh + 1)],
                  r_Y[0][:, 128 * hh:128 * (hh + 1)], t_idb[:, :],
                  ("rY0", "idb"), ("pbt",))
        cp("ve", r_X[0][:, :], pbt[:, 0:512], ("pbt",), ("rX0",))
        pzero(pb4, "pb4")
        for hh in range(H):
            mm(pb4[:, 128 * hh:128 * (hh + 1)], t_idb[:, :], t_idb[:, :],
               False, False, ("idb",), ("pb4",))
            mm(pb4[:, 128 * hh:128 * (hh + 1)],
               r_X[0][:, 128 * hh:128 * (hh + 1)], t_idb[:, :], False, True,
               ("rX0", "idb"), ("pb4",))
        for j in range(6):
            xj, xjn = r_X[j % 2], f"rX{j % 2}"
            yj, yjn = r_Y[j % 2], f"rY{j % 2}"
            xn, xnn = r_X[(j + 1) % 2], f"rX{(j + 1) % 2}"
            yn, ynn = r_Y[(j + 1) % 2], f"rY{(j + 1) % 2}"
            for hh in range(H):
                mm(pb2[:, 128 * hh:128 * (hh + 1)],
                   yj[:, 128 * hh:128 * (hh + 1)],
                   xj[:, 128 * hh:128 * (hh + 1)], True, True, (yjn, xjn),
                   ("pb2",))
            cp("ve", xn[:, :], pb2[:, 0:512], ("pb2",), (xnn,))
            if j < 5:
                for hh in range(H):
                    mm(pb2[:, 128 * hh:128 * (hh + 1)],
                       xj[:, 128 * hh:128 * (hh + 1)],
                       yj[:, 128 * hh:128 * (hh + 1)], True, True,
                       (xjn, yjn), ("pb2",))
                cp("ve", yn[:, :], pb2[:, 0:512], ("pb2",), (ynn,))
            zs, zsn = r_Zs[j % 2], f"rZs{j % 2}"
            cp("ve", zs[:, :], pb4[:, 0:512], ("pb4",), (zsn,))
            for hh in range(H):
                mm(pb4[:, 128 * hh:128 * (hh + 1)],
                   xn[:, 128 * hh:128 * (hh + 1)],
                   zs[:, 128 * hh:128 * (hh + 1)], False, True, (xnn, zsn),
                   ("pb4",))
        cp("ve", t_TT[:, :], pb4[:, 0:512], ("pb4",), ("TT",))

        if ti == 0 and ci == 0:
            def dtap(nm, srcap, rdname, ncols):
                if nm in tap_outs:
                    for c0 in range(0, ncols, 512):
                        w = min(512, ncols - c0)
                        cp("ve", t_sbc[:, 0:w], srcap[:, c0:c0 + w],
                           (rdname,), ("sbc",))
                        dma("gp", tap_outs[nm][:, c0:c0 + w], t_sbc[:, 0:w],
                            ("sbc",), (nm + "_tap",))
            dtap("dY0", r_Y[0], "rY0", 512)
            dtap("dX0", r_X[0], "rX0", 512)
            dtap("dTT", t_TT, "TT", 512)
            dtap("dMT", t_MT, "MT", 512)
            if "dRblk" in tap_outs:
                dma("gp", tap_outs["dRblk"], rblk[0:16, :], (rbn,),
                    ("dRblk_tap",))
            if "dCols" in tap_outs:
                dma("gp", tap_outs["dCols"], col[:, :], (coln,),
                    ("dCols_tap",))
        for hh in range(H):
            hc = hh % 2
            rp, rpn = r_Rp[hc], f"rRp{hc}"
            dl, dln = r_Dl[hc], f"rDl{hc}"
            for jv in range(4):
                vs = slice(TILE * (4 * hh + jv) + ci * C,
                           TILE * (4 * hh + jv) + (ci + 1) * C)
                trans(pbt[:, 128 * jv:128 * (jv + 1)], t_v[:, vs],
                      t_idb[:, :], ("v", "idb"), ("pbt",))
            ts("ve", rp[:, 0:512], pbt[:, 0:512],
               col[:, C_BETA + hh:C_BETA + hh + 1], None, ALU.mult, ALU.mult,
               ("pbt", coln), (rpn,))
            for j2 in range(2):
                ks = slice(TILE * (2 * hh + j2) + ci * C,
                           TILE * (2 * hh + j2) + (ci + 1) * C)
                trans(pbt[:, 512 + 128 * j2:512 + 128 * (j2 + 1)],
                      t_k[:, ks], t_idb[:, :], ("k", "idb"), ("pbt",))
            ts("ve", rp[:, 512:768], pbt[:, 512:768],
               col[:, C_EXPB + hh:C_EXPB + hh + 1], None, ALU.mult, ALU.mult,
               ("pbt", coln), (rpn,))
            ts("ve", t_Kh[:, Kd * hh:Kd * (hh + 1)], pbt[:, 512:768],
               col[:, C_EXPD + hh:C_EXPD + hh + 1], None, ALU.mult, ALU.mult,
               ("pbt", coln), ("Kh",))
            mm(pdel[:, 0:512], t_TT[:, 128 * hh:128 * (hh + 1)],
               rp[:, 0:512], True, True, ("TT", rpn), ("pdel",))
            mm(pdel[:, 512:768], t_TT[:, 128 * hh:128 * (hh + 1)],
               rp[:, 512:768], True, True, ("TT", rpn), ("pdel",))
            cp("ve", dl[:, :], pdel[:, 0:768], ("pdel",), (dln,))
            mm(pb3[:, 0:128], t_selo[:, 128 * (P_E + hh):128 * (P_E + hh + 1)],
               rblk[0:16, :], True, True, ("selo", rbn), ("pb3",))
            act(t_eqb[:, :], pb3[:, 0:128], AF.Exp, ("pb3",), ("eqb",))
            for j2 in range(2):
                ks = slice(TILE * (2 * hh + j2) + ci * C,
                           TILE * (2 * hh + j2) + (ci + 1) * C)
                tt("ve", r_Qt[j2][:, :], t_q[:, ks], t_eqb[:, :], ALU.mult,
                   ("q", "eqb"), (f"rQt{j2}",))
            for jv in range(4):
                for j2 in range(2):
                    mm(pb2[:, 128 * jv:128 * (jv + 1)],
                       t_S[:, 1024 * hh + 512 * j2 + 128 * jv:
                           1024 * hh + 512 * j2 + 128 * (jv + 1)],
                       r_Qt[j2][:, :], j2 == 0, False,
                       (f"S{hh}", f"rQt{j2}"), ("pb2",))
                mm(pb2[:, 128 * jv:128 * (jv + 1)],
                   dl[:, 128 * jv:128 * (jv + 1)],
                   t_MT[:, 128 * hh:128 * (hh + 1)], False, True,
                   (dln, "MT"), ("pb2",))
            p.emit("ve", lambda e, hh=hh, ci=ci: e.tensor_copy(
                out=t_o[:, :].rearrange("p (c t) -> p c t", c=VCH)
                [:, 4 * hh:4 * hh + 4, ci * C:(ci + 1) * C],
                in_=pb2[:, 0:512].rearrange("p (c t) -> p c t", c=4)),
                reads=("pb2",), writes=("o",))
            for m_ in range(2):
                mm(pdel[:, 256 * m_:256 * (m_ + 1)],
                   dl[:, 512 + 128 * m_:512 + 128 * (m_ + 1)],
                   t_Kh[:, Kd * hh:Kd * (hh + 1)], True, True,
                   (dln, "Kh"), ("pdel",))
            ts("ve", t_g0[:, 0:128], t_idb[:, :], t_gCc[:, hh:hh + 1], None,
               ALU.mult, ALU.mult, ("idb", "gCc"), ("g0",))
            ts("ve", t_g0[:, 384:512], t_idb[:, :], t_gCc[:, hh:hh + 1],
               None, ALU.mult, ALU.mult, ("idb", "gCc"), ("g0",))
            wp, wpn = r_Wp[hc], f"rWp{hc}"
            tt("ve", wp[:, 0:256], t_g0[:, 0:256], pdel[:, 0:256],
               ALU.subtract, ("g0", "pdel"), (wpn,))
            tt("ve", wp[:, 256:512], t_g0[:, 256:512], pdel[:, 256:512],
               ALU.subtract, ("g0", "pdel"), (wpn,))
            for a_ in range(2):
                for b_ in range(2):
                    mm(pdel[:, 512 * a_:512 * (a_ + 1)],
                       wp[:, 256 * b_ + 128 * a_:256 * b_ + 128 * (a_ + 1)],
                       t_S[:, 1024 * hh + 512 * b_:1024 * hh + 512 * (b_ + 1)],
                       b_ == 0, False, (wpn, f"S{hh}"), ("pdel",))
                mm(pdel[:, 512 * a_:512 * (a_ + 1)],
                   t_Kh[:, Kd * hh + 128 * a_:Kd * hh + 128 * (a_ + 1)],
                   dl[:, 0:512], False, True, ("Kh", dln), ("pdel",))
            for a_ in range(2):
                cp("ve", t_S[:, 1024 * hh + 512 * a_:1024 * hh + 512 * (a_ + 1)],
                   pdel[:, 512 * a_:512 * (a_ + 1)], ("pdel",), (f"S{hh}",))
            if ti == 0 and ci == 0 and hh == 0:
                def dtap2(nm, srcap, rdname, ncols):
                    if nm in tap_outs:
                        for c0 in range(0, ncols, 512):
                            w = min(512, ncols - c0)
                            cp("ve", t_sbc[:, 0:w], srcap[:, c0:c0 + w],
                               (rdname,), ("sbc",))
                            dma("gp", tap_outs[nm][:, c0:c0 + w],
                                t_sbc[:, 0:w], ("sbc",), (nm + "_tap",))
                dtap2("dRp", rp, rpn, 768)
                dtap2("dDl", dl, dln, 768)
                dtap2("dKh", t_Kh, "Kh", 1024)
                dtap2("dWp", wp, wpn, 512)
                dtap2("dS", t_S, "S0", 4096)

    def o_gate_enc_z(ti):
        t0 = ti * TILE
        pzero(pb3, "pb3")
        for vc in range(VCH):
            cs = slice(TILE * vc, TILE * (vc + 1))
            sq, sqn = r_sq[vc % 2], f"rsq{vc % 2}"
            tt("ve", sq[:, :], t_o[:, cs], t_o[:, cs], ALU.mult, ("o",),
               (sqn,))
            mm(pb3[0:17, 0:TILE], t_hotv[:, 17 * vc:17 * (vc + 1)], sq[:, :],
               False, False, ("hotv", sqn), ("pb3",))
        cp("ve", t_rstage[0:17, :], pb3[0:17, 0:TILE], ("pb3",), ("rstage",))
        for ci in range(NCH):
            cur = ci % 2
            col, coln = r_cols[cur], f"rcols{cur}"
            rblk, rbn = r_rblk[cur], f"rrblk{cur}"
            ccs = slice(ci * C, (ci + 1) * C)
            trans(pb4[:, 0:128], t_rstage[:, ccs], t_idf[:, :],
                  ("rstage", "idf"), ("pb4",))
            cp("ve", col[:, 0:21], pb4[:, 0:21], ("pb4",), (coln,))
            act(col[:, C_T1:C_T1 + 4], col[:, C_SSO:C_SSO + 4], AF.Ln,
                (coln, "eps6"), (coln,), bias=t_eps6[:, :], scale=1.0 / Vd)
            act(col[:, CS_B:CS_B + 4], col[:, C_T1:C_T1 + 4], AF.Exp,
                (coln,), (coln,), scale=-0.5)
            trans(pb4[0:16, 128:256], col[:, SEL0:SEL0 + 16], t_idf[:, :],
                  (coln, "idf"), ("pb4",))
            cp("ve", rblk[0:16, :], pb4[0:16, 128:256], ("pb4",),
               (f"orb{ci}",))
            cp("ve", t_sbc[0:16, 128 * ci:128 * (ci + 1)], rblk[0:16, :],
               (f"orb{ci}",), ("osbc",))
        for hh in range(H):
            for ci in range(NCH):
                mm(pb3[:, 128 * ci:128 * (ci + 1)],
                   t_selo[:, 128 * (P_B + hh):128 * (P_B + hh + 1)],
                   t_sbc[0:16, 128 * ci:128 * (ci + 1)], True, True,
                   ("selo", "osbc"), ("pb3",))
            for jv in range(4):
                vc = 4 * hh + jv
                cs = slice(TILE * vc, TILE * (vc + 1))
                sl, sn = r_wsl[vc % 4], f"rwsl{vc % 4}"
                w_slab(sl, sn, wg_b, "wh3", vc, DCH, VAL)
                pb, pn = (pb0, "pb0") if vc % 2 == 0 else (pb1, "pb1")
                for ic in range(DCH):
                    mm(pb[:, 0:TILE], sl[:, 128 * ic:128 * (ic + 1)],
                       t_h[:, TILE * ic:TILE * (ic + 1)], ic == 0,
                       ic == DCH - 1, (sn, "h"), (pn,))
                ga, gan = r_ga[vc % 2], f"rga{vc % 2}"
                act(ga[:, :], pb[:, 0:TILE], AF.Silu, (pn,), (gan,))
                sq, sqn = r_sq[vc % 2], f"rsq{vc % 2}"
                tt("ve", sq[:, :], t_o[:, cs], pb3[:, 0:TILE], ALU.mult,
                   ("o", "pb3"), (sqn,))
                stt("ve", t_o[:, cs], sq[:, :], t_onwc[:, jv:jv + 1],
                    ga[:, :], ALU.mult, ALU.mult, (sqn, "onwc", gan), ("o",))
        if "gt" in tap_outs:
            dma("gp", tap_outs["gt"].rearrange("(c p) t -> p c t", p=128)
                [:, :, t0:t0 + TILE],
                t_o[:, :].rearrange("p (c t) -> p c t", c=VCH), ("o",),
                ("gttap",), force_sig=True)
        for q_ in range(2):
            pbs = [(pb0, "pb0"), (pb1, "pb1"), (pb2, "pb2"), (pb4, "pb4")]
            for vc in range(VCH):
                sl, sn = r_wo2[vc % 2], f"rwo2{vc % 2}"
                src = wo_b.rearrange("(vc p) o -> p vc o", p=128)[
                    :, vc:vc + 1, 512 * q_:512 * (q_ + 1)]
                dma("sy", sl[:, 0:512].rearrange("p (a j) -> p a j", a=1),
                    src, (sn, "wh4"), (sn,))
                for jo in range(4):
                    oc = 4 * q_ + jo
                    pb, pn = pbs[jo]
                    mm(pb[:, 0:TILE], sl[:, 128 * jo:128 * (jo + 1)],
                       t_o[:, TILE * vc:TILE * (vc + 1)], vc == 0, False,
                       (sn, "o"), (pn,))
            for jo in range(4):
                oc = 4 * q_ + jo
                pb, pn = pbs[jo]
                for ci in range(NCH):
                    mm(pb[:, 128 * ci:128 * (ci + 1)],
                       t_selo[:, 128 * P_ROWD:128 * (P_ROWD + 1)],
                       t_hrb[0:16, 128 * ci:128 * (ci + 1)], False,
                       ci == NCH - 1, ("selo", "hrb"), (pn,))
            for jo in range(4):
                oc = 4 * q_ + jo
                pb, pn = pbs[jo]
                cs = slice(TILE * oc, TILE * (oc + 1))
                sq, sqn = r_sq[jo % 2], f"rsq{jo % 2}"
                for ci in range(NCH):
                    mm(pb3[:, 128 * ci:128 * (ci + 1)],
                       t_selo[:, 128 * P_S:128 * (P_S + 1)],
                       t_hrb[0:16, 128 * ci:128 * (ci + 1)], True, True,
                       ("selo", "hrb"), ("pb3",))
                tt("ve", r_32[jo % 2][:, :], t_x1[:, cs], pb3[:, 0:TILE],
                   ALU.mult, ("x1", "pb3"), (f"r32{jo % 2}",))
                tt("ve", t_z[:, cs], r_32[jo % 2][:, :], pb[:, 0:TILE],
                   ALU.add, (f"r32{jo % 2}", pn), ("z",))
                cp("ve", t_zb[:, cs], t_z[:, cs], ("z",), ("zb",))
        if "z" in tap_outs:
            dma("gp", x_3d(tap_outs["z"], t0), sb_3d(t_z), ("z",),
                ("ztap",), force_sig=True)

    def w2_stage(ti):
        t0 = ti * TILE
        pzero(pb3, "pb3")
        stats_group_bf(t_zb, "zb", "z")
        cp("ve", t_rstage[0:19, :], pb3[0:19, 0:TILE], ("pb3",), ("rstage",))
        for ci in range(NCH):
            cur = ci % 2
            col, coln = r_cols[cur], f"rcols{cur}"
            rblk, rbn = r_rblk[cur], f"rrblk{cur}"
            ccs = slice(ci * C, (ci + 1) * C)
            trans(pb4[:, 0:128], t_rstage[:, ccs], t_idf[:, :],
                  ("rstage", "idf"), ("pb4",))
            cp("ve", col[:, 0:21], pb4[:, 0:21], ("pb4",), (coln,))
            ts("ve", col[:, C_M:C_M + 1], col[:, C_SZ:C_SZ + 1], 1.0 / D,
               None, ALU.mult, ALU.mult, (coln,), (coln,))
            ts("ve", col[:, C_T1:C_T1 + 1], col[:, C_SZ2:C_SZ2 + 1], 1.0 / D,
               None, ALU.mult, ALU.mult, (coln,), (coln,))
            tt("ve", col[:, C_T2:C_T2 + 1], col[:, C_M:C_M + 1],
               col[:, C_M:C_M + 1], ALU.mult, (coln,), (coln,))
            tt("ve", col[:, C_V:C_V + 1], col[:, C_T1:C_T1 + 1],
               col[:, C_T2:C_T2 + 1], ALU.subtract, (coln,), (coln,))
            act(col[:, C_T3:C_T3 + 1], col[:, C_V:C_V + 1], AF.Ln,
                (coln, "eps5"), (coln,), bias=t_eps5[:, :])
            act(col[:, CS_X1:CS_X1 + 1], col[:, C_T3:C_T3 + 1], AF.Exp,
                (coln,), (coln,), scale=-0.5)
            tt("ve", col[:, CS_X2:CS_X2 + 1], col[:, C_M:C_M + 1],
               col[:, CS_X1:CS_X1 + 1], ALU.mult, (coln,), (coln,))
            ts("ve", col[:, CS_X2:CS_X2 + 1], col[:, CS_X2:CS_X2 + 1], -1.0,
               None, ALU.mult, ALU.mult, (coln,), (coln,))
            trans(pb4[0:16, 128:256], col[:, SEL0:SEL0 + 16], t_idf[:, :],
                  (coln, "idf"), ("pb4",))
            cp("ve", t_zrb[0:16, 128 * ci:128 * (ci + 1)],
               pb4[0:16, 128:256], ("pb4",), ("zrb",))
        for ci in range(NCH):
            mm(pb4[:, 128 * ci:128 * (ci + 1)],
               t_selo[:, 128 * P_X1:128 * (P_X1 + 1)],
               t_zrb[0:16, 128 * ci:128 * (ci + 1)], True, True,
               ("selo", "zrb"), ("pb4",))
        cp("ve", t_sbc[:, 0:TILE], pb4[:, 0:TILE], ("pb4",), ("sbc",))
        if upto == "S1v":
            return
        for oc in range(DCH):
            sl, sn = r_wsl[oc % 4], f"rwsl{oc % 4}"
            w_slab(sl, sn, w2p_b, "wh5", oc, DCH, D)
            pb, pn = (pb0, "pb0") if oc % 2 == 0 else (pb1, "pb1")
            for ic in range(DCH):
                mm(pb[:, 0:TILE], sl[:, 128 * ic:128 * (ic + 1)],
                   t_zb[:, TILE * ic:TILE * (ic + 1)], ic == 0,
                   ic == DCH - 1, (sn, "zb"), (pn,))
            for ci in range(NCH):
                mm(pb2[:, 128 * ci:128 * (ci + 1)],
                   t_selw2[:, 128 * oc:128 * (oc + 1)],
                   t_zrb[0:16, 128 * ci:128 * (ci + 1)], True, True,
                   ("selw2", "zrb"), ("pb2",))
            cs = slice(TILE * oc, TILE * (oc + 1))
            ff, fn_ = r_ff[oc % 2], f"rff{oc % 2}"
            tt("ve", ff[:, :], pb[:, 0:TILE], t_sbc[:, 0:TILE], ALU.mult,
               (pn, "sbc"), (fn_,))
            gl, gln = r_ga[oc % 2], f"rga{oc % 2}"
            sq2, sq2n = r_sq[oc % 2], f"rsq{oc % 2}"
            tt("ve", sq2[:, :], ff[:, :], pb2[:, 0:TILE], ALU.add,
               (fn_, "pb2"), (sq2n,))
            act(gl[:, :], sq2[:, :], AF.Gelu, (sq2n, "gelb"), (gln,),
                bias=t_gelb[:, oc:oc + 1])
            tt("ve", t_x1[:, cs], t_x1[:, cs], t_z[:, cs], ALU.add,
               ("x1", "z"), ("x1",))
            tt("ve", t_z[:, cs], t_x1[:, cs], gl[:, :], ALU.add,
               ("x1", gln), ("z",))
            if upto != "S1w1":
                dma("gp", x2sp[128 * oc:128 * (oc + 1), t0:t0 + TILE],
                    t_z[:, cs], ("z",), (f"x2w{ti}_{oc}",), force_sig=True)
        if upto == "S1w1":
            return
        if "x2f" in tap_outs:
            dma("gp", x_3d(tap_outs["x2f"], t0), sb_3d(t_z), ("z",),
                ("x2tap",), force_sig=True)
        if upto == "S1w":
            return
        pzero(pb3, "pb3")
        for c in range(DCH):
            cs = slice(TILE * c, TILE * (c + 1))
            sq, sqn = r_32[c % 2], f"r32{c % 2}"
            tt("ve", sq[:, :], t_z[:, cs], t_z[:, cs], ALU.mult, ("z",),
               (sqn,))
            mm(pb3[0:20, 0:TILE], t_onespad[:, 8:28], t_z[:, cs],
               False, False, ("onespad", "z"), ("pb3",))
            mm(pb3[0:21, 0:TILE], t_onespad[:, 28:49], sq[:, :],
               False, c == DCH - 1, ("onespad", sqn), ("pb3",))
        cp("ve", t_rstage[0:21, :], pb3[0:21, 0:TILE], ("pb3",), ("rstage",))
        if upto == "S1x":
            return
        for ci in range(NCH):
            cur = ci % 2
            col, coln = r_cols[cur], f"rcols{cur}"
            ccs = slice(ci * C, (ci + 1) * C)
            trans(pb4[:, 0:128], t_rstage[:, ccs], t_idf[:, :],
                  ("rstage", "idf"), ("pb4",))
            cp("ve", col[:, 0:21], pb4[:, 0:21], ("pb4",), (coln,))
            ts("ve", col[:, C_M:C_M + 1], col[:, C_SY:C_SY + 1], 1.0 / D,
               None, ALU.mult, ALU.mult, (coln,), (coln,))
            ts("ve", col[:, C_T1:C_T1 + 1], col[:, C_SY2:C_SY2 + 1], 1.0 / D,
               None, ALU.mult, ALU.mult, (coln,), (coln,))
            tt("ve", col[:, C_T2:C_T2 + 1], col[:, C_M:C_M + 1],
               col[:, C_M:C_M + 1], ALU.mult, (coln,), (coln,))
            tt("ve", col[:, C_V:C_V + 1], col[:, C_T1:C_T1 + 1],
               col[:, C_T2:C_T2 + 1], ALU.subtract, (coln,), (coln,))
            act(col[:, C_T3:C_T3 + 1], col[:, C_V:C_V + 1], AF.Ln,
                (coln, "eps5"), (coln,), bias=t_eps5[:, :])
            act(col[:, CS_X1:CS_X1 + 1], col[:, C_T3:C_T3 + 1], AF.Exp,
                (coln,), (coln,), scale=-0.5)
            tt("ve", col[:, CS_X2:CS_X2 + 1], col[:, C_M:C_M + 1],
               col[:, CS_X1:CS_X1 + 1], ALU.mult, (coln,), (coln,))
            ts("ve", col[:, CS_X2:CS_X2 + 1], col[:, CS_X2:CS_X2 + 1], -1.0,
               None, ALU.mult, ALU.mult, (coln,), (coln,))
            trans(pb4[0:16, 128:256], col[:, SEL0:SEL0 + 16], t_idf[:, :],
                  (coln, "idf"), ("pb4",))
            cp("ve", t_zrb[0:16, 128 * ci:128 * (ci + 1)],
               pb4[0:16, 128:256], ("pb4",), ("zrb",))
            mm(pb4[:, 256:384], t_selo[:, 128 * P_X1:128 * (P_X1 + 1)],
               t_zrb[0:16, 128 * ci:128 * (ci + 1)], True, True,
               ("selo", "zrb"), ("pb4",))
            mm(pb4[:, 384:512], t_selo[:, 128 * P_X2:128 * (P_X2 + 1)],
               t_zrb[0:16, 128 * ci:128 * (ci + 1)], True, True,
               ("selo", "zrb"), ("pb4",))
            for dc in range(DCH):
                cs2 = slice(TILE * dc + ci * C, TILE * dc + (ci + 1) * C)
                jj = dc % 2
                tt("ve", r_32[jj][:, 0:C], t_z[:, cs2], pb4[:, 256:384],
                   ALU.mult, ("z", "pb4"), (f"r32{jj}",))
                tt("ve", r_32[jj][:, 0:C], r_32[jj][:, 0:C], pb4[:, 384:512],
                   ALU.add, (f"r32{jj}", "pb4"), (f"r32{jj}",))
                red(t_am2[:, 16 + ci * DCH + dc:17 + ci * DCH + dc],
                    r_32[jj][:, 0:C], ALU.max,
                    True, (f"r32{jj}",), (f"bm_{ti}_{ci}_{dc}",))
                dma("gp", lnx2sp[128 * dc:128 * (dc + 1),
                                 t0 + ci * C:t0 + (ci + 1) * C],
                    r_32[jj][:, 0:C], (f"r32{jj}",),
                    (f"lxw{ti}_{ci}_{dc}",), force_sig=True)
        red(t_am2[:, 8 + ti:9 + ti], t_am2[:, 16:16 + NCH * DCH], ALU.max,
            False,
            tuple(f"bm_{ti}_{ci}_{dc}" for ci in range(NCH)
                  for dc in range(DCH)), (f"bmT{ti}",))

    for ti in range(NT):
        sweep1_tile(ti)
        if upto.startswith("S1") and upto != "S1" and ti == 0:
            break
    if upto.startswith("S1"):
        for oc in range(DCH):
            dma("gp", out_d[128 * oc:128 * (oc + 1), 0:TILE],
                t_rstage[:, 0:TILE], ("rstage",), ("outdummy",))
        n = p.build(nc, sems, spacers, es)
        import sys
        print(f"[kernel] built {n} ops ({upto})", file=sys.stderr)
        es.close()
        return nc, tap_outs, p
    amax_collective(2)

    def sweep2_tile(ti):
        t0 = ti * TILE
        dma("sy", sb_3d(t_z), x_3d(lnx2sp, t0),
            ("z",) + tuple(f"lxw{t}_{ci}_{dc}" for t in range(NT)
                           for ci in range(NCH)
                           for dc in range(DCH)), ("z",))
        act(t_q8[:, :], t_z[:, :], AF.Copy, ("z", "sxh2"), ("q8t",),
            scale=t_sx2h[:, :])
        def resid_x2(oc):
            rx, rxn = r_x0[oc % 2], f"rx0{oc % 2}"
            dma("sy", rx[:, :], x2sp[128 * oc:128 * (oc + 1), t0:t0 + TILE],
                (rxn,) + tuple(f"x2w{t}_{k}" for t in range(NT)
                               for k in range(DCH)), (rxn,))
            return rx[:, :], rxn
        for oc in range(DCH):
            sl, sn = r_f8[oc % 2], f"rf8{oc % 2}"
            w_slab(sl, sn, wf8sp, "wf8", oc, DCH, D)
            pb, pn = (pb0, "pb0") if oc % 2 == 0 else (pb1, "pb1")
            for ic in range(DCH):
                mm(pb[:, 0:TILE], sl[:, 128 * ic:128 * (ic + 1)],
                   t_q8[:, TILE * ic:TILE * (ic + 1)], ic == 0,
                   ic == DCH - 1, (sn, "q8t"), (pn,))
            ff, fn_ = r_ff[oc % 2], f"rff{oc % 2}"
            ts("ve", ff[:, :], pb[:, 0:TILE], t_f2[:, :],
               t_fbc[:, oc:oc + 1], ALU.mult, ALU.add, (pn, "fd2", "fbc"),
               (fn_,))
            rx, rxn = resid_x2(oc)
            jj = oc % 2
            stt("ve", r_32[jj][:, :], ff[:, :], 0.5, rx, ALU.mult, ALU.add,
                (fn_, rxn), (f"r32{jj}",))
            dma("gp", out_d[128 * oc:128 * (oc + 1), t0:t0 + TILE],
                r_32[jj][:, :], (f"r32{jj}",), (f"outw{ti}_{oc}",),
                force_sig=True)

    for ti in range(NT):
        sweep2_tile(ti)

    n = p.build(nc, sems, spacers, es)
    import sys
    print(f"[kernel] built {n} ops", file=sys.stderr)
    es.close()
    return nc, tap_outs, p


def _prep_inputs(inputs, core):
    bf = ml_dtypes.bfloat16
    x = np.asarray(inputs["L0_attn_x"], np.float32)
    d = {}
    d["x0"] = np.ascontiguousarray(x[core].T)
    d["wq"] = np.asarray(inputs["L0_attn_Wq"], np.float32)
    d["wk"] = np.asarray(inputs["L0_attn_Wk"], np.float32)
    d["wv"] = np.asarray(inputs["L0_attn_Wv"], np.float32)
    d["wg"] = np.asarray(inputs["L0_attn_Wg"], np.float32)
    d["wo"] = np.asarray(inputs["L0_attn_Wo"], np.float32)
    d["w2"] = np.asarray(inputs["L0_attn_W2"], np.float32)
    d["wfT"] = np.ascontiguousarray(
        np.asarray(inputs["L0_ffn_weight"], np.float32).T)
    wa = np.asarray(inputs["L0_attn_Wa"], np.float32)
    wb = np.asarray(inputs["L0_attn_Wb"], np.float32)
    wabp = np.zeros((DCH, 128, 68), np.float32)
    wabp[:, :, 32:36] = wa.reshape(DCH, 128, 4)
    wabp[:, :, 64:68] = wb.reshape(DCH, 128, 4)
    d["wabp"] = np.ascontiguousarray(
        wabp.transpose(1, 0, 2).reshape(128, 68 * DCH)).astype(bf)
    for nm, key, w in (("cqw", "L0_attn_conv_q_w", KEY),
                       ("ckw", "L0_attn_conv_k_w", KEY),
                       ("cvw", "L0_attn_conv_v_w", VAL)):
        cwt = np.asarray(inputs[key], np.float32)
        nch = w // 128
        d[nm] = np.ascontiguousarray(
            cwt.reshape(nch, 128, 4).transpose(1, 0, 2).reshape(128, 4 * nch))
    def cols(v):
        v = np.asarray(v, np.float32)
        n = v.shape[0] // 128
        return np.ascontiguousarray(v.reshape(n, 128).T)
    d["g1c"] = cols(inputs["L0_attn_ln1_g"])
    d["b1c"] = cols(inputs["L0_attn_ln1_b"])
    d["rmc"] = cols(inputs["L0_attn_rms_in_w"])
    d["g2c"] = cols(inputs["L0_attn_ln2_g"])
    d["b2wc"] = cols(inputs["L0_attn_b2"])
    d["b2c"] = cols(inputs["L0_attn_ln2_b"])
    d["fbc"] = cols(inputs["L0_ffn_bias"])
    d["onwc"] = cols(inputs["L0_attn_o_norm_w"])
    dtb = np.zeros((128, 1), np.float32)
    dtb[32:36, 0] = np.asarray(inputs["L0_attn_dt_bias"], np.float32)
    d["dtbp"] = dtb
    alo = np.zeros((128, 1), np.float32)
    alo[32:36, 0] = np.asarray(inputs["L0_attn_A_log"], np.float32)
    d["alogp"] = alo
    d["idf"] = np.eye(128, dtype=np.float32)
    d["idb"] = np.eye(128).astype(bf)
    d["onesr"] = np.ones((1, 128), np.float32)
    d["onescb"] = np.ones((128, 1)).astype(bf)
    d["negbig"] = (np.eye(128) * -1e30).astype(bf)
    killY = np.tril(np.ones((128, 128), np.float32))
    killM = np.tril(np.ones((128, 128), np.float32), k=-1)
    d["maskY4"] = np.tile(killY, (1, 4)).astype(bf)
    d["maskM4"] = np.tile(killM, (1, 4)).astype(bf)
    hq = np.zeros((128, 9 * DCH), np.float32)
    for c in range(DCH):
        hq[:, 9 * c + RW_SSQ + c // 2] = 1.0
    d["hotq"] = hq.astype(bf)
    hk = np.zeros((128, 13 * DCH), np.float32)
    for c in range(DCH):
        hk[:, 13 * c + RW_SSK + c // 2] = 1.0
    d["hotk"] = hk.astype(bf)
    hv = np.zeros((128, 17 * VCH), np.float32)
    for c in range(VCH):
        hv[:, 17 * c + RW_SSO + c // 4] = 1.0
    d["hotv"] = hv.astype(bf)
    selo = np.zeros((16, 16, 128), np.float32)
    for j in range(16):
        selo[j, j, :] = 1.0
    d["selo_f"] = np.ascontiguousarray(
        selo.transpose(1, 0, 2).reshape(16, 16 * 128))
    onespad = np.zeros((128, 49), np.float32)
    onespad[:, 0] = 1.0; onespad[:, 7] = 1.0
    onespad[:, 27] = 1.0; onespad[:, 48] = 1.0
    d["onespad"] = onespad
    e127 = np.zeros((128, 1), np.float32)
    e127[127, 0] = 1.0
    d["e127"] = e127
    return d


def kernel(**inputs):
    nc, _, _ = build_program("full")
    in_maps = [_prep_inputs(inputs, c) for c in range(B)]
    res = run_bass_kernel_spmd(nc, in_maps, core_ids=list(range(B)))
    outs = [np.asarray(r["out"]).T for r in res.results]
    return np.ascontiguousarray(np.stack(outs, axis=0))


# version 09 (final)
# speedup vs baseline: 1.1055x; 1.1055x over previous
import numpy as np
import ml_dtypes
import concourse.bass as bass
import concourse.mybir as mybir
from concourse.bass_utils import run_bass_kernel_spmd

F32 = mybir.dt.float32
BF16 = mybir.dt.bfloat16
FP8 = mybir.dt.float8e4
AF = mybir.ActivationFunctionType
ALU = mybir.AluOpType
AX = mybir.AxisListType.X

B, T, D = 8, 4096, 1024
H, Kd, Vd, CONV = 4, 256, 512, 4
KEY, VAL = 1024, 2048
TILE = 512
NT = T // TILE
C = 128
NCH = TILE // C
DCH = D // 128
VCH = VAL // 128
LN_EPS, EPS = 1e-5, 1e-6

ENGS = ("te", "ve", "se", "sy", "gp")

CW = 80
C_S1, C_SG2X, C_SGBX, C_SX2, C_SG2X2 = 0, 1, 2, 3, 4
C_SSQ, C_SSK, C_SSO = 5, 9, 13
C_SZ, C_SZ2, C_SY, C_SY2 = 17, 18, 19, 20
C_C, C_BETA = 21, 25
C_M, C_V, C_S, C_SP, C_R = 29, 30, 31, 32, 33
C_T1, C_T2, C_T3 = 34, 35, 36
SEL0 = 38
CS_S, CS_ROWA, CS_ROWB, CS_ROWD, CS_RC = 38, 39, 40, 41, 42
CS_B, CS_E = 43, 47
CS_X1, CS_X2 = 51, 52
C_LNRK, C_LNRQ, C_LNB, C_LA, C_EXPB, C_EXPD = 54, 58, 62, 66, 70, 74

P_S, P_ROWA, P_ROWB, P_ROWD, P_RC = 0, 1, 2, 3, 4
P_B, P_E = 5, 9
P_X1, P_X2 = 13, 14

RW_STAT = 0
RW_SSQ, RW_SSK, RW_SSO = 5, 9, 13
RW_SZ, RW_SZ2, RW_SY, RW_SY2 = 17, 18, 19, 20
RW_G, RW_BP = 32, 64


class Prog:
    def __init__(self):
        self.ops = []
        self.last_w = {}
        self.readers = {}

    def emit(self, eng, fn, reads=(), writes=(), dma=False, force_sig=False,
             inc=None):
        idx = len(self.ops)
        deps = set()
        for b in reads:
            w = self.last_w.get(b)
            if w is not None:
                deps.add(w)
        for b in writes:
            w = self.last_w.get(b)
            if w is not None:
                deps.add(w)
            for r in self.readers.get(b, ()):
                deps.add(r)
        deps.discard(idx)
        self.ops.append(dict(eng=eng, fn=fn, deps=sorted(deps),
                             sig=bool(force_sig),
                             inc=(inc if inc is not None else
                                  (16 if dma else 1)),
                             dma=dma,
                             skey=(("d_" + writes[0]) if (dma and writes)
                                   else eng)))
        for b in reads:
            self.readers.setdefault(b, []).append(idx)
        for b in writes:
            self.last_w[b] = idx
            self.readers[b] = []
        return idx

    def build(self, nc, sems, spacers=None, es=None):
        ops = self.ops
        for op in ops:
            for d in op["deps"]:
                if ops[d]["eng"] != op["eng"] or ops[d]["dma"]:
                    ops[d]["sig"] = True
        gfn = getattr(self, "groupfn", None)
        for op in ops:
            if op["dma"] and gfn is not None:
                op["sg"], op["tot"] = gfn(op["skey"])
            else:
                op["sg"], op["tot"] = op["skey"], False
        dkeys = sorted({op["sg"] for op in ops if op["dma"]})
        assert len(dkeys) + len(ENGS) <= 100, len(dkeys)
        import sys as _sys
        print(f"[kernel] dma sems: {len(dkeys)}", file=_sys.stderr)
        for k in dkeys:
            if k not in sems:
                sems[k] = es.enter_context(nc.semaphore(k))
        cum = {e: 0 for e in ENGS}
        cum.update({k: 0 for k in dkeys})
        val = [0] * len(ops)
        for i, op in enumerate(ops):
            if op["sig"]:
                cum[op["sg"]] += op["inc"]
            val[i] = cum[op["sg"]]
        for i, op in enumerate(ops):
            if op["tot"]:
                val[i] = cum[op["sg"]]
            for d in op["deps"]:
                assert not (op["tot"] and ops[d]["sg"] == op["sg"]), \
                    "total-group member depends on same-group member"
        per_eng = {e: [] for e in ENGS}
        for i, op in enumerate(ops):
            per_eng[op["eng"]].append(i)

        def run_engine(eng_name, eng_obj):
            waited = {}
            prev_i = None
            for i in per_eng[eng_name]:
                op = ops[i]
                need = {}
                for d in op["deps"]:
                    pe = ops[d]["eng"]
                    if (pe != eng_name or ops[d]["dma"]) and ops[d]["sig"]:
                        sk = ops[d]["sg"]
                        need[sk] = max(need.get(sk, 0), val[d])
                for sk, v in sorted(need.items()):
                    if waited.get(sk, 0) < v:
                        eng_obj.wait_ge(sems[sk], v)
                        waited[sk] = v
                if (spacers and eng_name in spacers and prev_i is not None
                        and prev_i in op["deps"] and not ops[prev_i]["dma"]):
                    spacers[eng_name](eng_obj)
                inst = op["fn"](eng_obj)
                if op["sig"]:
                    inst.then_inc(sems[op["sg"]], op["inc"])
                prev_i = i

        with nc.Block() as block:
            @block.tensor
            def _(e):
                run_engine("te", e)

            @block.vector
            def _(e):
                run_engine("ve", e)

            @block.scalar
            def _(e):
                run_engine("se", e)

            @block.sync
            def _(e):
                run_engine("sy", e)

            @block.gpsimd
            def _(e):
                run_engine("gp", e)
                for en in sorted(cum):
                    if cum[en] > 0:
                        e.wait_ge(sems[en], cum[en])
        return len(ops)


def build_program(upto="full", taps=()):
    taps = set(taps)
    nc = bass.Bass("TRN2", target_bir_lowering=False)
    p = Prog()

    _konst = {"d_" + n for n in (
        "idf", "idb", "onesr", "onescb", "negbig", "maskY", "maskM",
        "hotq", "hotk", "hotv", "selo", "e127", "onespad",
        "g1c", "b1c", "rmc", "g2c", "b2wc", "fbc", "b2c",
        "onwc", "dtbp", "wabp", "convq", "convk", "convv", "nega")}

    def _sgroup(k):
        if k in _konst:
            return "g_konst", True
        if k.startswith("d_x2w"):
            return "g_x2sp", False
        if k.startswith("d_lxw"):
            return "g_lxsp", False
        if k.startswith("d_outw"):
            return "g_outw", False
        if k.startswith("d_ln0w"):
            return "g_ln0sp", False
        if k.startswith("d_q8w"):
            return "g_q8sp", False
        if k.endswith("_tap"):
            return "g_tap", False
        return k, False
    p.groupfn = _sgroup

    def din(name, shape, dt=F32):
        return nc.dram_tensor(name, shape, dt, kind="ExternalInput").ap()

    x0 = din("x0", [D, T])
    wq = din("wq", [D, KEY]); wk = din("wk", [D, KEY])
    wv = din("wv", [D, VAL]); wg = din("wg", [D, VAL])
    wo = din("wo", [VAL, D]); w2 = din("w2", [D, D])
    wfT = din("wfT", [D, D])
    wabp = din("wabp", [128, 68 * DCH], BF16)
    cqw = din("cqw", [128, 4 * DCH])
    ckw = din("ckw", [128, 4 * DCH])
    cvw = din("cvw", [128, 4 * VCH])
    g1c = din("g1c", [128, DCH]); b1c = din("b1c", [128, DCH])
    rmc = din("rmc", [128, DCH]); g2c = din("g2c", [128, DCH])
    b2wc = din("b2wc", [128, DCH]); fbc = din("fbc", [128, DCH])
    b2c = din("b2c", [128, DCH])
    onwc = din("onwc", [128, 4])
    dtbp = din("dtbp", [128, 1]); alogp = din("alogp", [128, 1])
    idf = din("idf", [128, 128]); idb = din("idb", [128, 128], BF16)
    onesr = din("onesr", [1, 128])
    onescb = din("onescb", [128, 1], BF16)
    negbig = din("negbig", [128, 128], BF16)
    maskY4 = din("maskY4", [128, 512], BF16)
    maskM4 = din("maskM4", [128, 512], BF16)
    hotq = din("hotq", [128, 9 * DCH], BF16)
    hotk = din("hotk", [128, 13 * DCH], BF16)
    hotv = din("hotv", [128, 17 * VCH], BF16)
    selo_f = din("selo_f", [16, 16 * 128])
    e127 = din("e127", [128, 1])
    onespad = din("onespad", [128, 49])

    out_d = nc.dram_tensor("out", [D, T], F32, kind="ExternalOutput").ap()

    def dint(name, shape, dt=F32):
        return nc.dram_tensor(name, shape, dt, kind="Internal").ap()

    ln0sp = dint("ln0sp", [D, T])
    q8sp = dint("q8sp", [D, T], FP8)
    x2sp = dint("x2sp", [D, T])
    lnx2sp = dint("lnx2sp", [D, T])
    wq_b = dint("wq_b", [D, KEY], BF16); wk_b = dint("wk_b", [D, KEY], BF16)
    wv_b = dint("wv_b", [D, VAL], BF16); wg_b = dint("wg_b", [D, VAL], BF16)
    wo_b = dint("wo_b", [VAL, D], BF16); w2p_b = dint("w2p_b", [D, D], BF16)
    wf8sp = dint("wf8sp", [D, D], FP8)
    cc1i = nc.dram_tensor("cc1i", [1, 128], F32)
    cc1o = nc.dram_tensor("cc1o", [1, 128], F32)
    cc2i = nc.dram_tensor("cc2i", [1, 128], F32)
    cc2o = nc.dram_tensor("cc2o", [1, 128], F32)

    tap_outs = {}
    def mktap(name, shape, dt=F32):
        tap_outs[name] = nc.dram_tensor("tap_" + name, shape, dt,
                                        kind="ExternalOutput").ap()

    for nm in taps:
        if nm == "sca":
            mktap("sca", [1, 16])
        elif nm in ("x1", "h", "z", "x2f"):
            mktap(nm, [D, T], F32 if nm in ("x1", "z", "x2f") else F32)
        elif nm in ("q", "k"):
            mktap(nm, [KEY, T])
        elif nm in ("v", "o", "gt"):
            mktap(nm, [VAL, T])
        elif nm == "gb":
            mktap(nm, [128, T])
        elif nm == "enc":
            mktap(nm, [D, T])
        elif nm in ("dY0", "dX0", "dTT", "dMT", "dEhY", "dEhM"):
            mktap(nm, [128, 512], F32)
        elif nm in ("dRp", "dDl"):
            mktap(nm, [128, 768], F32)
        elif nm == "dKh":
            mktap(nm, [128, 1024], F32)
        elif nm == "dRblk":
            mktap(nm, [16, 128], F32)
        elif nm == "dCols":
            mktap(nm, [128, 80], F32)
        elif nm == "dS":
            mktap(nm, [128, 4096], F32)
        elif nm == "dWp":
            mktap(nm, [128, 512], F32)
        elif nm in ("dRst0", "dRst1", "dRs2"):
            mktap(nm, [128, 512], F32)
        elif nm == "dCar":
            mktap(nm, [128, 96], F32)
        elif nm in ("dPreK", "dPreQ"):
            mktap(nm, [128, 515], F32)
        elif nm in ("dCvK",):
            mktap(nm, [128, 512], F32)
        elif nm == "dLn0":
            mktap(nm, [D, T], F32)
        elif nm == "dQ8":
            mktap(nm, [D, T], FP8)

    from contextlib import ExitStack
    es = ExitStack()
    def S(name, shape, dt):
        return es.enter_context(nc.sbuf_tensor(name, shape, dt))
    def PS(name, shape, dt=F32):
        return es.enter_context(nc.psum_tensor(name, shape, dt))

    t_idf = S("t_idf", [128, 128], F32)
    t_idb = S("t_idb", [128, 128], BF16)
    t_onesr = S("t_onesr", [1, 128], F32)
    t_onescb = S("t_onescb", [128, 1], BF16)
    t_negbig = S("t_negbig", [128, 128], BF16)
    t_maskY = S("t_maskY", [128, 512], BF16)
    t_maskM = S("t_maskM", [128, 512], BF16)
    t_hotq = S("t_hotq", [128, 9 * DCH], BF16)
    t_hotk = S("t_hotk", [128, 13 * DCH], BF16)
    t_hotv = S("t_hotv", [128, 17 * VCH], BF16)
    t_selo = S("t_selo", [16, 16 * 128], F32)
    t_e127 = S("t_e127", [128, 1], F32)
    t_onespad = S("t_onespad", [128, 49], F32)
    t_g1c = S("t_g1c", [128, DCH], F32)
    t_b1c = S("t_b1c", [128, DCH], F32)
    t_rmc = S("t_rmc", [128, DCH], F32)
    t_g2c = S("t_g2c", [128, DCH], F32)
    t_b2wc = S("t_b2wc", [128, DCH], F32)
    t_b2c = S("t_b2c", [128, DCH], F32)
    t_gelb = S("t_gelb", [128, DCH], F32)
    t_fbc = S("t_fbc", [128, DCH], F32)
    t_onwc = S("t_onwc", [128, 4], F32)
    t_dtbp = S("t_dtbp", [128, 1], F32)
    t_nega = S("t_nega", [128, 1], F32)
    t_statq = S("t_statq", [128, 10 * DCH], BF16)
    t_statz = S("t_statz", [128, 44], BF16)
    t_selP = S("t_selP", [16, 128 * DCH], F32)
    t_selw2 = S("t_selw2", [16, 128 * DCH], F32)
    t_wabp = S("t_wabp", [128, 68 * DCH], BF16)
    t_convq = S("t_convq", [128, 4 * DCH], F32)
    t_convk = S("t_convk", [128, 4 * DCH], F32)
    t_convv = S("t_convv", [128, 4 * VCH], F32)
    t_eps5 = S("t_eps5", [128, 1], F32)
    t_eps6 = S("t_eps6", [128, 1], F32)
    t_one1 = S("t_one1", [128, 1], F32)
    t_ones36 = S("t_ones36", [36, TILE], F32)
    t_w2cs = S("t_w2cs", [1, D], F32)
    t_w2csc = S("t_w2csc", [128, DCH], F32)
    t_ucol = S("t_ucol", [128, DCH], F32)
    t_wcol = S("t_wcol", [128, DCH], F32)
    t_cst = S("t_cst", [1, 8], F32)
    t_cstb = S("t_cstb", [128, 4], F32)
    t_am1 = S("t_am1", [128, 32], F32)
    t_am2 = S("t_am2", [128, 56], F32)
    t_rowcc = S("t_rowcc", [1, 128], F32)
    t_s1x = S("t_s1x", [1, 16], F32)
    t_gcc = S("t_gcc", [1, 128], F32)
    t_sxh = S("t_sxh", [128, 1], F32)
    t_sx2h = S("t_sx2h", [128, 1], F32)
    t_f1 = S("t_f1", [128, 1], F32)
    t_f2 = S("t_f2", [128, 1], F32)
    t_swh = S("t_swh", [128, 1], F32)
    t_stg16 = S("t_stg16", [128, 16], F32)

    r_wsl = [S(f"r_wsl{i}", [128, 1024], BF16) for i in range(4)]
    r_wo2 = [S(f"r_wo2{i}", [128, 1024], BF16) for i in range(2)]
    r_f8 = [S(f"r_f8{i}", [128, 1024], FP8) for i in range(3)]
    r_wf = [w[:, :].bitcast(F32) for w in r_wo2]
    r_x0 = [S(f"r_x0{i}", [128, TILE], F32) for i in range(2)]
    r_32 = [S(f"r_32{i}", [128, TILE], F32) for i in range(2)]
    r_sq = [S(f"r_sq{i}", [128, TILE], BF16) for i in range(2)]
    r_ff = [S(f"r_ff{i}", [128, TILE], BF16) for i in range(2)]
    r_ga = [S(f"r_ga{i}", [128, TILE], BF16) for i in range(2)]
    r_dgv = [S(f"r_dgv{i}", [128, 128], BF16) for i in range(2)]
    r_pre = [S(f"r_pre{i}", [128, 64 + TILE], BF16) for i in range(2)]

    t_x1 = S("t_x1", [128, DCH * TILE], F32)
    t_z = S("t_z", [128, DCH * TILE], F32)
    t_zb = S("t_zb", [128, DCH * TILE], BF16)
    t_h = S("t_h", [128, DCH * TILE], BF16)
    t_q8 = S("t_q8", [128, DCH * TILE], FP8)
    t_q = S("t_q", [128, DCH * TILE], BF16)
    t_k = S("t_k", [128, DCH * TILE], BF16)
    t_v = S("t_v", [128, VCH * TILE], BF16)
    t_o = S("t_o", [128, VCH * TILE], BF16)
    t_carry = S("t_carry", [128, (2 * DCH + VCH) * 64], BF16)
    t_rstage = S("t_rstage", [128, TILE], F32)
    t_rstg2 = S("t_rstg2", [128, TILE], F32)
    r_cols = [S(f"r_cols{i}", [128, CW], F32) for i in range(2)]
    r_rblk = [S(f"r_rblk{i}", [16, 128], F32) for i in range(2)]
    t_sbc = S("t_sbc", [128, TILE], F32)
    t_eqb = S("t_eqb", [128, 128], F32)
    t_cc4 = S("t_cc4", [1, 4], F32)
    t_cCb = S("t_cCb", [128, 4], F32)
    t_gCc = S("t_gCc", [128, 4], F32)

    t_hrb = S("t_hrb", [16, TILE], F32)
    t_zrb = S("t_zrb", [16, TILE], F32)
    t_S = S("t_S", [128, H * 2 * Vd], BF16)
    r_X = [S(f"r_X{i}", [128, 4 * C], BF16) for i in range(2)]
    r_Y = [S(f"r_Y{i}", [128, 4 * C], BF16) for i in range(2)]
    r_Zs = [S(f"r_Zs{i}", [128, 4 * C], BF16) for i in range(2)]
    t_TT = S("t_TT", [128, 4 * C], BF16)
    t_MT = S("t_MT", [128, 4 * C], BF16)
    t_Eh = S("t_Eh", [128, 4 * C], BF16)
    r_Rp = [S(f"r_Rp{i}", [128, Vd + Kd], BF16) for i in range(2)]
    r_Dl = [S(f"r_Dl{i}", [128, Vd + Kd], BF16) for i in range(2)]
    t_Kh = S("t_Kh", [128, H * Kd], BF16)
    r_Wp = [S(f"r_Wp{i}", [128, 2 * Kd], BF16) for i in range(2)]
    t_g0 = S("t_g0", [128, 512], BF16)
    r_Qt = [S(f"r_Qt{i}", [128, C], BF16) for i in range(2)]

    pb0 = PS("pb0", [128, 512]); pb1 = PS("pb1", [128, 512])
    pb2 = PS("pb2", [128, 512]); pb3 = PS("pb3", [128, 512])
    pb4 = PS("pb4", [128, 512])
    pdel = PS("pdel", [128, 1024])
    pbt = PS("pbt", [128, 1024], BF16)

    sems = {e: es.enter_context(nc.semaphore("sem_" + e)) for e in ENGS}
    t_space = S("t_space", [1, 128], F32)

    def _sp(e):
        e.tensor_single_scalar(out=t_space[0:1, 0:64],
                               in_=t_space[0:1, 64:128],
                               scalar=1.0, op=ALU.mult)
        e.tensor_single_scalar(out=t_space[0:1, 64:128],
                               in_=t_space[0:1, 0:64],
                               scalar=1.0, op=ALU.mult)
    spacers = {"ve": _sp, "gp": _sp}

    def dma(eng, dst, src, reads, writes, force_sig=True):
        p.emit(eng, lambda e, d=dst, s=src: e.dma_start(out=d, in_=s),
               reads=reads, writes=writes, dma=True, force_sig=True)

    def mm(out, lhsT, rhs, start, stop, reads, writes):
        p.emit("te", lambda e, o=out, a=lhsT, b=rhs, s1=start, s2=stop:
               e.matmul(o, a, b, start=s1, stop=s2),
               reads=reads, writes=writes)

    def trans(out, in_, ident, reads, writes):
        p.emit("te", lambda e, o=out, i=in_, d=ident: e.transpose(o, i, d),
               reads=reads, writes=writes)

    def cp(eng, out, in_, reads, writes):
        p.emit(eng, lambda e, o=out, i=in_: e.tensor_copy(out=o, in_=i),
               reads=reads, writes=writes)

    def act(out, in_, fn, reads, writes, bias=0.0, scale=1.0):
        p.emit("se", lambda e, o=out, i=in_, f=fn, b=bias, s=scale:
               e.activation(out=o, in_=i, func=f, bias=b, scale=s),
               reads=reads, writes=writes)

    def tt(eng, out, a, b, op, reads, writes):
        p.emit(eng, lambda e, o=out, x=a, y=b, f=op:
               e.tensor_tensor(out=o, in0=x, in1=y, op=f),
               reads=reads, writes=writes)

    def ts(eng, out, a, s1, s2, op0, op1, reads, writes):
        if s2 is None:
            p.emit(eng, lambda e, o=out, x=a, u=s1, f0=op0:
                   e.tensor_single_scalar(out=o, in_=x, scalar=u, op=f0),
                   reads=reads, writes=writes)
        else:
            p.emit(eng, lambda e, o=out, x=a, u=s1, v=s2, f0=op0, f1=op1:
                   e.tensor_scalar(out=o, in0=x, scalar1=u, scalar2=v,
                                   op0=f0, op1=f1),
                   reads=reads, writes=writes)

    def stt(eng, out, in0, scalar, in1, op0, op1, reads, writes):
        p.emit(eng, lambda e, o=out, x=in0, s=scalar, y=in1, f0=op0, f1=op1:
               e.scalar_tensor_tensor(out=o, in0=x, scalar=s, in1=y,
                                      op0=f0, op1=f1),
               reads=reads, writes=writes)

    def red(out, in_, op, absv, reads, writes):
        p.emit("ve", lambda e, o=out, i=in_, f=op, a=absv:
               e.tensor_reduce(o, i, AX, f, apply_absolute_value=a),
               reads=reads, writes=writes)

    def recip(out, in_, reads, writes):
        p.emit("ve", lambda e, o=out, i=in_: e.reciprocal(out=o, in_=i),
               reads=reads, writes=writes)

    def mset(eng, ap_, v, writes):
        p.emit(eng, lambda e, a=ap_, x=v: e.memset(a, x), writes=writes)

    def scan(out, d0, d1, reads, writes):
        p.emit("ve", lambda e, o=out, a=d0, b=d1:
               e.tensor_tensor_scan(out=o, data0=a, data1=b, initial=0.0,
                                    op0=ALU.mult, op1=ALU.add),
               reads=reads, writes=writes)

    def pzero(pb, pname):
        p.emit("ve", lambda e, a=pb: e.memset(a[:, 0:512], 0.0),
               reads=(), writes=(pname,))

    def tap(name, src_ap, reads):
        if name in tap_outs:
            dma("gp", tap_outs[name], src_ap, reads=reads,
                writes=(name + "_tap",), force_sig=True)

    def selbc(out, pos, rblk, rname, start, stop, wname, extra_reads=()):
        mm(out, t_selo[:, 128 * pos:128 * (pos + 1)], rblk[0:16, :],
           start, stop, ("selo", rname) + tuple(extra_reads), (wname,))

    for dst, src, nm in [
        (t_idf, idf, "idf"), (t_idb, idb, "idb"), (t_onesr, onesr, "onesr"),
        (t_onescb, onescb, "onescb"), (t_negbig, negbig, "negbig"),
        (t_maskY, maskY4, "maskY"), (t_maskM, maskM4, "maskM"),
        (t_hotq, hotq, "hotq"), (t_hotk, hotk, "hotk"), (t_hotv, hotv, "hotv"),
        (t_selo, selo_f, "selo"), (t_e127, e127, "e127"),
        (t_onespad, onespad, "onespad"),
        (t_g1c, g1c, "g1c"), (t_b1c, b1c, "b1c"), (t_rmc, rmc, "rmc"),
        (t_g2c, g2c, "g2c"), (t_b2wc, b2wc, "b2wc"), (t_fbc, fbc, "fbc"),
        (t_b2c, b2c, "b2c"),
        (t_onwc, onwc, "onwc"), (t_dtbp, dtbp, "dtbp"),
        (t_wabp, wabp, "wabp"), (t_convq, cqw, "convq"),
        (t_convk, ckw, "convk"), (t_convv, cvw, "convv"),
    ]:
        dma("sy", dst[:, :], src, (), (nm,))
    dma("sy", t_nega[:, :], alogp, (), ("nega",))
    act(t_nega[32:36, :], t_nega[32:36, :], AF.Exp, ("nega",), ("nega",))
    ts("ve", t_nega[32:36, :], t_nega[32:36, :], -1.0, None, ALU.mult,
       ALU.mult, ("nega",), ("nega",))
    mset("ve", t_space[:, :], 0.0, ("space",))
    mset("gp", t_space[:, :], 0.0, ("space",))
    for tns, v, nm in [(t_eps5, LN_EPS, "eps5"), (t_eps6, EPS, "eps6"),
                       (t_one1, 1.0, "one1")]:
        mset("ve", tns[:, :], v, (nm,))
    mset("ve", t_ones36[:, :], 1.0, ("ones36",))
    mset("ve", t_S[:, :], 0.0, ("S0", "S1", "S2", "S3"))
    mset("ve", t_carry[:, :], 0.0, ("carry",))
    mset("ve", t_g0[:, :], 0.0, ("g0",))
    mset("ve", t_statq[:, :], 0.0, ("statq",))
    mset("ve", t_statz[:, :], 0.0, ("statz",))
    mset("ve", t_am1[:, :], 0.0, ("am1",))
    mset("ve", t_am2[:, :], 0.0, ("am2",))
    for i_ in range(2):
        mset("ve", r_cols[i_][:, :], 0.0, (f"rcols{i_}",))
        mset("ve", r_rblk[i_][:, :], 0.0, (f"rrblk{i_}",))
    mset("ve", t_hrb[:, :], 0.0, ("hrb",))
    mset("ve", t_zrb[:, :], 0.0, ("zrb",))
    mset("ve", t_rstage[:, :], 0.0, ("rstage",))
    mset("ve", t_rstg2[:, :], 0.0, ("rstg2",))
    mset("ve", t_sbc[:, :], 0.0, ("osbc", "sbc"))

    for c in range(DCH):
        b0 = 10 * c
        cp("ve", t_statq[:, b0:b0 + 1], t_onescb[:, :], ("onescb",), ("statq",))
        tt("ve", t_statq[:, b0 + 1:b0 + 2], t_g1c[:, c:c + 1],
           t_g1c[:, c:c + 1], ALU.mult, ("g1c",), ("statq",))
        tt("ve", t_statq[:, b0 + 2:b0 + 3], t_g1c[:, c:c + 1],
           t_b1c[:, c:c + 1], ALU.mult, ("g1c", "b1c"), ("statq",))
        cp("ve", t_statq[:, b0 + 8:b0 + 9], t_onescb[:, :], ("onescb",),
           ("statq",))
        tt("ve", t_statq[:, b0 + 9:b0 + 10], t_g1c[:, c:c + 1],
           t_g1c[:, c:c + 1], ALU.mult, ("g1c",), ("statq",))
    cp("ve", t_statz[:, 17:18], t_onescb[:, :], ("onescb",), ("statz",))
    cp("ve", t_statz[:, 18 + 18:18 + 19], t_onescb[:, :], ("onescb",),
       ("statz",))
    t_staty = S("t_staty", [128, 42], BF16)
    mset("ve", t_staty[:, :], 0.0, ("staty",))
    cp("ve", t_staty[:, 19:20], t_onescb[:, :], ("onescb",), ("staty",))
    cp("ve", t_staty[:, 20 + 20:20 + 21], t_onescb[:, :], ("onescb",),
       ("staty",))

    tt("ve", t_ucol[:, :], t_g1c[:, :], t_rmc[:, :], ALU.mult,
       ("g1c", "rmc"), ("ucol",))
    tt("ve", t_wcol[:, :], t_b1c[:, :], t_rmc[:, :], ALU.mult,
       ("b1c", "rmc"), ("wcol",))
    t_bcolb = S("t_bcolb", [128, 3 * DCH], BF16)
    for c in range(DCH):
        tt("ve", t_bcolb[:, 3 * c:3 * c + 1], t_g1c[:, c:c + 1],
           t_g1c[:, c:c + 1], ALU.mult, ("g1c",), ("bcolb",))
        tt("ve", t_bcolb[:, 3 * c + 1:3 * c + 2], t_g1c[:, c:c + 1],
           t_b1c[:, c:c + 1], ALU.mult, ("g1c", "b1c"), ("bcolb",))
        tt("ve", t_bcolb[:, 3 * c + 2:3 * c + 3], t_b1c[:, c:c + 1],
           t_b1c[:, c:c + 1], ALU.mult, ("b1c",), ("bcolb",))
    for j in range(3):
        for c in range(DCH):
            mm(pb3[0:1, 128 * j:128 * j + 1], t_bcolb[:, 3 * c + j:3 * c + j + 1],
               t_onescb[:, :], c == 0, c == DCH - 1, ("bcolb", "onescb"),
               ("pb3",))
    cp("ve", t_cst[:, 0:1], pb3[0:1, 0:1], ("pb3",), ("cst",))
    cp("ve", t_cst[:, 1:2], pb3[0:1, 128:129], ("pb3",), ("cst",))
    cp("ve", t_cst[:, 2:3], pb3[0:1, 256:257], ("pb3",), ("cst",))
    ts("ve", t_cst[:, 0:3], t_cst[:, 0:3], 1.0 / D, None, ALU.mult, ALU.mult,
       ("cst",), ("cst",))
    for j_ in range(3):
        mm(pb4[:, 132 + j_:133 + j_], t_onesr[:, :], t_cst[0:1, j_:j_ + 1],
           True, True, ("onesr", "cst"), ("pb4",))
    cp("ve", t_cstb[:, 0:3], pb4[:, 132:135], ("pb4",), ("cstb",))


    p.emit("ve", lambda e: e.memset(pdel[:, 0:1024], 0.0), reads=(),
           writes=("pdelw",))
    specs = [(wq, wq_b, KEY, False), (wk, wk_b, KEY, False),
             (wv, wv_b, VAL, False), (wg, wg_b, VAL, False),
             (wo, wo_b, D, False), (w2, w2p_b, D, True)]
    for si, (src, dst, width, isw2) in enumerate(specs):
        nrow = src.shape[0] // 128
        for rc in range(nrow):
            for hf in range(width // 512):
                ii = (rc * (width // 512) + hf) % 2
                sl, bn = r_wf[ii], f"rwo2{ii}"
                bsl, bbn = r_wsl[ii], f"rwsl{ii}"
                dma("sy", sl[:, 0:512], src[128 * rc:128 * (rc + 1),
                                            512 * hf:512 * (hf + 1)],
                    (bn,), (bn,))
                if isw2:
                    mm(pdel[0:1, 512 * (hf % 2):512 * (hf % 2) + 512],
                       t_b2c[:, rc:rc + 1], sl[:, 0:512],
                       False, False, ("b2c", bn), ("pdelw",))
                    ts("ve", bsl[:, 0:512], sl[:, 0:512],
                       t_g2c[:, rc:rc + 1], None,
                       ALU.mult, ALU.mult, (bn, "g2c"), (bbn,))
                else:
                    cp("ve", bsl[:, 0:512], sl[:, 0:512], (bn,), (bbn,))
                dma("gp", dst[128 * rc:128 * (rc + 1),
                              512 * hf:512 * (hf + 1)], bsl[:, 0:512],
                    (bbn,), (f"wh{si}",))
    cp("ve", t_w2cs[:, :], pdel[0:1, 0:1024], ("pdelw",), ("w2cs",))
    for c in range(DCH):
        trans(pb4[:, 0:1], t_w2cs[0:1, 128 * c:128 * (c + 1)],
              t_idf[0:1, 0:1], ("w2cs", "idf"), ("pb4",))
        cp("ve", t_gelb[:, c:c + 1], pb4[:, 0:1], ("pb4",), ("gelb",))
    for c in range(DCH):
        tt("ve", t_gelb[:, c:c + 1], t_gelb[:, c:c + 1], t_b2wc[:, c:c + 1],
           ALU.add, ("gelb", "b2wc"), ("gelb",))
    for rc in range(DCH):
        sl, bn = r_wsl[rc % 2], f"rwsl{rc % 2}"
        dma("sy", sl[:, :], w2p_b[128 * rc:128 * (rc + 1), 0:1024],
            (bn, "wh5"), (bn,))
        mm(pdel[0:1, 0:512], t_onescb[:, :], sl[:, 0:512], rc == 0, False,
           ("onescb", bn), ("pdel",))
        mm(pdel[0:1, 512:1024], t_onescb[:, :], sl[:, 512:1024], False,
           rc == DCH - 1, ("onescb", bn), ("pdel",))
    cp("ve", t_w2cs[:, :], pdel[0:1, 0:1024], ("pdel",), ("w2cs",))
    for c in range(DCH):
        trans(pb4[:, 0:1], t_w2cs[0:1, 128 * c:128 * (c + 1)],
              t_idf[0:1, 0:1], ("w2cs", "idf"), ("pb4",))
        cp("ve", t_w2csc[:, c:c + 1], pb4[:, 0:1], ("pb4",), ("w2csc",))

    for c in range(DCH):
        mset("ve", t_stg16[:, :], 0.0, ("stg16",))
        cp("ve", t_stg16[:, P_ROWB:P_ROWB + 1], t_ucol[:, c:c + 1],
           ("ucol",), ("stg16",))
        cp("ve", t_stg16[:, P_RC:P_RC + 1], t_wcol[:, c:c + 1],
           ("wcol",), ("stg16",))
        trans(pb4[0:16, 0:128], t_stg16[:, :], t_idf[:, :],
              ("stg16", "idf"), ("pb4",))
        cp("ve", t_selP[:, 128 * c:128 * (c + 1)], pb4[0:16, 0:128],
           ("pb4",), ("selP",))
        mset("ve", t_stg16[:, :], 0.0, ("stg16",))
        cp("ve", t_stg16[:, P_X2:P_X2 + 1], t_w2csc[:, c:c + 1],
           ("w2csc",), ("stg16",))
        trans(pb4[0:16, 0:128], t_stg16[:, :], t_idf[:, :],
              ("stg16", "idf"), ("pb4",))
        cp("ve", t_selw2[:, 128 * c:128 * (c + 1)], pb4[0:16, 0:128],
           ("pb4",), ("selw2",))

    for rc in range(DCH):
        for hf in range(2):
            sl, bn = r_wf[(2 * rc + hf) % 2], f"rwo2{(2 * rc + hf) % 2}"
            dma("sy", sl[:, 0:512], wfT[128 * rc:128 * (rc + 1),
                                        512 * hf:512 * (hf + 1)],
                (bn,), (bn,))
            red(t_am1[:, 1 + 2 * rc + hf:2 + 2 * rc + hf], sl[:, 0:512],
                ALU.max, True, (bn,), ("am1",))
    red(t_am1[:, 0:1], t_am1[:, 1:1 + 2 * DCH], ALU.max, False, ("am1",),
        ("am1c",))
    trans(pb4[0:32, 0:128], t_am1[:, 0:32], t_idf[:, :], ("am1c", "idf"),
          ("pb4",))
    cp("ve", t_rowcc[:, :], pb4[0:1, 0:128], ("pb4",), ("rowcc",))
    red(t_s1x[:, 0:1], t_rowcc[:, :], ALU.max, False, ("rowcc",), ("s1x",))
    ts("ve", t_s1x[:, 1:2], t_s1x[:, 0:1], 1e-12, None, ALU.max, ALU.max,
       ("s1x",), ("s1x",))
    recip(t_s1x[:, 2:3], t_s1x[:, 1:2], ("s1x",), ("s1x",))
    ts("ve", t_s1x[:, 3:4], t_s1x[:, 2:3], 448.0, 448.0, ALU.mult, ALU.min,
       ("s1x",), ("s1x",))
    ts("ve", t_s1x[:, 4:5], t_s1x[:, 3:4], 0.5, None, ALU.mult, ALU.mult,
       ("s1x",), ("s1x",))
    recip(t_s1x[:, 5:6], t_s1x[:, 3:4], ("s1x",), ("s1x",))
    mm(pb4[:, 128:129], t_onesr[:, :], t_s1x[:, 4:5], True, True,
       ("onesr", "s1x"), ("pb4",))
    cp("ve", t_swh[:, :], pb4[:, 128:129], ("pb4",), ("swh",))
    for rc in range(DCH):
        for hf in range(2):
            sl, bn = r_wf[(2 * rc + hf) % 2], f"rwo2{(2 * rc + hf) % 2}"
            q8b, qn = r_f8[(2 * rc + hf) % 2], f"rf8{(2 * rc + hf) % 2}"
            dma("sy", sl[:, 0:512], wfT[128 * rc:128 * (rc + 1),
                                        512 * hf:512 * (hf + 1)],
                (bn,), (bn,))
            act(q8b[:, 0:512], sl[:, 0:512], AF.Copy, (bn, "swh"), (qn,),
                scale=t_swh[:, :])
            dma("gp", wf8sp[128 * rc:128 * (rc + 1),
                            512 * hf:512 * (hf + 1)], q8b[:, 0:512], (qn,),
                ("wf8",))

    if upto == "setup":
        for oc in range(DCH):
            dma("gp", out_d[128 * oc:128 * (oc + 1), 0:TILE],
                t_rstage[:, 0:TILE], ("rstage",), ("outdummy",))
        n = p.build(nc, sems, spacers, es)
        import sys
        print(f"[kernel] built {n} ops (setup)", file=sys.stderr)
        es.close()
        return nc, tap_outs, p

    def x_3d(hbm, t0):
        return hbm.rearrange("(c p) t -> p c t", p=128)[:, :, t0:t0 + TILE]

    def sb_3d(tns):
        return tns[:, :].rearrange("p (c t) -> p c t", c=DCH)

    def phaseA_tile(ti):
        t0 = ti * TILE
        dma("sy", sb_3d(t_x1), x_3d(x0, t0), ("x1",), ("x1",))
        pzero(pb3, "pb3")
        for c in range(DCH):
            cs = slice(TILE * c, TILE * (c + 1))
            sq, sqn = r_32[c % 2], f"r32{c % 2}"
            tt("ve", sq[:, :], t_x1[:, cs], t_x1[:, cs], ALU.mult, ("x1",),
               (sqn,))
            mm(pb3[0:1, 0:TILE], t_onespad[:, 0:1], t_x1[:, cs],
               False, False, ("onespad", "x1"), ("pb3",))
            mm(pb3[0:4, 0:TILE], t_onespad[:, 4:8], sq[:, :],
               False, c == DCH - 1, ("onespad", sqn), ("pb3",))
        cp("ve", t_rstage[0:5, :], pb3[0:5, 0:TILE], ("pb3",), ("rstage",))
        for ci in range(NCH):
            cur = ci % 2
            col = r_cols[cur]
            coln = f"rcols{cur}"
            rblk = r_rblk[cur]
            rbn = f"rrblk{cur}"
            ccs = slice(ci * C, (ci + 1) * C)
            trans(pb4[:, 0:128], t_rstage[:, ccs], t_idf[:, :],
                  ("rstage", "idf"), ("pb4",))
            cp("ve", col[:, 0:21], pb4[:, 0:21], ("pb4",), (coln,))
            ts("ve", col[:, C_M:C_M + 1], col[:, C_S1:C_S1 + 1], 1.0 / D,
               None, ALU.mult, ALU.mult, (coln,), (coln,))
            ts("ve", col[:, C_T1:C_T1 + 1], col[:, C_SX2:C_SX2 + 1], 1.0 / D,
               None, ALU.mult, ALU.mult, (coln,), (coln,))
            tt("ve", col[:, C_T2:C_T2 + 1], col[:, C_M:C_M + 1],
               col[:, C_M:C_M + 1], ALU.mult, (coln,), (coln,))
            tt("ve", col[:, C_V:C_V + 1], col[:, C_T1:C_T1 + 1],
               col[:, C_T2:C_T2 + 1], ALU.subtract, (coln,), (coln,))
            act(col[:, C_T3:C_T3 + 1], col[:, C_V:C_V + 1], AF.Ln,
                (coln, "eps5"), (coln,), bias=t_eps5[:, :])
            act(col[:, CS_S:CS_S + 1], col[:, C_T3:C_T3 + 1], AF.Exp,
                (coln,), (coln,), scale=-0.5)
            tt("ve", col[:, CS_ROWD:CS_ROWD + 1], col[:, C_M:C_M + 1],
               col[:, CS_S:CS_S + 1], ALU.mult, (coln,), (coln,))
            ts("ve", col[:, CS_ROWD:CS_ROWD + 1], col[:, CS_ROWD:CS_ROWD + 1],
               -1.0, None, ALU.mult, ALU.mult, (coln,), (coln,))
            trans(pb4[0:16, 128:256], col[:, SEL0:SEL0 + 16], t_idf[:, :],
                  (coln, "idf"), ("pb4",))
            cp("ve", rblk[0:16, :], pb4[0:16, 128:256], ("pb4",), (rbn,))
            selbc(pb0[:, 0:C], P_S, rblk, rbn, True, True, "pb0")
            selbc(pb1[:, 0:C], P_ROWD, rblk, rbn, True, True, "pb1")
            for c in range(DCH):
                cs2 = slice(TILE * c + ci * C, TILE * c + (ci + 1) * C)
                tt("ve", t_z[:, cs2], t_x1[:, cs2], pb0[:, 0:C], ALU.mult,
                   ("x1", "pb0"), ("z",))
                tt("ve", t_z[:, cs2], t_z[:, cs2], pb1[:, 0:C], ALU.add,
                   ("z", "pb1"), ("z",))
        red(t_am1[:, 17 + ti:18 + ti], t_z[:, :], ALU.max, True, ("z",),
            (f"amT{ti}",))
        dma("gp", x_3d(ln0sp, t0), sb_3d(t_z), ("z",), (f"ln0w{ti}",),
            force_sig=True)

    def amax_collective(which):
        tam, amn = (t_am1, "ax1") if which == 1 else (t_am2, "ax2")
        cci, cco = (cc1i, cc1o) if which == 1 else (cc2i, cc2o)
        sx, tf = (t_sxh, t_f1) if which == 1 else (t_sx2h, t_f2)
        rd = tuple(f"amT{ti}" if which == 1 else f"bmT{ti}"
                   for ti in range(NT))
        red(tam[:, 0:1], tam[:, 17:17 + NT] if which == 1
            else tam[:, 8:8 + NT], ALU.max, False,
            rd + (("am1",) if which == 1 else ("am2",)), (amn + "c",))
        trans(pb4[0:32, 0:128], tam[:, 0:32], t_idf[:, :],
              (amn + "c", "idf", "am1" if which == 1 else "am2"),
              ("pb4",))
        cp("ve", t_rowcc[:, :], pb4[0:1, 0:128], ("pb4",), (amn + "r",))
        red(t_s1x[:, 6:7], t_rowcc[:, :], ALU.max, False, (amn + "r",),
            (amn + "v",))
        mset("gp", t_gcc[:, :], 0.0, (amn + "g",))
        cp("gp", t_gcc[:, 0:1], t_s1x[:, 6:7], (amn + "v",), (amn + "g",))
        dma("gp", cci.ap()[:, :], t_gcc[:, :], (amn + "g",), (amn + "i",))
        p.emit("gp", lambda e, ci_=cci, co_=cco: e.collective_compute(
            "AllReduce", ALU.max, replica_groups=[list(range(8))],
            ins=[ci_.ap().opt()], outs=[co_.ap().opt()]),
            reads=(amn + "i",), writes=(amn + "o",), dma=True, force_sig=True,
            inc=1)
        dma("gp", t_gcc[:, :], cco.ap()[:, :], (amn + "o",), (amn + "gg",))
        ts("ve", t_s1x[:, 7:8], t_gcc[0:1, 0:1], 1e-12, None, ALU.max,
           ALU.max, (amn + "gg",), (amn + "a",))
        recip(t_s1x[:, 8:9], t_s1x[:, 7:8], (amn + "a",), (amn + "b",))
        ts("ve", t_s1x[:, 9:10], t_s1x[:, 8:9], 448.0, 448.0, ALU.mult,
           ALU.min, (amn + "b",), (amn + "sx",))
        ts("ve", t_s1x[:, 10:11], t_s1x[:, 9:10], 0.5, None, ALU.mult,
           ALU.mult, (amn + "sx",), (amn + "sxh",))
        recip(t_s1x[:, 11:12], t_s1x[:, 9:10], (amn + "sx",), (amn + "rx",))
        tt("ve", t_s1x[:, 12:13], t_s1x[:, 11:12], t_s1x[:, 5:6], ALU.mult,
           (amn + "rx", "s1x"), (amn + "ff",))
        ts("ve", t_s1x[:, 12:13], t_s1x[:, 12:13], 4.0, None, ALU.mult,
           ALU.mult, (amn + "ff",), (amn + "ff",))
        mm(pb4[:, 129:130], t_onesr[:, :], t_s1x[:, 10:11], True, True,
           ("onesr", amn + "sxh"), ("pb4",))
        cp("ve", sx[:, :], pb4[:, 129:130], ("pb4",), ("sxh%d" % which,))
        mm(pb4[:, 130:131], t_onesr[:, :], t_s1x[:, 12:13], True, True,
           ("onesr", amn + "ff"), ("pb4",))
        cp("ve", tf[:, :], pb4[:, 130:131], ("pb4",), ("fd%d" % which,))

    def phaseA2_tile(ti):
        t0 = ti * TILE
        dma("sy", sb_3d(t_z), x_3d(ln0sp, t0),
            ("z",) + tuple(f"ln0w{t}" for t in range(NT)),
            ("z",))
        act(t_q8[:, :], t_z[:, :], AF.Copy, ("z", "sxh1"), ("q8t",),
            scale=t_sxh[:, :])
        dma("gp", x_3d(q8sp, t0).bitcast(FP8), sb_3d(t_q8), ("q8t",),
            (f"q8w{ti}",), force_sig=True)

    for ti in range(NT):
        phaseA_tile(ti)
    if upto == "A1":
        for oc in range(DCH):
            dma("gp", out_d[128 * oc:128 * (oc + 1), 0:TILE],
                t_rstage[:, 0:TILE], ("rstage",), ("outdummy",))
        n = p.build(nc, sems, spacers, es)
        import sys
        print(f"[kernel] built {n} ops (A1)", file=sys.stderr)
        es.close()
        return nc, tap_outs, p
    if upto != "A2nc":
        amax_collective(1)
    else:
        mset("ve", t_sxh[:, :], 0.001, ("sxh1",))
    for ti in range(NT):
        phaseA2_tile(ti)
    if upto == "A2nc":
        for oc in range(DCH):
            dma("gp", out_d[128 * oc:128 * (oc + 1), 0:TILE],
                t_rstage[:, 0:TILE], ("rstage",), ("outdummy",))
        n = p.build(nc, sems, spacers, es)
        import sys
        print(f"[kernel] built {n} ops (A2nc)", file=sys.stderr)
        es.close()
        return nc, tap_outs, p

    if "sca" in taps:
        tap("sca", t_s1x[:, :], ("fd1", "sxh1", "s1x"))

    if upto == "A":
        for oc in range(DCH):
            dma("gp", out_d[128 * oc:128 * (oc + 1), 0:TILE], t_z[:, 0:TILE],
                ("z",), ("outdummy",))
        n = p.build(nc, sems, spacers, es)
        import sys
        print(f"[kernel] built {n} ops (upto=A)", file=sys.stderr)
        es.close()
        return nc, tap_outs, p

    def w_slab(dst, dstname, whbm, whname, oc, nic, wid):
        src = whbm.rearrange("(ic p) o -> p ic o", p=128)[
            :, :, 128 * oc:128 * (oc + 1)]
        dma("sy", dst[:, 0:nic * 128].rearrange("p (ic j) -> p ic j", ic=nic),
            src, (dstname, whname), (dstname,))

    def ffn_stage(ti, q8tile, q8name, resid_ring, residname, dq_col, dqn,
                  xout, xoutname, wsp, wspname):
        for oc in range(DCH):
            sl, sn = r_f8[oc % 3], f"rf8{oc % 3}"
            w_slab(sl, sn, wsp, wspname, oc, DCH, D)
            pb, pn = (pb0, "pb0") if oc % 2 == 0 else (pb1, "pb1")
            for ic in range(DCH):
                mm(pb[:, 0:TILE], sl[:, 128 * ic:128 * (ic + 1)],
                   q8tile[:, TILE * ic:TILE * (ic + 1)], ic == 0,
                   ic == DCH - 1, (sn, q8name), (pn,))
            ff, fn_ = r_ff[oc % 2], f"rff{oc % 2}"
            ts("ve", ff[:, :], pb[:, 0:TILE], dq_col[:, :],
               t_fbc[:, oc:oc + 1], ALU.mult, ALU.add,
               (pn, dqn, "fbc"), (fn_,))
            rx = resid_ring(oc)
            stt("ve", xout[:, TILE * oc:TILE * (oc + 1)], ff[:, :], 0.5,
                rx[0], ALU.mult, ALU.add, (fn_, rx[1]),
                (xoutname,))

    def stats_group(rhs_t, rhsname, kind):
        for c in range(DCH):
            cs = slice(TILE * c, TILE * (c + 1))
            xb, xbn = r_ff[c % 2], f"rff{c % 2}"
            sq, sqn = r_sq[c % 2], f"rsq{c % 2}"
            cp("ve", xb[:, :], rhs_t[:, cs], (rhsname,), (xbn,))
            tt("ve", sq[:, :], xb[:, :], xb[:, :], ALU.mult, (xbn,), (sqn,))
            if kind == "x1":
                mm(pb3[0:5, 0:TILE], t_statq[:, 10 * c:10 * c + 5], xb[:, :],
                   False, False, ("statq", xbn), ("pb3",))
                mm(pb3[0:5, 0:TILE], t_statq[:, 10 * c + 5:10 * c + 10],
                   sq[:, :], False, False, ("statq", sqn), ("pb3",))
            elif kind == "z":
                mm(pb3[0:18, 0:TILE], t_statz[:, 0:18], xb[:, :],
                   False, False, ("statz", xbn), ("pb3",))
                mm(pb3[0:19, 0:TILE], t_statz[:, 18:37], sq[:, :],
                   False, False, ("statz", sqn), ("pb3",))
            else:
                mm(pb3[0:20, 0:TILE], t_staty[:, 0:20], xb[:, :],
                   False, False, ("staty", xbn), ("pb3",))
                mm(pb3[0:21, 0:TILE], t_staty[:, 20:41], sq[:, :],
                   False, False, ("staty", sqn), ("pb3",))

    def stats_group_bf(rhs_t, rhsname, kind):
        for c in range(DCH):
            cs = slice(TILE * c, TILE * (c + 1))
            sq, sqn = r_sq[c % 2], f"rsq{c % 2}"
            tt("ve", sq[:, :], rhs_t[:, cs], rhs_t[:, cs], ALU.mult,
               (rhsname,), (sqn,))
            mm(pb3[0:18, 0:TILE], t_statz[:, 0:18], rhs_t[:, cs],
               False, False, ("statz", rhsname), ("pb3",))
            mm(pb3[0:19, 0:TILE], t_statz[:, 18:37], sq[:, :],
               False, False, ("statz", sqn), ("pb3",))

    def sweep1_tile(ti):
        t0 = ti * TILE
        TL = f"t{ti}_"
        dma("sy", sb_3d(t_q8), x_3d(q8sp, t0).bitcast(FP8),
            ("q8t",) + tuple(f"q8w{t}" for t in range(NT)), ("q8t",))
        def resid_x0(oc):
            rx, rxn = r_x0[oc % 2], f"rx0{oc % 2}"
            dma("sy", rx[:, :], x0[128 * oc:128 * (oc + 1), t0:t0 + TILE],
                (rxn,), (rxn,))
            return rx[:, :], rxn
        ffn_stage(ti, t_q8, "q8t", resid_x0, None, t_f1, "fd1", t_x1, "x1",
                  wf8sp, "wf8")
        if upto == "S1f":
            return
        if "x1" in tap_outs:
            dma("gp", x_3d(tap_outs["x1"], t0), sb_3d(t_x1), ("x1",),
                ("x1tap",), force_sig=True)
        pzero(pb3, "pb3")
        stats_group(t_x1, "x1", "x1")
        cp("ve", t_rstage[0:5, :], pb3[0:5, 0:TILE], ("pb3",), ("rstage",))
        hrblks = []
        for ci in range(NCH):
            cur = ci % 2
            col, coln = r_cols[cur], f"rcols{cur}"
            rblk, rbn = r_rblk[cur], f"rrblk{cur}"
            ccs = slice(ci * C, (ci + 1) * C)
            trans(pb4[:, 0:128], t_rstage[:, ccs], t_idf[:, :],
                  ("rstage", "idf"), ("pb4",))
            cp("ve", col[:, 0:21], pb4[:, 0:21], ("pb4",), (coln,))
            ts("ve", col[:, C_M:C_M + 1], col[:, C_S1:C_S1 + 1], 1.0 / D,
               None, ALU.mult, ALU.mult, (coln,), (coln,))
            ts("ve", col[:, C_T1:C_T1 + 1], col[:, C_SX2:C_SX2 + 1], 1.0 / D,
               None, ALU.mult, ALU.mult, (coln,), (coln,))
            tt("ve", col[:, C_T2:C_T2 + 1], col[:, C_M:C_M + 1],
               col[:, C_M:C_M + 1], ALU.mult, (coln,), (coln,))
            tt("ve", col[:, C_V:C_V + 1], col[:, C_T1:C_T1 + 1],
               col[:, C_T2:C_T2 + 1], ALU.subtract, (coln,), (coln,))
            act(col[:, C_T3:C_T3 + 1], col[:, C_V:C_V + 1], AF.Ln,
                (coln, "eps5"), (coln,), bias=t_eps5[:, :])
            act(col[:, CS_S:CS_S + 1], col[:, C_T3:C_T3 + 1], AF.Exp,
                (coln,), (coln,), scale=-0.5)
            tt("ve", col[:, C_T1:C_T1 + 1], col[:, CS_S:CS_S + 1],
               col[:, CS_S:CS_S + 1], ALU.mult, (coln,), (coln,))
            tt("ve", col[:, C_T2:C_T2 + 1], col[:, C_V:C_V + 1],
               col[:, C_T1:C_T1 + 1], ALU.mult, (coln,), (coln,))
            act(col[:, C_T3:C_T3 + 1], col[:, C_T2:C_T2 + 1], AF.Ln,
                (coln, "eps5"), (coln,), bias=t_eps5[:, :])
            act(col[:, C_SP:C_SP + 1], col[:, C_T3:C_T3 + 1], AF.Exp,
                (coln,), (coln,), scale=-0.5)
            ts("ve", col[:, C_T1:C_T1 + 1], col[:, C_SG2X:C_SG2X + 1],
               1.0 / D, None, ALU.mult, ALU.mult, (coln,), (coln,))
            tt("ve", col[:, C_T2:C_T2 + 1], col[:, C_M:C_M + 1],
               col[:, C_T1:C_T1 + 1], ALU.mult, (coln,), (coln,))
            ts("ve", col[:, C_T1:C_T1 + 1], col[:, C_SG2X2:C_SG2X2 + 1],
               1.0 / D, None, ALU.mult, ALU.mult, (coln,), (coln,))
            stt("ve", col[:, C_T2:C_T2 + 1], col[:, C_T2:C_T2 + 1], -2.0,
                col[:, C_T1:C_T1 + 1], ALU.mult, ALU.add, (coln,), (coln,))
            tt("ve", col[:, C_T1:C_T1 + 1], col[:, C_M:C_M + 1],
               col[:, C_M:C_M + 1], ALU.mult, (coln,), (coln,))
            stt("ve", col[:, C_T2:C_T2 + 1], col[:, C_T1:C_T1 + 1],
                t_cstb[:, 0:1], col[:, C_T2:C_T2 + 1], ALU.mult, ALU.add,
                (coln, "cstb"), (coln,))
            tt("ve", col[:, C_T1:C_T1 + 1], col[:, CS_S:CS_S + 1],
               col[:, CS_S:CS_S + 1], ALU.mult, (coln,), (coln,))
            tt("ve", col[:, C_T2:C_T2 + 1], col[:, C_T2:C_T2 + 1],
               col[:, C_T1:C_T1 + 1], ALU.mult, (coln,), (coln,))
            ts("ve", col[:, C_T1:C_T1 + 1], col[:, C_SGBX:C_SGBX + 1],
               1.0 / D, None, ALU.mult, ALU.mult, (coln,), (coln,))
            stt("ve", col[:, C_T3:C_T3 + 1], col[:, C_M:C_M + 1],
                t_cstb[:, 1:2], col[:, C_T1:C_T1 + 1], ALU.mult,
                ALU.subtract, (coln, "cstb"), (coln,))
            tt("ve", col[:, C_T3:C_T3 + 1], col[:, C_T3:C_T3 + 1],
               col[:, CS_S:CS_S + 1], ALU.mult, (coln,), (coln,))
            tt("ve", col[:, C_T1:C_T1 + 1], col[:, C_SP:C_SP + 1],
               col[:, C_SP:C_SP + 1], ALU.mult, (coln,), (coln,))
            tt("ve", col[:, C_T2:C_T2 + 1], col[:, C_T2:C_T2 + 1],
               col[:, C_T1:C_T1 + 1], ALU.mult, (coln,), (coln,))
            tt("ve", col[:, C_T3:C_T3 + 1], col[:, C_T3:C_T3 + 1],
               col[:, C_SP:C_SP + 1], ALU.mult, (coln,), (coln,))
            stt("ve", col[:, C_T2:C_T2 + 1], col[:, C_T3:C_T3 + 1], -2.0,
                col[:, C_T2:C_T2 + 1], ALU.mult, ALU.add, (coln,), (coln,))
            ts("ve", col[:, C_T2:C_T2 + 1], col[:, C_T2:C_T2 + 1], 1.0,
               t_cstb[:, 2:3], ALU.mult, ALU.add, (coln, "cstb"), (coln,))
            act(col[:, C_T3:C_T3 + 1], col[:, C_T2:C_T2 + 1], AF.Ln,
                (coln, "eps6"), (coln,), bias=t_eps6[:, :])
            act(col[:, C_R:C_R + 1], col[:, C_T3:C_T3 + 1], AF.Exp,
                (coln,), (coln,), scale=-0.5)
            cp("ve", col[:, CS_RC:CS_RC + 1], col[:, C_R:C_R + 1], (coln,),
               (coln,))
            tt("ve", col[:, C_T1:C_T1 + 1], col[:, CS_S:CS_S + 1],
               col[:, C_SP:C_SP + 1], ALU.mult, (coln,), (coln,))
            tt("ve", col[:, CS_ROWA:CS_ROWA + 1], col[:, C_T1:C_T1 + 1],
               col[:, C_R:C_R + 1], ALU.mult, (coln,), (coln,))
            tt("ve", col[:, CS_ROWB:CS_ROWB + 1], col[:, C_M:C_M + 1],
               col[:, CS_ROWA:CS_ROWA + 1], ALU.mult, (coln,), (coln,))
            ts("ve", col[:, CS_ROWB:CS_ROWB + 1], col[:, CS_ROWB:CS_ROWB + 1],
               -1.0, None, ALU.mult, ALU.mult, (coln,), (coln,))
            tt("ve", col[:, CS_ROWD:CS_ROWD + 1], col[:, C_M:C_M + 1],
               col[:, CS_S:CS_S + 1], ALU.mult, (coln,), (coln,))
            ts("ve", col[:, CS_ROWD:CS_ROWD + 1], col[:, CS_ROWD:CS_ROWD + 1],
               -1.0, None, ALU.mult, ALU.mult, (coln,), (coln,))
            trans(pb4[0:16, 128:256], col[:, SEL0:SEL0 + 16], t_idf[:, :],
                  (coln, "idf"), ("pb4",))
            cp("ve", rblk[0:16, :], pb4[0:16, 128:256], ("pb4",), (rbn,))
            cp("ve", t_hrb[0:16, 128 * ci:128 * (ci + 1)], rblk[0:16, :],
               (rbn,), ("hrb",))
            selbc(pb0[:, 0:C], P_ROWA, rblk, rbn, True, True, "pb0")
            for dc in range(DCH):
                mm(pb1[:, 0:C], t_selP[:, 128 * dc:128 * (dc + 1)],
                   rblk[0:16, :], True, True, ("selP", rbn), ("pb1",))
                cs2 = slice(TILE * dc + ci * C, TILE * dc + (ci + 1) * C)
                sqq, sqqn = r_sq[dc % 2], f"rsq{dc % 2}"
                tt("ve", sqq[:, 0:C], t_x1[:, cs2], pb0[:, 0:C], ALU.mult,
                   ("x1", "pb0"), (sqqn,))
                ts("ve", sqq[:, 0:C], sqq[:, 0:C], t_ucol[:, dc:dc + 1],
                   None, ALU.mult, ALU.mult, (sqqn, "ucol"), (sqqn,))
                tt("ve", t_h[:, cs2], sqq[:, 0:C], pb1[:, 0:C], ALU.add,
                   (sqqn, "pb1"), ("h",))
        if "h" in tap_outs:
            dma("gp", x_3d(tap_outs["h"], t0), sb_3d(t_h), ("h",),
                ("htap",), force_sig=True)

        if upto == "S1h":
            return
        def proj(dst, dstname, whbm, whname, dgt, dgname, convt, convname,
                 nout, carry_base, rebuild_v=False, ti=ti):
            for oc in range(nout):
                sl, sn = r_wsl[oc % 4], f"rwsl{oc % 4}"
                w_slab(sl, sn, whbm, whname, oc, DCH, nout * 128)
                pb, pn = (pb0, "pb0") if oc % 2 == 0 else (pb1, "pb1")
                for ic in range(DCH):
                    mm(pb[:, 0:TILE], sl[:, 128 * ic:128 * (ic + 1)],
                       t_h[:, TILE * ic:TILE * (ic + 1)], ic == 0,
                       ic == DCH - 1, (sn, "h"), (pn,))
                pre, pren = r_pre[oc % 2], f"rpre{oc % 2}"
                cb = 64 * (carry_base + oc)
                cp("ve", pre[:, 0:64], t_carry[:, cb:cb + 64], ("carry",),
                   (pren,))
                cp("ve", pre[:, 64:64 + TILE], pb[:, 0:TILE], (pn,), (pren,))
                cp("ve", t_carry[:, cb:cb + 64], pre[:, TILE:TILE + 64],
                   (pren,), ("carry",))
                for tp in range(4):
                    if rebuild_v:
                        dgt_, dgn = r_dgv[tp % 2], f"rdgv{tp % 2}"
                        ts("ve", dgt_[:, :], t_idb[:, :],
                           convt[:, 4 * oc + tp:4 * oc + tp + 1], None,
                           ALU.mult, ALU.mult, ("idb", convname), (dgn,))
                        dgap = dgt_[:, :]
                    else:
                        j = 4 * oc + tp
                        dgap, dgn = dgt[:, 128 * j:128 * (j + 1)], dgname
                    mm(pb2[:, 0:TILE], dgap,
                       pre[:, 61 + tp:61 + tp + TILE], tp == 0, tp == 3,
                       (dgn, pren), ("pb2",))
                act(dst[:, TILE * oc:TILE * (oc + 1)], pb2[:, 0:TILE],
                    AF.Silu, ("pb2",), (dstname,))

        if ti == 1 and "dCar" in tap_outs:
            cp("ve", t_sbc[:, 0:96], t_carry[:, :], ("carry",), ("sbc",))
            dma("gp", tap_outs["dCar"], t_sbc[:, 0:96], ("sbc",),
                ("dCar_tap",))
        proj(t_q, "q", wq_b, "wh0", None, "dgq", t_convq, "convq", DCH,
             0, rebuild_v=True)
        proj(t_k, "k", wk_b, "wh1", None, "dgk", t_convk, "convk", DCH,
             DCH, rebuild_v=True)
        proj(t_v, "v", wv_b, "wh2", None, "dgv", t_convv, "convv", VCH,
             2 * DCH, rebuild_v=True)
        if "q" in tap_outs:
            dma("gp", tap_outs["q"].rearrange("(c p) t -> p c t", p=128)
                [:, :, t0:t0 + TILE],
                t_q[:, :].rearrange("p (c t) -> p c t", c=DCH), ("q",),
                ("qtap",), force_sig=True)
        if "k" in tap_outs:
            dma("gp", tap_outs["k"].rearrange("(c p) t -> p c t", p=128)
                [:, :, t0:t0 + TILE],
                t_k[:, :].rearrange("p (c t) -> p c t", c=DCH), ("k",),
                ("ktap",), force_sig=True)
        if "v" in tap_outs:
            dma("gp", tap_outs["v"].rearrange("(c p) t -> p c t", p=128)
                [:, :, t0:t0 + TILE],
                t_v[:, :].rearrange("p (c t) -> p c t", c=VCH), ("v",),
                ("vtap",), force_sig=True)

        if upto == "S1q":
            return
        for c in range(DCH):
            mm(pb3[0:68, 0:TILE], t_wabp[:, 68 * c:68 * (c + 1)],
               t_h[:, TILE * c:TILE * (c + 1)], False, False,
               ("wabp", "h"), ("pb3",))
        for c in range(DCH):
            cs = slice(TILE * c, TILE * (c + 1))
            sq, sqn = r_sq[c % 2], f"rsq{c % 2}"
            tt("ve", sq[:, :], t_q[:, cs], t_q[:, cs], ALU.mult, ("q",),
               (sqn,))
            mm(pb3[0:9, 0:TILE], t_hotq[:, 9 * c:9 * (c + 1)], sq[:, :],
               False, False, ("hotq", sqn), ("pb3",))
        for c in range(DCH):
            cs = slice(TILE * c, TILE * (c + 1))
            sq, sqn = r_sq[c % 2], f"rsq{c % 2}"
            tt("ve", sq[:, :], t_k[:, cs], t_k[:, cs], ALU.mult, ("k",),
               (sqn,))
            mm(pb3[0:13, 0:TILE], t_hotk[:, 13 * c:13 * (c + 1)], sq[:, :],
               False, False, ("hotk", sqn), ("pb3",))
        cp("ve", t_rstage[0:32, :], pb3[0:32, 0:TILE], ("pb3",), ("rstage",))
        cp("ve", t_rstage[32:64, :], pb3[32:64, 0:TILE], ("pb3",),
           ("rstage",))
        cp("ve", t_rstage[64:96, :], pb3[64:96, 0:TILE], ("pb3",),
           ("rstage",))
        if ti == 0 and "dRst0" in tap_outs:
            dma("gp", tap_outs["dRst0"], t_rstage[:, :], ("rstage",),
                ("dRst0_tap",))
        act(t_rstg2[32:36, :], t_rstage[32:36, :], AF.Exp,
            ("rstage", "dtbp"), ("rstg2",), bias=t_dtbp[32:36, :])
        act(t_rstage[32:36, :], t_rstg2[32:36, :], AF.Ln,
            ("rstg2", "one1"), ("rstage",), bias=t_one1[32:36, :])
        ts("ve", t_rstg2[32:36, :], t_rstage[32:36, :],
           t_nega[32:36, :], None, ALU.mult, ALU.mult, ("rstage", "nega"),
           ("rstg2",))
        act(t_rstg2[64:68, :], t_rstage[64:68, :], AF.Exp, ("rstage",),
            ("rstg2",), scale=-1.0)
        ts("ve", t_rstg2[64:68, :], t_rstg2[64:68, :], 1.0, None, ALU.add,
           ALU.add, ("rstg2",), ("rstg2",))
        recip(t_rstage[64:68, :], t_rstg2[64:68, :], ("rstg2",), ("rstage",))
        for ci in range(NCH):
            ccs = slice(ci * C, (ci + 1) * C)
            scan(t_rstage[32:36, ccs], t_ones36[32:36, ccs],
                 t_rstg2[32:36, ccs], ("ones36", "rstg2"), ("rstage",))

        if ti == 0 and "dRst1" in tap_outs:
            dma("gp", tap_outs["dRst1"], t_rstage[:, :], ("rstage",),
                ("dRst1_tap",))
        if ti == 0 and "dRs2" in tap_outs:
            dma("gp", tap_outs["dRs2"], t_rstg2[:, :], ("rstg2",),
                ("dRs2_tap",))
        if upto == "S1g":
            return
        for ci in range(NCH):
            delta_chunk(ti, ci)
        if upto == "S1d":
            return

        if "o" in tap_outs:
            dma("gp", tap_outs["o"].rearrange("(c p) t -> p c t", p=128)
                [:, :, t0:t0 + TILE],
                t_o[:, :].rearrange("p (c t) -> p c t", c=VCH), ("o",),
                ("otap",), force_sig=True)

        o_gate_enc_z(ti)
        if upto == "S1o":
            return
        w2_stage(ti)

    def delta_chunk(ti, ci):
        cur = ci % 2
        col, coln = r_cols[cur], f"rcols{cur}"
        rblk, rbn = r_rblk[cur], f"rrblk{cur}"
        ccs = slice(ci * C, (ci + 1) * C)
        trans(pb4[:, 0:128], t_rstage[:, ccs], t_idf[:, :],
              ("rstage", "idf"), ("pb4",))
        cp("ve", col[:, 0:21], pb4[:, 0:21], ("pb4",), (coln,))
        cp("ve", col[:, C_C:C_C + 4], pb4[:, 32:36], ("pb4",), (coln,))
        cp("ve", col[:, C_BETA:C_BETA + 4], pb4[:, 64:68], ("pb4",), (coln,))
        act(col[:, C_T1:C_T1 + 4], col[:, C_SSK:C_SSK + 4], AF.Ln,
            (coln, "eps6"), (coln,), bias=t_eps6[:, :])
        ts("ve", col[:, C_LNRK:C_LNRK + 4], col[:, C_T1:C_T1 + 4], -0.5,
           None, ALU.mult, ALU.mult, (coln,), (coln,))
        act(col[:, C_T1:C_T1 + 4], col[:, C_SSQ:C_SSQ + 4], AF.Ln,
            (coln, "eps6"), (coln,), bias=t_eps6[:, :])
        ts("ve", col[:, C_LNRQ:C_LNRQ + 4], col[:, C_T1:C_T1 + 4], -0.5,
           -0.5 * float(np.log(Kd)), ALU.mult, ALU.add, (coln,), (coln,))
        act(col[:, C_LNB:C_LNB + 4], col[:, C_BETA:C_BETA + 4], AF.Ln,
            (coln,), (coln,))
        tt("ve", col[:, C_LA:C_LA + 4], col[:, C_LNRK:C_LNRK + 4],
           col[:, C_C:C_C + 4], ALU.subtract, (coln,), (coln,))
        tt("ve", col[:, C_T1:C_T1 + 4], col[:, C_C:C_C + 4],
           col[:, C_LNRK:C_LNRK + 4], ALU.add, (coln,), (coln,))
        tt("ve", col[:, CS_B:CS_B + 4], col[:, C_T1:C_T1 + 4],
           col[:, C_LNB:C_LNB + 4], ALU.add, (coln,), (coln,))
        tt("ve", col[:, CS_E:CS_E + 4], col[:, C_C:C_C + 4],
           col[:, C_LNRQ:C_LNRQ + 4], ALU.add, (coln,), (coln,))
        act(col[:, C_EXPB:C_EXPB + 4], col[:, CS_B:CS_B + 4], AF.Exp,
            (coln,), (coln,))
        mm(pb4[0:1, 200:204], t_e127[:, :], col[:, C_C:C_C + 4], True, True,
           ("e127", coln), ("pb4",))
        cp("ve", t_cc4[:, :], pb4[0:1, 200:204], ("pb4",), ("cc4",))
        for hh in range(H):
            mm(pb4[:, 204 + hh:205 + hh], t_onesr[:, :],
               t_cc4[0:1, hh:hh + 1], True, True, ("onesr", "cc4"), ("pb4",))
        cp("ve", t_cCb[:, :], pb4[:, 204:208], ("pb4",), ("cCb",))
        act(t_gCc[:, :], t_cCb[:, :], AF.Exp, ("cCb",), ("gCc",))
        for hh in range(H):
            act(col[:, C_EXPD + hh:C_EXPD + hh + 1],
                col[:, C_LA + hh:C_LA + hh + 1], AF.Exp, (coln, "cCb"),
                (coln,), bias=t_cCb[:, hh:hh + 1])
        trans(pb4[0:16, 128:256], col[:, SEL0:SEL0 + 16], t_idf[:, :],
              (coln, "idf"), ("pb4",))
        cp("ve", rblk[0:16, :], pb4[0:16, 128:256], ("pb4",), (rbn,))

        for hh in range(H):
            for j2 in range(2):
                ks = slice(TILE * (2 * hh + j2) + ci * C,
                           TILE * (2 * hh + j2) + (ci + 1) * C)
                mm(pb0[:, 128 * hh:128 * (hh + 1)], t_k[:, ks], t_k[:, ks],
                   j2 == 0, j2 == 1, ("k",), ("pb0",))
                mm(pb1[:, 128 * hh:128 * (hh + 1)], t_k[:, ks], t_q[:, ks],
                   j2 == 0, j2 == 1, ("k", "q"), ("pb1",))
        mm(pb3[:, 0:512], t_negbig[:, :], t_maskY[:, :], True, True,
           ("negbig", "maskY"), ("pb3",))
        for hh in range(H):
            mm(pb3[:, 128 * hh:128 * (hh + 1)],
               t_selo[:, 128 * (P_B + hh):128 * (P_B + hh + 1)],
               rblk[0:16, :], False, True, ("selo", rbn), ("pb3",))
        for hh in range(H):
            act(t_Eh[:, 128 * hh:128 * (hh + 1)],
                pb3[:, 128 * hh:128 * (hh + 1)], AF.Exp, ("pb3", coln),
                ("Eh",), bias=col[:, C_LA + hh:C_LA + hh + 1])
        if ti == 0 and ci == 0 and "dEhY" in tap_outs:
            cp("ve", t_sbc[:, 0:512], t_Eh[:, :], ("Eh",), ("sbc",))
            dma("gp", tap_outs["dEhY"], t_sbc[:, 0:512], ("sbc",),
                ("dEhY_tap",))
            cp("ve", t_sbc[:, 0:512], pb3[:, 0:512], ("pb3",), ("sbc",))
            dma("gp", tap_outs["dEhM"], t_sbc[:, 0:512], ("sbc",),
                ("dEhM_tap",))
        stt("ve", r_Y[0][:, :], t_Eh[:, :], -1.0, pb0[:, 0:512], ALU.mult,
            ALU.mult, ("Eh", "pb0"), ("rY0",))
        mm(pb3[:, 0:512], t_negbig[:, :], t_maskM[:, :], True, True,
           ("negbig", "maskM"), ("pb3",))
        for hh in range(H):
            mm(pb3[:, 128 * hh:128 * (hh + 1)],
               t_selo[:, 128 * (P_E + hh):128 * (P_E + hh + 1)],
               rblk[0:16, :], False, True, ("selo", rbn), ("pb3",))
        for hh in range(H):
            act(t_Eh[:, 128 * hh:128 * (hh + 1)],
                pb3[:, 128 * hh:128 * (hh + 1)], AF.Exp, ("pb3", coln),
                ("Eh",), bias=col[:, C_LA + hh:C_LA + hh + 1])
        tt("ve", t_MT[:, :], t_Eh[:, :], pb1[:, 0:512], ALU.mult,
           ("Eh", "pb1"), ("MT",))
        for hh in range(H):
            trans(pbt[:, 128 * hh:128 * (hh + 1)],
                  r_Y[0][:, 128 * hh:128 * (hh + 1)], t_idb[:, :],
                  ("rY0", "idb"), ("pbt",))
        cp("ve", r_X[0][:, :], pbt[:, 0:512], ("pbt",), ("rX0",))
        pzero(pb4, "pb4")
        for hh in range(H):
            mm(pb4[:, 128 * hh:128 * (hh + 1)], t_idb[:, :], t_idb[:, :],
               False, False, ("idb",), ("pb4",))
            mm(pb4[:, 128 * hh:128 * (hh + 1)],
               r_X[0][:, 128 * hh:128 * (hh + 1)], t_idb[:, :], False, True,
               ("rX0", "idb"), ("pb4",))
        for j in range(6):
            xj, xjn = r_X[j % 2], f"rX{j % 2}"
            yj, yjn = r_Y[j % 2], f"rY{j % 2}"
            xn, xnn = r_X[(j + 1) % 2], f"rX{(j + 1) % 2}"
            yn, ynn = r_Y[(j + 1) % 2], f"rY{(j + 1) % 2}"
            for hh in range(H):
                mm(pb2[:, 128 * hh:128 * (hh + 1)],
                   yj[:, 128 * hh:128 * (hh + 1)],
                   xj[:, 128 * hh:128 * (hh + 1)], True, True, (yjn, xjn),
                   ("pb2",))
            cp("ve", xn[:, :], pb2[:, 0:512], ("pb2",), (xnn,))
            if j < 5:
                for hh in range(H):
                    mm(pb2[:, 128 * hh:128 * (hh + 1)],
                       xj[:, 128 * hh:128 * (hh + 1)],
                       yj[:, 128 * hh:128 * (hh + 1)], True, True,
                       (xjn, yjn), ("pb2",))
                cp("ve", yn[:, :], pb2[:, 0:512], ("pb2",), (ynn,))
            zs, zsn = r_Zs[j % 2], f"rZs{j % 2}"
            cp("ve", zs[:, :], pb4[:, 0:512], ("pb4",), (zsn,))
            for hh in range(H):
                mm(pb4[:, 128 * hh:128 * (hh + 1)],
                   xn[:, 128 * hh:128 * (hh + 1)],
                   zs[:, 128 * hh:128 * (hh + 1)], False, True, (xnn, zsn),
                   ("pb4",))
        cp("ve", t_TT[:, :], pb4[:, 0:512], ("pb4",), ("TT",))

        if ti == 0 and ci == 0:
            def dtap(nm, srcap, rdname, ncols):
                if nm in tap_outs:
                    for c0 in range(0, ncols, 512):
                        w = min(512, ncols - c0)
                        cp("ve", t_sbc[:, 0:w], srcap[:, c0:c0 + w],
                           (rdname,), ("sbc",))
                        dma("gp", tap_outs[nm][:, c0:c0 + w], t_sbc[:, 0:w],
                            ("sbc",), (nm + "_tap",))
            dtap("dY0", r_Y[0], "rY0", 512)
            dtap("dX0", r_X[0], "rX0", 512)
            dtap("dTT", t_TT, "TT", 512)
            dtap("dMT", t_MT, "MT", 512)
            if "dRblk" in tap_outs:
                dma("gp", tap_outs["dRblk"], rblk[0:16, :], (rbn,),
                    ("dRblk_tap",))
            if "dCols" in tap_outs:
                dma("gp", tap_outs["dCols"], col[:, :], (coln,),
                    ("dCols_tap",))
        for hh in range(H):
            hc = hh % 2
            rp, rpn = r_Rp[hc], f"rRp{hc}"
            dl, dln = r_Dl[hc], f"rDl{hc}"
            for jv in range(4):
                vs = slice(TILE * (4 * hh + jv) + ci * C,
                           TILE * (4 * hh + jv) + (ci + 1) * C)
                trans(pbt[:, 128 * jv:128 * (jv + 1)], t_v[:, vs],
                      t_idb[:, :], ("v", "idb"), ("pbt",))
            ts("ve", rp[:, 0:512], pbt[:, 0:512],
               col[:, C_BETA + hh:C_BETA + hh + 1], None, ALU.mult, ALU.mult,
               ("pbt", coln), (rpn,))
            for j2 in range(2):
                ks = slice(TILE * (2 * hh + j2) + ci * C,
                           TILE * (2 * hh + j2) + (ci + 1) * C)
                trans(pbt[:, 512 + 128 * j2:512 + 128 * (j2 + 1)],
                      t_k[:, ks], t_idb[:, :], ("k", "idb"), ("pbt",))
            ts("ve", rp[:, 512:768], pbt[:, 512:768],
               col[:, C_EXPB + hh:C_EXPB + hh + 1], None, ALU.mult, ALU.mult,
               ("pbt", coln), (rpn,))
            ts("ve", t_Kh[:, Kd * hh:Kd * (hh + 1)], pbt[:, 512:768],
               col[:, C_EXPD + hh:C_EXPD + hh + 1], None, ALU.mult, ALU.mult,
               ("pbt", coln), ("Kh",))
            mm(pdel[:, 0:512], t_TT[:, 128 * hh:128 * (hh + 1)],
               rp[:, 0:512], True, True, ("TT", rpn), ("pdel",))
            mm(pdel[:, 512:768], t_TT[:, 128 * hh:128 * (hh + 1)],
               rp[:, 512:768], True, True, ("TT", rpn), ("pdel",))
            cp("ve", dl[:, :], pdel[:, 0:768], ("pdel",), (dln,))
            mm(pb3[:, 0:128], t_selo[:, 128 * (P_E + hh):128 * (P_E + hh + 1)],
               rblk[0:16, :], True, True, ("selo", rbn), ("pb3",))
            act(t_eqb[:, :], pb3[:, 0:128], AF.Exp, ("pb3",), ("eqb",))
            for j2 in range(2):
                ks = slice(TILE * (2 * hh + j2) + ci * C,
                           TILE * (2 * hh + j2) + (ci + 1) * C)
                tt("ve", r_Qt[j2][:, :], t_q[:, ks], t_eqb[:, :], ALU.mult,
                   ("q", "eqb"), (f"rQt{j2}",))
            for jv in range(4):
                for j2 in range(2):
                    mm(pb2[:, 128 * jv:128 * (jv + 1)],
                       t_S[:, 1024 * hh + 512 * j2 + 128 * jv:
                           1024 * hh + 512 * j2 + 128 * (jv + 1)],
                       r_Qt[j2][:, :], j2 == 0, False,
                       (f"S{hh}", f"rQt{j2}"), ("pb2",))
                mm(pb2[:, 128 * jv:128 * (jv + 1)],
                   dl[:, 128 * jv:128 * (jv + 1)],
                   t_MT[:, 128 * hh:128 * (hh + 1)], False, True,
                   (dln, "MT"), ("pb2",))
            p.emit("ve", lambda e, hh=hh, ci=ci: e.tensor_copy(
                out=t_o[:, :].rearrange("p (c t) -> p c t", c=VCH)
                [:, 4 * hh:4 * hh + 4, ci * C:(ci + 1) * C],
                in_=pb2[:, 0:512].rearrange("p (c t) -> p c t", c=4)),
                reads=("pb2",), writes=("o",))
            for m_ in range(2):
                mm(pdel[:, 256 * m_:256 * (m_ + 1)],
                   dl[:, 512 + 128 * m_:512 + 128 * (m_ + 1)],
                   t_Kh[:, Kd * hh:Kd * (hh + 1)], True, True,
                   (dln, "Kh"), ("pdel",))
            ts("ve", t_g0[:, 0:128], t_idb[:, :], t_gCc[:, hh:hh + 1], None,
               ALU.mult, ALU.mult, ("idb", "gCc"), ("g0",))
            ts("ve", t_g0[:, 384:512], t_idb[:, :], t_gCc[:, hh:hh + 1],
               None, ALU.mult, ALU.mult, ("idb", "gCc"), ("g0",))
            wp, wpn = r_Wp[hc], f"rWp{hc}"
            tt("ve", wp[:, 0:256], t_g0[:, 0:256], pdel[:, 0:256],
               ALU.subtract, ("g0", "pdel"), (wpn,))
            tt("ve", wp[:, 256:512], t_g0[:, 256:512], pdel[:, 256:512],
               ALU.subtract, ("g0", "pdel"), (wpn,))
            for a_ in range(2):
                for b_ in range(2):
                    mm(pdel[:, 512 * a_:512 * (a_ + 1)],
                       wp[:, 256 * b_ + 128 * a_:256 * b_ + 128 * (a_ + 1)],
                       t_S[:, 1024 * hh + 512 * b_:1024 * hh + 512 * (b_ + 1)],
                       b_ == 0, False, (wpn, f"S{hh}"), ("pdel",))
                mm(pdel[:, 512 * a_:512 * (a_ + 1)],
                   t_Kh[:, Kd * hh + 128 * a_:Kd * hh + 128 * (a_ + 1)],
                   dl[:, 0:512], False, True, ("Kh", dln), ("pdel",))
            for a_ in range(2):
                cp("ve", t_S[:, 1024 * hh + 512 * a_:1024 * hh + 512 * (a_ + 1)],
                   pdel[:, 512 * a_:512 * (a_ + 1)], ("pdel",), (f"S{hh}",))
            if ti == 0 and ci == 0 and hh == 0:
                def dtap2(nm, srcap, rdname, ncols):
                    if nm in tap_outs:
                        for c0 in range(0, ncols, 512):
                            w = min(512, ncols - c0)
                            cp("ve", t_sbc[:, 0:w], srcap[:, c0:c0 + w],
                               (rdname,), ("sbc",))
                            dma("gp", tap_outs[nm][:, c0:c0 + w],
                                t_sbc[:, 0:w], ("sbc",), (nm + "_tap",))
                dtap2("dRp", rp, rpn, 768)
                dtap2("dDl", dl, dln, 768)
                dtap2("dKh", t_Kh, "Kh", 1024)
                dtap2("dWp", wp, wpn, 512)
                dtap2("dS", t_S, "S0", 4096)

    def o_gate_enc_z(ti):
        t0 = ti * TILE
        pzero(pb3, "pb3")
        for vc in range(VCH):
            cs = slice(TILE * vc, TILE * (vc + 1))
            sq, sqn = r_sq[vc % 2], f"rsq{vc % 2}"
            tt("ve", sq[:, :], t_o[:, cs], t_o[:, cs], ALU.mult, ("o",),
               (sqn,))
            mm(pb3[0:17, 0:TILE], t_hotv[:, 17 * vc:17 * (vc + 1)], sq[:, :],
               False, False, ("hotv", sqn), ("pb3",))
        cp("ve", t_rstage[0:17, :], pb3[0:17, 0:TILE], ("pb3",), ("rstage",))
        for ci in range(NCH):
            cur = ci % 2
            col, coln = r_cols[cur], f"rcols{cur}"
            rblk, rbn = r_rblk[cur], f"rrblk{cur}"
            ccs = slice(ci * C, (ci + 1) * C)
            trans(pb4[:, 0:128], t_rstage[:, ccs], t_idf[:, :],
                  ("rstage", "idf"), ("pb4",))
            cp("ve", col[:, 0:21], pb4[:, 0:21], ("pb4",), (coln,))
            act(col[:, C_T1:C_T1 + 4], col[:, C_SSO:C_SSO + 4], AF.Ln,
                (coln, "eps6"), (coln,), bias=t_eps6[:, :], scale=1.0 / Vd)
            act(col[:, CS_B:CS_B + 4], col[:, C_T1:C_T1 + 4], AF.Exp,
                (coln,), (coln,), scale=-0.5)
            trans(pb4[0:16, 128:256], col[:, SEL0:SEL0 + 16], t_idf[:, :],
                  (coln, "idf"), ("pb4",))
            cp("ve", rblk[0:16, :], pb4[0:16, 128:256], ("pb4",),
               (f"orb{ci}",))
            cp("ve", t_sbc[0:16, 128 * ci:128 * (ci + 1)], rblk[0:16, :],
               (f"orb{ci}",), ("osbc",))
        for hh in range(H):
            for ci in range(NCH):
                mm(pb3[:, 128 * ci:128 * (ci + 1)],
                   t_selo[:, 128 * (P_B + hh):128 * (P_B + hh + 1)],
                   t_sbc[0:16, 128 * ci:128 * (ci + 1)], True, True,
                   ("selo", "osbc"), ("pb3",))
            for jv in range(4):
                vc = 4 * hh + jv
                cs = slice(TILE * vc, TILE * (vc + 1))
                sl, sn = r_wsl[vc % 4], f"rwsl{vc % 4}"
                w_slab(sl, sn, wg_b, "wh3", vc, DCH, VAL)
                pb, pn = (pb0, "pb0") if vc % 2 == 0 else (pb1, "pb1")
                for ic in range(DCH):
                    mm(pb[:, 0:TILE], sl[:, 128 * ic:128 * (ic + 1)],
                       t_h[:, TILE * ic:TILE * (ic + 1)], ic == 0,
                       ic == DCH - 1, (sn, "h"), (pn,))
                ga, gan = r_ga[vc % 2], f"rga{vc % 2}"
                act(ga[:, :], pb[:, 0:TILE], AF.Silu, (pn,), (gan,))
                sq, sqn = r_sq[vc % 2], f"rsq{vc % 2}"
                tt("ve", sq[:, :], t_o[:, cs], pb3[:, 0:TILE], ALU.mult,
                   ("o", "pb3"), (sqn,))
                stt("ve", t_o[:, cs], sq[:, :], t_onwc[:, jv:jv + 1],
                    ga[:, :], ALU.mult, ALU.mult, (sqn, "onwc", gan), ("o",))
        if "gt" in tap_outs:
            dma("gp", tap_outs["gt"].rearrange("(c p) t -> p c t", p=128)
                [:, :, t0:t0 + TILE],
                t_o[:, :].rearrange("p (c t) -> p c t", c=VCH), ("o",),
                ("gttap",), force_sig=True)
        for q_ in range(2):
            pbs = [(pb0, "pb0"), (pb1, "pb1"), (pb2, "pb2"), (pb4, "pb4")]
            for vc in range(VCH):
                sl, sn = r_wo2[vc % 2], f"rwo2{vc % 2}"
                src = wo_b.rearrange("(vc p) o -> p vc o", p=128)[
                    :, vc:vc + 1, 512 * q_:512 * (q_ + 1)]
                dma("sy", sl[:, 0:512].rearrange("p (a j) -> p a j", a=1),
                    src, (sn, "wh4"), (sn,))
                for jo in range(4):
                    oc = 4 * q_ + jo
                    pb, pn = pbs[jo]
                    mm(pb[:, 0:TILE], sl[:, 128 * jo:128 * (jo + 1)],
                       t_o[:, TILE * vc:TILE * (vc + 1)], vc == 0, False,
                       (sn, "o"), (pn,))
            for jo in range(4):
                oc = 4 * q_ + jo
                pb, pn = pbs[jo]
                for ci in range(NCH):
                    mm(pb[:, 128 * ci:128 * (ci + 1)],
                       t_selo[:, 128 * P_ROWD:128 * (P_ROWD + 1)],
                       t_hrb[0:16, 128 * ci:128 * (ci + 1)], False,
                       ci == NCH - 1, ("selo", "hrb"), (pn,))
            for jo in range(4):
                oc = 4 * q_ + jo
                pb, pn = pbs[jo]
                cs = slice(TILE * oc, TILE * (oc + 1))
                sq, sqn = r_sq[jo % 2], f"rsq{jo % 2}"
                for ci in range(NCH):
                    mm(pb3[:, 128 * ci:128 * (ci + 1)],
                       t_selo[:, 128 * P_S:128 * (P_S + 1)],
                       t_hrb[0:16, 128 * ci:128 * (ci + 1)], True, True,
                       ("selo", "hrb"), ("pb3",))
                tt("ve", r_32[jo % 2][:, :], t_x1[:, cs], pb3[:, 0:TILE],
                   ALU.mult, ("x1", "pb3"), (f"r32{jo % 2}",))
                tt("ve", t_z[:, cs], r_32[jo % 2][:, :], pb[:, 0:TILE],
                   ALU.add, (f"r32{jo % 2}", pn), ("z",))
                cp("ve", t_zb[:, cs], t_z[:, cs], ("z",), ("zb",))
        if "z" in tap_outs:
            dma("gp", x_3d(tap_outs["z"], t0), sb_3d(t_z), ("z",),
                ("ztap",), force_sig=True)

    def w2_stage(ti):
        t0 = ti * TILE
        pzero(pb3, "pb3")
        stats_group_bf(t_zb, "zb", "z")
        cp("ve", t_rstage[0:19, :], pb3[0:19, 0:TILE], ("pb3",), ("rstage",))
        for ci in range(NCH):
            cur = ci % 2
            col, coln = r_cols[cur], f"rcols{cur}"
            rblk, rbn = r_rblk[cur], f"rrblk{cur}"
            ccs = slice(ci * C, (ci + 1) * C)
            trans(pb4[:, 0:128], t_rstage[:, ccs], t_idf[:, :],
                  ("rstage", "idf"), ("pb4",))
            cp("ve", col[:, 0:21], pb4[:, 0:21], ("pb4",), (coln,))
            ts("ve", col[:, C_M:C_M + 1], col[:, C_SZ:C_SZ + 1], 1.0 / D,
               None, ALU.mult, ALU.mult, (coln,), (coln,))
            ts("ve", col[:, C_T1:C_T1 + 1], col[:, C_SZ2:C_SZ2 + 1], 1.0 / D,
               None, ALU.mult, ALU.mult, (coln,), (coln,))
            tt("ve", col[:, C_T2:C_T2 + 1], col[:, C_M:C_M + 1],
               col[:, C_M:C_M + 1], ALU.mult, (coln,), (coln,))
            tt("ve", col[:, C_V:C_V + 1], col[:, C_T1:C_T1 + 1],
               col[:, C_T2:C_T2 + 1], ALU.subtract, (coln,), (coln,))
            act(col[:, C_T3:C_T3 + 1], col[:, C_V:C_V + 1], AF.Ln,
                (coln, "eps5"), (coln,), bias=t_eps5[:, :])
            act(col[:, CS_X1:CS_X1 + 1], col[:, C_T3:C_T3 + 1], AF.Exp,
                (coln,), (coln,), scale=-0.5)
            tt("ve", col[:, CS_X2:CS_X2 + 1], col[:, C_M:C_M + 1],
               col[:, CS_X1:CS_X1 + 1], ALU.mult, (coln,), (coln,))
            ts("ve", col[:, CS_X2:CS_X2 + 1], col[:, CS_X2:CS_X2 + 1], -1.0,
               None, ALU.mult, ALU.mult, (coln,), (coln,))
            trans(pb4[0:16, 128:256], col[:, SEL0:SEL0 + 16], t_idf[:, :],
                  (coln, "idf"), ("pb4",))
            cp("ve", t_zrb[0:16, 128 * ci:128 * (ci + 1)],
               pb4[0:16, 128:256], ("pb4",), ("zrb",))
        for ci in range(NCH):
            mm(pb4[:, 128 * ci:128 * (ci + 1)],
               t_selo[:, 128 * P_X1:128 * (P_X1 + 1)],
               t_zrb[0:16, 128 * ci:128 * (ci + 1)], True, True,
               ("selo", "zrb"), ("pb4",))
        cp("ve", t_sbc[:, 0:TILE], pb4[:, 0:TILE], ("pb4",), ("sbc",))
        if upto == "S1v":
            return
        for oc in range(DCH):
            sl, sn = r_wsl[oc % 4], f"rwsl{oc % 4}"
            w_slab(sl, sn, w2p_b, "wh5", oc, DCH, D)
            pb, pn = (pb0, "pb0") if oc % 2 == 0 else (pb1, "pb1")
            for ic in range(DCH):
                mm(pb[:, 0:TILE], sl[:, 128 * ic:128 * (ic + 1)],
                   t_zb[:, TILE * ic:TILE * (ic + 1)], ic == 0,
                   ic == DCH - 1, (sn, "zb"), (pn,))
            for ci in range(NCH):
                mm(pb2[:, 128 * ci:128 * (ci + 1)],
                   t_selw2[:, 128 * oc:128 * (oc + 1)],
                   t_zrb[0:16, 128 * ci:128 * (ci + 1)], True, True,
                   ("selw2", "zrb"), ("pb2",))
            cs = slice(TILE * oc, TILE * (oc + 1))
            ff, fn_ = r_ff[oc % 2], f"rff{oc % 2}"
            tt("ve", ff[:, :], pb[:, 0:TILE], t_sbc[:, 0:TILE], ALU.mult,
               (pn, "sbc"), (fn_,))
            gl, gln = r_ga[oc % 2], f"rga{oc % 2}"
            sq2, sq2n = r_sq[oc % 2], f"rsq{oc % 2}"
            tt("ve", sq2[:, :], ff[:, :], pb2[:, 0:TILE], ALU.add,
               (fn_, "pb2"), (sq2n,))
            act(gl[:, :], sq2[:, :], AF.Gelu, (sq2n, "gelb"), (gln,),
                bias=t_gelb[:, oc:oc + 1])
            tt("ve", t_x1[:, cs], t_x1[:, cs], t_z[:, cs], ALU.add,
               ("x1", "z"), ("x1",))
            tt("ve", t_z[:, cs], t_x1[:, cs], gl[:, :], ALU.add,
               ("x1", gln), ("z",))
            if upto != "S1w1":
                dma("gp", x2sp[128 * oc:128 * (oc + 1), t0:t0 + TILE],
                    t_z[:, cs], ("z",), (f"x2w{ti}_{oc}",), force_sig=True)
        if upto == "S1w1":
            return
        if "x2f" in tap_outs:
            dma("gp", x_3d(tap_outs["x2f"], t0), sb_3d(t_z), ("z",),
                ("x2tap",), force_sig=True)
        if upto == "S1w":
            return
        pzero(pb3, "pb3")
        for c in range(DCH):
            cs = slice(TILE * c, TILE * (c + 1))
            sq, sqn = r_32[c % 2], f"r32{c % 2}"
            tt("ve", sq[:, :], t_z[:, cs], t_z[:, cs], ALU.mult, ("z",),
               (sqn,))
            mm(pb3[0:20, 0:TILE], t_onespad[:, 8:28], t_z[:, cs],
               False, False, ("onespad", "z"), ("pb3",))
            mm(pb3[0:21, 0:TILE], t_onespad[:, 28:49], sq[:, :],
               False, c == DCH - 1, ("onespad", sqn), ("pb3",))
        cp("ve", t_rstage[0:21, :], pb3[0:21, 0:TILE], ("pb3",), ("rstage",))
        if upto == "S1x":
            return
        for ci in range(NCH):
            cur = ci % 2
            col, coln = r_cols[cur], f"rcols{cur}"
            ccs = slice(ci * C, (ci + 1) * C)
            trans(pb4[:, 0:128], t_rstage[:, ccs], t_idf[:, :],
                  ("rstage", "idf"), ("pb4",))
            cp("ve", col[:, 0:21], pb4[:, 0:21], ("pb4",), (coln,))
            ts("ve", col[:, C_M:C_M + 1], col[:, C_SY:C_SY + 1], 1.0 / D,
               None, ALU.mult, ALU.mult, (coln,), (coln,))
            ts("ve", col[:, C_T1:C_T1 + 1], col[:, C_SY2:C_SY2 + 1], 1.0 / D,
               None, ALU.mult, ALU.mult, (coln,), (coln,))
            tt("ve", col[:, C_T2:C_T2 + 1], col[:, C_M:C_M + 1],
               col[:, C_M:C_M + 1], ALU.mult, (coln,), (coln,))
            tt("ve", col[:, C_V:C_V + 1], col[:, C_T1:C_T1 + 1],
               col[:, C_T2:C_T2 + 1], ALU.subtract, (coln,), (coln,))
            act(col[:, C_T3:C_T3 + 1], col[:, C_V:C_V + 1], AF.Ln,
                (coln, "eps5"), (coln,), bias=t_eps5[:, :])
            act(col[:, CS_X1:CS_X1 + 1], col[:, C_T3:C_T3 + 1], AF.Exp,
                (coln,), (coln,), scale=-0.5)
            tt("ve", col[:, CS_X2:CS_X2 + 1], col[:, C_M:C_M + 1],
               col[:, CS_X1:CS_X1 + 1], ALU.mult, (coln,), (coln,))
            ts("ve", col[:, CS_X2:CS_X2 + 1], col[:, CS_X2:CS_X2 + 1], -1.0,
               None, ALU.mult, ALU.mult, (coln,), (coln,))
            trans(pb4[0:16, 128:256], col[:, SEL0:SEL0 + 16], t_idf[:, :],
                  (coln, "idf"), ("pb4",))
            cp("ve", t_zrb[0:16, 128 * ci:128 * (ci + 1)],
               pb4[0:16, 128:256], ("pb4",), ("zrb",))
            mm(pb4[:, 256:384], t_selo[:, 128 * P_X1:128 * (P_X1 + 1)],
               t_zrb[0:16, 128 * ci:128 * (ci + 1)], True, True,
               ("selo", "zrb"), ("pb4",))
            mm(pb4[:, 384:512], t_selo[:, 128 * P_X2:128 * (P_X2 + 1)],
               t_zrb[0:16, 128 * ci:128 * (ci + 1)], True, True,
               ("selo", "zrb"), ("pb4",))
            for dc in range(DCH):
                cs2 = slice(TILE * dc + ci * C, TILE * dc + (ci + 1) * C)
                jj = dc % 2
                tt("ve", r_32[jj][:, 0:C], t_z[:, cs2], pb4[:, 256:384],
                   ALU.mult, ("z", "pb4"), (f"r32{jj}",))
                tt("ve", r_32[jj][:, 0:C], r_32[jj][:, 0:C], pb4[:, 384:512],
                   ALU.add, (f"r32{jj}", "pb4"), (f"r32{jj}",))
                red(t_am2[:, 16 + ci * DCH + dc:17 + ci * DCH + dc],
                    r_32[jj][:, 0:C], ALU.max,
                    True, (f"r32{jj}",), (f"bm_{ti}_{ci}_{dc}",))
                dma("gp", lnx2sp[128 * dc:128 * (dc + 1),
                                 t0 + ci * C:t0 + (ci + 1) * C],
                    r_32[jj][:, 0:C], (f"r32{jj}",),
                    (f"lxw{ti}_{ci}_{dc}",), force_sig=True)
        red(t_am2[:, 8 + ti:9 + ti], t_am2[:, 16:16 + NCH * DCH], ALU.max,
            False,
            tuple(f"bm_{ti}_{ci}_{dc}" for ci in range(NCH)
                  for dc in range(DCH)), (f"bmT{ti}",))

    for ti in range(NT):
        sweep1_tile(ti)
        if upto.startswith("S1") and upto != "S1" and ti == 0:
            break
    if upto.startswith("S1"):
        for oc in range(DCH):
            dma("gp", out_d[128 * oc:128 * (oc + 1), 0:TILE],
                t_rstage[:, 0:TILE], ("rstage",), ("outdummy",))
        n = p.build(nc, sems, spacers, es)
        import sys
        print(f"[kernel] built {n} ops ({upto})", file=sys.stderr)
        es.close()
        return nc, tap_outs, p
    amax_collective(2)

    def sweep2_tile(ti):
        t0 = ti * TILE
        dma("sy", sb_3d(t_z), x_3d(lnx2sp, t0),
            ("z",) + tuple(f"lxw{t}_{ci}_{dc}" for t in range(NT)
                           for ci in range(NCH)
                           for dc in range(DCH)), ("z",))
        act(t_q8[:, :], t_z[:, :], AF.Copy, ("z", "sxh2"), ("q8t",),
            scale=t_sx2h[:, :])
        def resid_x2(oc):
            rx, rxn = r_x0[oc % 2], f"rx0{oc % 2}"
            dma("sy", rx[:, :], x2sp[128 * oc:128 * (oc + 1), t0:t0 + TILE],
                (rxn,) + tuple(f"x2w{t}_{k}" for t in range(NT)
                               for k in range(DCH)), (rxn,))
            return rx[:, :], rxn
        for oc in range(DCH):
            sl, sn = r_f8[oc % 3], f"rf8{oc % 3}"
            w_slab(sl, sn, wf8sp, "wf8", oc, DCH, D)
            pb, pn = (pb0, "pb0") if oc % 2 == 0 else (pb1, "pb1")
            for ic in range(DCH):
                mm(pb[:, 0:TILE], sl[:, 128 * ic:128 * (ic + 1)],
                   t_q8[:, TILE * ic:TILE * (ic + 1)], ic == 0,
                   ic == DCH - 1, (sn, "q8t"), (pn,))
            ff, fn_ = r_ff[oc % 2], f"rff{oc % 2}"
            ts("ve", ff[:, :], pb[:, 0:TILE], t_f2[:, :],
               t_fbc[:, oc:oc + 1], ALU.mult, ALU.add, (pn, "fd2", "fbc"),
               (fn_,))
            rx, rxn = resid_x2(oc)
            jj = oc % 2
            stt("ve", r_32[jj][:, :], ff[:, :], 0.5, rx, ALU.mult, ALU.add,
                (fn_, rxn), (f"r32{jj}",))
            dma("gp", out_d[128 * oc:128 * (oc + 1), t0:t0 + TILE],
                r_32[jj][:, :], (f"r32{jj}",), (f"outw{ti}_{oc}",),
                force_sig=True)

    for ti in range(NT):
        sweep2_tile(ti)

    n = p.build(nc, sems, spacers, es)
    import sys
    print(f"[kernel] built {n} ops", file=sys.stderr)
    es.close()
    return nc, tap_outs, p


def _prep_inputs(inputs, core):
    bf = ml_dtypes.bfloat16
    x = np.asarray(inputs["L0_attn_x"], np.float32)
    d = {}
    d["x0"] = np.ascontiguousarray(x[core].T)
    d["wq"] = np.asarray(inputs["L0_attn_Wq"], np.float32)
    d["wk"] = np.asarray(inputs["L0_attn_Wk"], np.float32)
    d["wv"] = np.asarray(inputs["L0_attn_Wv"], np.float32)
    d["wg"] = np.asarray(inputs["L0_attn_Wg"], np.float32)
    d["wo"] = np.asarray(inputs["L0_attn_Wo"], np.float32)
    d["w2"] = np.asarray(inputs["L0_attn_W2"], np.float32)
    d["wfT"] = np.ascontiguousarray(
        np.asarray(inputs["L0_ffn_weight"], np.float32).T)
    wa = np.asarray(inputs["L0_attn_Wa"], np.float32)
    wb = np.asarray(inputs["L0_attn_Wb"], np.float32)
    wabp = np.zeros((DCH, 128, 68), np.float32)
    wabp[:, :, 32:36] = wa.reshape(DCH, 128, 4)
    wabp[:, :, 64:68] = wb.reshape(DCH, 128, 4)
    d["wabp"] = np.ascontiguousarray(
        wabp.transpose(1, 0, 2).reshape(128, 68 * DCH)).astype(bf)
    for nm, key, w in (("cqw", "L0_attn_conv_q_w", KEY),
                       ("ckw", "L0_attn_conv_k_w", KEY),
                       ("cvw", "L0_attn_conv_v_w", VAL)):
        cwt = np.asarray(inputs[key], np.float32)
        nch = w // 128
        d[nm] = np.ascontiguousarray(
            cwt.reshape(nch, 128, 4).transpose(1, 0, 2).reshape(128, 4 * nch))
    def cols(v):
        v = np.asarray(v, np.float32)
        n = v.shape[0] // 128
        return np.ascontiguousarray(v.reshape(n, 128).T)
    d["g1c"] = cols(inputs["L0_attn_ln1_g"])
    d["b1c"] = cols(inputs["L0_attn_ln1_b"])
    d["rmc"] = cols(inputs["L0_attn_rms_in_w"])
    d["g2c"] = cols(inputs["L0_attn_ln2_g"])
    d["b2wc"] = cols(inputs["L0_attn_b2"])
    d["b2c"] = cols(inputs["L0_attn_ln2_b"])
    d["fbc"] = cols(inputs["L0_ffn_bias"])
    d["onwc"] = cols(inputs["L0_attn_o_norm_w"])
    dtb = np.zeros((128, 1), np.float32)
    dtb[32:36, 0] = np.asarray(inputs["L0_attn_dt_bias"], np.float32)
    d["dtbp"] = dtb
    alo = np.zeros((128, 1), np.float32)
    alo[32:36, 0] = np.asarray(inputs["L0_attn_A_log"], np.float32)
    d["alogp"] = alo
    d["idf"] = np.eye(128, dtype=np.float32)
    d["idb"] = np.eye(128).astype(bf)
    d["onesr"] = np.ones((1, 128), np.float32)
    d["onescb"] = np.ones((128, 1)).astype(bf)
    d["negbig"] = (np.eye(128) * -1e30).astype(bf)
    killY = np.tril(np.ones((128, 128), np.float32))
    killM = np.tril(np.ones((128, 128), np.float32), k=-1)
    d["maskY4"] = np.tile(killY, (1, 4)).astype(bf)
    d["maskM4"] = np.tile(killM, (1, 4)).astype(bf)
    hq = np.zeros((128, 9 * DCH), np.float32)
    for c in range(DCH):
        hq[:, 9 * c + RW_SSQ + c // 2] = 1.0
    d["hotq"] = hq.astype(bf)
    hk = np.zeros((128, 13 * DCH), np.float32)
    for c in range(DCH):
        hk[:, 13 * c + RW_SSK + c // 2] = 1.0
    d["hotk"] = hk.astype(bf)
    hv = np.zeros((128, 17 * VCH), np.float32)
    for c in range(VCH):
        hv[:, 17 * c + RW_SSO + c // 4] = 1.0
    d["hotv"] = hv.astype(bf)
    selo = np.zeros((16, 16, 128), np.float32)
    for j in range(16):
        selo[j, j, :] = 1.0
    d["selo_f"] = np.ascontiguousarray(
        selo.transpose(1, 0, 2).reshape(16, 16 * 128))
    onespad = np.zeros((128, 49), np.float32)
    onespad[:, 0] = 1.0; onespad[:, 7] = 1.0
    onespad[:, 27] = 1.0; onespad[:, 48] = 1.0
    d["onespad"] = onespad
    e127 = np.zeros((128, 1), np.float32)
    e127[127, 0] = 1.0
    d["e127"] = e127
    return d


def kernel(**inputs):
    nc, _, _ = build_program("full")
    in_maps = [_prep_inputs(inputs, c) for c in range(B)]
    res = run_bass_kernel_spmd(nc, in_maps, core_ids=list(range(B)))
    outs = [np.asarray(r["out"]).T for r in res.results]
    return np.ascontiguousarray(np.stack(outs, axis=0))


# version 10
# speedup vs baseline: 1.1753x; 1.0631x over previous
import numpy as np
import ml_dtypes
import concourse.bass as bass
import concourse.mybir as mybir
from concourse.bass_utils import run_bass_kernel_spmd

F32 = mybir.dt.float32
BF16 = mybir.dt.bfloat16
FP8 = mybir.dt.float8e4
AF = mybir.ActivationFunctionType
ALU = mybir.AluOpType
AX = mybir.AxisListType.X

B, T, D = 8, 4096, 1024
H, Kd, Vd, CONV = 4, 256, 512, 4
KEY, VAL = 1024, 2048
TILE = 512
NT = T // TILE
C = 128
NCH = TILE // C
DCH = D // 128
VCH = VAL // 128
LN_EPS, EPS = 1e-5, 1e-6

ENGS = ("te", "ve", "se", "sy", "gp")

CW = 80
C_S1, C_SG2X, C_SGBX, C_SX2, C_SG2X2 = 0, 1, 2, 3, 4
C_SSQ, C_SSK, C_SSO = 5, 9, 13
C_SZ, C_SZ2, C_SY, C_SY2 = 17, 18, 19, 20
C_C, C_BETA = 21, 25
C_M, C_V, C_S, C_SP, C_R = 29, 30, 31, 32, 33
C_T1, C_T2, C_T3 = 34, 35, 36
SEL0 = 38
CS_S, CS_ROWA, CS_ROWB, CS_ROWD, CS_RC = 38, 39, 40, 41, 42
CS_B, CS_E = 43, 47
CS_X1, CS_X2 = 51, 52
C_LNRK, C_LNRQ, C_LNB, C_LA, C_EXPB, C_EXPD = 54, 58, 62, 66, 70, 74

P_S, P_ROWA, P_ROWB, P_ROWD, P_RC = 0, 1, 2, 3, 4
P_B, P_E = 5, 9
P_X1, P_X2 = 13, 14

RW_STAT = 0
RW_SSQ, RW_SSK, RW_SSO = 5, 9, 13
RW_SZ, RW_SZ2, RW_SY, RW_SY2 = 17, 18, 19, 20
RW_G, RW_BP = 32, 64


class Prog:
    def __init__(self):
        self.ops = []
        self.last_w = {}
        self.readers = {}

    def emit(self, eng, fn, reads=(), writes=(), dma=False, force_sig=False,
             inc=None):
        idx = len(self.ops)
        deps = set()
        for b in reads:
            w = self.last_w.get(b)
            if w is not None:
                deps.add(w)
        for b in writes:
            w = self.last_w.get(b)
            if w is not None:
                deps.add(w)
            for r in self.readers.get(b, ()):
                deps.add(r)
        deps.discard(idx)
        self.ops.append(dict(eng=eng, fn=fn, deps=sorted(deps),
                             sig=bool(force_sig),
                             inc=(inc if inc is not None else
                                  (16 if dma else 1)),
                             dma=dma,
                             skey=(("d_" + writes[0]) if (dma and writes)
                                   else eng)))
        for b in reads:
            self.readers.setdefault(b, []).append(idx)
        for b in writes:
            self.last_w[b] = idx
            self.readers[b] = []
        return idx

    def build(self, nc, sems, spacers=None, es=None):
        ops = self.ops
        for op in ops:
            for d in op["deps"]:
                if ops[d]["eng"] != op["eng"] or ops[d]["dma"]:
                    ops[d]["sig"] = True
        gfn = getattr(self, "groupfn", None)
        for op in ops:
            if op["dma"] and gfn is not None:
                op["sg"], op["tot"] = gfn(op["skey"])
            else:
                op["sg"], op["tot"] = op["skey"], False
        dkeys = sorted({op["sg"] for op in ops if op["dma"]})
        assert len(dkeys) + len(ENGS) <= 100, len(dkeys)
        import sys as _sys
        print(f"[kernel] dma sems: {len(dkeys)}", file=_sys.stderr)
        for k in dkeys:
            if k not in sems:
                sems[k] = es.enter_context(nc.semaphore(k))
        cum = {e: 0 for e in ENGS}
        cum.update({k: 0 for k in dkeys})
        val = [0] * len(ops)
        for i, op in enumerate(ops):
            if op["sig"]:
                cum[op["sg"]] += op["inc"]
            val[i] = cum[op["sg"]]
        for i, op in enumerate(ops):
            if op["tot"]:
                val[i] = cum[op["sg"]]
            for d in op["deps"]:
                assert not (op["tot"] and ops[d]["sg"] == op["sg"]), \
                    "total-group member depends on same-group member"
        per_eng = {e: [] for e in ENGS}
        for i, op in enumerate(ops):
            per_eng[op["eng"]].append(i)

        def run_engine(eng_name, eng_obj):
            waited = {}
            prev_i = None
            for i in per_eng[eng_name]:
                op = ops[i]
                need = {}
                for d in op["deps"]:
                    pe = ops[d]["eng"]
                    if (pe != eng_name or ops[d]["dma"]) and ops[d]["sig"]:
                        sk = ops[d]["sg"]
                        need[sk] = max(need.get(sk, 0), val[d])
                for sk, v in sorted(need.items()):
                    if waited.get(sk, 0) < v:
                        eng_obj.wait_ge(sems[sk], v)
                        waited[sk] = v
                if (spacers and eng_name in spacers and prev_i is not None
                        and prev_i in op["deps"] and not ops[prev_i]["dma"]):
                    spacers[eng_name](eng_obj)
                inst = op["fn"](eng_obj)
                if op["sig"]:
                    inst.then_inc(sems[op["sg"]], op["inc"])
                prev_i = i

        with nc.Block() as block:
            @block.tensor
            def _(e):
                run_engine("te", e)

            @block.vector
            def _(e):
                run_engine("ve", e)

            @block.scalar
            def _(e):
                run_engine("se", e)

            @block.sync
            def _(e):
                run_engine("sy", e)

            @block.gpsimd
            def _(e):
                run_engine("gp", e)
                for en in sorted(cum):
                    if cum[en] > 0:
                        e.wait_ge(sems[en], cum[en])
        return len(ops)


def build_program(upto="full", taps=()):
    taps = set(taps)
    nc = bass.Bass("TRN2", target_bir_lowering=False)
    p = Prog()

    _konst = {"d_" + n for n in (
        "idf", "idb", "onesr", "onescb", "negbig", "maskY", "maskM",
        "hotq", "hotk", "hotv", "selo", "e127", "onespad",
        "g1c", "b1c", "rmc", "g2c", "b2wc", "fbc", "b2c",
        "onwc", "dtbp", "wabp", "convq", "convk", "convv", "nega")}

    def _sgroup(k):
        if k in _konst:
            return "g_konst", True
        if k.startswith("d_x2w"):
            return "g_x2sp", False
        if k.startswith("d_lxw"):
            return "g_lxsp", False
        if k.startswith("d_outw"):
            return "g_outw", False
        if k.startswith("d_ln0w"):
            return "g_ln0sp", False
        if k.startswith("d_q8w"):
            return "g_q8sp", False
        if k.endswith("_tap"):
            return "g_tap", False
        return k, False
    p.groupfn = _sgroup

    def din(name, shape, dt=F32):
        return nc.dram_tensor(name, shape, dt, kind="ExternalInput").ap()

    x0 = din("x0", [D, T])
    wq = din("wq", [D, KEY]); wk = din("wk", [D, KEY])
    wv = din("wv", [D, VAL]); wg = din("wg", [D, VAL])
    wo = din("wo", [VAL, D]); w2 = din("w2", [D, D])
    wfT = din("wfT", [D, D])
    wabp = din("wabp", [128, 68 * DCH], BF16)
    cqw = din("cqw", [128, 4 * DCH])
    ckw = din("ckw", [128, 4 * DCH])
    cvw = din("cvw", [128, 4 * VCH])
    g1c = din("g1c", [128, DCH]); b1c = din("b1c", [128, DCH])
    rmc = din("rmc", [128, DCH]); g2c = din("g2c", [128, DCH])
    b2wc = din("b2wc", [128, DCH]); fbc = din("fbc", [128, DCH])
    b2c = din("b2c", [128, DCH])
    onwc = din("onwc", [128, 4])
    dtbp = din("dtbp", [128, 1]); alogp = din("alogp", [128, 1])
    idf = din("idf", [128, 128]); idb = din("idb", [128, 128], BF16)
    onesr = din("onesr", [1, 128])
    onescb = din("onescb", [128, 1], BF16)
    negbig = din("negbig", [128, 128], BF16)
    maskY4 = din("maskY4", [128, 512], BF16)
    maskM4 = din("maskM4", [128, 512], BF16)
    hotq = din("hotq", [128, 9 * DCH], BF16)
    hotk = din("hotk", [128, 13 * DCH], BF16)
    hotv = din("hotv", [128, 17 * VCH], BF16)
    selo_f = din("selo_f", [16, 16 * 128])
    e127 = din("e127", [128, 1])
    onespad = din("onespad", [128, 49])

    out_d = nc.dram_tensor("out", [D, T], F32, kind="ExternalOutput").ap()

    def dint(name, shape, dt=F32):
        return nc.dram_tensor(name, shape, dt, kind="Internal").ap()

    ln0sp = dint("ln0sp", [D, T])
    q8sp = dint("q8sp", [D, T], FP8)
    x2sp = dint("x2sp", [D, T])
    lnx2sp = dint("lnx2sp", [D, T])
    wq_b = dint("wq_b", [D, KEY], BF16); wk_b = dint("wk_b", [D, KEY], BF16)
    wv_b = dint("wv_b", [D, VAL], BF16); wg_b = dint("wg_b", [D, VAL], BF16)
    wo_b = dint("wo_b", [VAL, D], BF16); w2p_b = dint("w2p_b", [D, D], BF16)
    wf8sp = dint("wf8sp", [D, D], FP8)
    cc1i = nc.dram_tensor("cc1i", [1, 128], F32)
    cc1o = nc.dram_tensor("cc1o", [1, 128], F32)
    cc2i = nc.dram_tensor("cc2i", [1, 128], F32)
    cc2o = nc.dram_tensor("cc2o", [1, 128], F32)

    tap_outs = {}
    def mktap(name, shape, dt=F32):
        tap_outs[name] = nc.dram_tensor("tap_" + name, shape, dt,
                                        kind="ExternalOutput").ap()

    for nm in taps:
        if nm == "sca":
            mktap("sca", [1, 16])
        elif nm in ("x1", "h", "z", "x2f"):
            mktap(nm, [D, T], F32 if nm in ("x1", "z", "x2f") else F32)
        elif nm in ("q", "k"):
            mktap(nm, [KEY, T])
        elif nm in ("v", "o", "gt"):
            mktap(nm, [VAL, T])
        elif nm == "gb":
            mktap(nm, [128, T])
        elif nm == "enc":
            mktap(nm, [D, T])
        elif nm in ("dY0", "dX0", "dTT", "dMT", "dEhY", "dEhM"):
            mktap(nm, [128, 512], F32)
        elif nm in ("dRp", "dDl"):
            mktap(nm, [128, 768], F32)
        elif nm == "dKh":
            mktap(nm, [128, 1024], F32)
        elif nm == "dRblk":
            mktap(nm, [16, 128], F32)
        elif nm == "dCols":
            mktap(nm, [128, 80], F32)
        elif nm == "dS":
            mktap(nm, [128, 4096], F32)
        elif nm == "dWp":
            mktap(nm, [128, 512], F32)
        elif nm in ("dRst0", "dRst1", "dRs2"):
            mktap(nm, [128, 512], F32)
        elif nm == "dCar":
            mktap(nm, [128, 96], F32)
        elif nm in ("dPreK", "dPreQ"):
            mktap(nm, [128, 515], F32)
        elif nm in ("dCvK",):
            mktap(nm, [128, 512], F32)
        elif nm == "dLn0":
            mktap(nm, [D, T], F32)
        elif nm == "dQ8":
            mktap(nm, [D, T], FP8)

    from contextlib import ExitStack
    es = ExitStack()
    def S(name, shape, dt):
        return es.enter_context(nc.sbuf_tensor(name, shape, dt))
    def PS(name, shape, dt=F32):
        return es.enter_context(nc.psum_tensor(name, shape, dt))

    t_idf = S("t_idf", [128, 128], F32)
    t_idb = S("t_idb", [128, 128], BF16)
    t_onesr = S("t_onesr", [1, 128], F32)
    t_onescb = S("t_onescb", [128, 1], BF16)
    t_negbig = S("t_negbig", [128, 128], BF16)
    t_maskY = S("t_maskY", [128, 512], BF16)
    t_maskM = S("t_maskM", [128, 512], BF16)
    t_hotq = S("t_hotq", [128, 9 * DCH], BF16)
    t_hotk = S("t_hotk", [128, 13 * DCH], BF16)
    t_hotv = S("t_hotv", [128, 17 * VCH], BF16)
    t_selo = S("t_selo", [16, 16 * 128], F32)
    t_e127 = S("t_e127", [128, 1], F32)
    t_onespad = S("t_onespad", [128, 49], F32)
    t_g1c = S("t_g1c", [128, DCH], F32)
    t_b1c = S("t_b1c", [128, DCH], F32)
    t_rmc = S("t_rmc", [128, DCH], F32)
    t_g2c = S("t_g2c", [128, DCH], F32)
    t_b2wc = S("t_b2wc", [128, DCH], F32)
    t_b2c = S("t_b2c", [128, DCH], F32)
    t_gelb = S("t_gelb", [128, DCH], F32)
    t_fbc = S("t_fbc", [128, DCH], F32)
    t_onwc = S("t_onwc", [128, 4], F32)
    t_dtbp = S("t_dtbp", [128, 1], F32)
    t_nega = S("t_nega", [128, 1], F32)
    t_statq = S("t_statq", [128, 10 * DCH], BF16)
    t_statz = S("t_statz", [128, 44], BF16)
    t_selP = S("t_selP", [16, 128 * DCH], F32)
    t_selw2 = S("t_selw2", [16, 128 * DCH], F32)
    t_wabp = S("t_wabp", [128, 68 * DCH], BF16)
    t_convq = S("t_convq", [128, 4 * DCH], F32)
    t_convk = S("t_convk", [128, 4 * DCH], F32)
    t_convv = S("t_convv", [128, 4 * VCH], F32)
    t_eps5 = S("t_eps5", [128, 1], F32)
    t_eps6 = S("t_eps6", [128, 1], F32)
    t_one1 = S("t_one1", [128, 1], F32)
    t_ones36 = S("t_ones36", [36, TILE], F32)
    t_w2cs = S("t_w2cs", [1, D], F32)
    t_w2csc = S("t_w2csc", [128, DCH], F32)
    t_ucol = S("t_ucol", [128, DCH], F32)
    t_wcol = S("t_wcol", [128, DCH], F32)
    t_cst = S("t_cst", [1, 8], F32)
    t_cstb = S("t_cstb", [128, 4], F32)
    t_am1 = S("t_am1", [128, 32], F32)
    t_am2 = S("t_am2", [128, 56], F32)
    t_rowcc = S("t_rowcc", [1, 128], F32)
    t_s1x = S("t_s1x", [1, 16], F32)
    t_gcc = S("t_gcc", [1, 128], F32)
    t_sxh = S("t_sxh", [128, 1], F32)
    t_sx2h = S("t_sx2h", [128, 1], F32)
    t_f1 = S("t_f1", [128, 1], F32)
    t_f2 = S("t_f2", [128, 1], F32)
    t_swh = S("t_swh", [128, 1], F32)
    t_stg16 = S("t_stg16", [128, 16], F32)

    r_wsl = [S(f"r_wsl{i}", [128, 1024], BF16) for i in range(4)]
    r_wo2 = [S(f"r_wo2{i}", [128, 1024], BF16) for i in range(2)]
    r_f8 = [S(f"r_f8{i}", [128, 1024], FP8) for i in range(3)]
    r_wf = [w[:, :].bitcast(F32) for w in r_wo2]
    r_x0 = [S(f"r_x0{i}", [128, TILE], F32) for i in range(2)]
    r_32 = [S(f"r_32{i}", [128, TILE], F32) for i in range(2)]
    r_sq = [S(f"r_sq{i}", [128, TILE], BF16) for i in range(2)]
    r_ff = [S(f"r_ff{i}", [128, TILE], BF16) for i in range(2)]
    r_ga = [S(f"r_ga{i}", [128, TILE], BF16) for i in range(2)]
    r_dgv = [S(f"r_dgv{i}", [128, 128], BF16) for i in range(2)]
    r_pre = [S(f"r_pre{i}", [128, 64 + TILE], BF16) for i in range(2)]

    t_x1 = S("t_x1", [128, DCH * TILE], F32)
    t_z = S("t_z", [128, DCH * TILE], F32)
    t_zb = S("t_zb", [128, DCH * TILE], BF16)
    t_h = S("t_h", [128, DCH * TILE], BF16)
    t_q8 = S("t_q8", [128, DCH * TILE], FP8)
    t_q = S("t_q", [128, DCH * TILE], BF16)
    t_k = S("t_k", [128, DCH * TILE], BF16)
    t_v = S("t_v", [128, VCH * TILE], BF16)
    t_o = S("t_o", [128, VCH * TILE], BF16)
    t_carry = S("t_carry", [128, (2 * DCH + VCH) * 64], BF16)
    t_rstage = S("t_rstage", [128, TILE], F32)
    t_rstg2 = S("t_rstg2", [128, TILE], F32)
    r_cols = [S(f"r_cols{i}", [128, CW], F32) for i in range(2)]
    r_rblk = [S(f"r_rblk{i}", [16, 128], F32) for i in range(2)]
    t_sbc = S("t_sbc", [128, TILE], F32)
    t_eqb = S("t_eqb", [128, 128], F32)
    t_cc4 = S("t_cc4", [1, 4], F32)
    t_cCb = S("t_cCb", [128, 4], F32)
    t_gCc = S("t_gCc", [128, 4], F32)

    t_hrb = S("t_hrb", [16, TILE], F32)
    t_zrb = S("t_zrb", [16, TILE], F32)
    t_S = S("t_S", [128, H * 2 * Vd], BF16)
    r_X = [S(f"r_X{i}", [128, 4 * C], BF16) for i in range(2)]
    r_Y = [S(f"r_Y{i}", [128, 4 * C], BF16) for i in range(2)]
    r_Zs = [S(f"r_Zs{i}", [128, 4 * C], BF16) for i in range(2)]
    t_TT = S("t_TT", [128, 4 * C], BF16)
    t_MT = S("t_MT", [128, 4 * C], BF16)
    t_Eh = S("t_Eh", [128, 4 * C], BF16)
    r_Rp = [S(f"r_Rp{i}", [128, Vd + Kd], BF16) for i in range(2)]
    r_Dl = [S(f"r_Dl{i}", [128, Vd + Kd], BF16) for i in range(2)]
    t_Kh = S("t_Kh", [128, H * Kd], BF16)
    r_Wp = [S(f"r_Wp{i}", [128, 2 * Kd], BF16) for i in range(2)]
    t_g0 = S("t_g0", [128, 512], BF16)
    r_Qt = [S(f"r_Qt{i}", [128, C], BF16) for i in range(2)]

    pb0 = PS("pb0", [128, 512]); pb1 = PS("pb1", [128, 512])
    pb2 = PS("pb2", [128, 512]); pb3 = PS("pb3", [128, 512])
    pb4 = PS("pb4", [128, 512])
    pdel = PS("pdel", [128, 1024])
    pbt = PS("pbt", [128, 1024], BF16)

    sems = {e: es.enter_context(nc.semaphore("sem_" + e)) for e in ENGS}
    t_space = S("t_space", [1, 128], F32)

    def _sp(e):
        e.tensor_single_scalar(out=t_space[0:1, 0:64],
                               in_=t_space[0:1, 64:128],
                               scalar=1.0, op=ALU.mult)
    spacers = {"ve": _sp, "gp": _sp}

    def dma(eng, dst, src, reads, writes, force_sig=True):
        p.emit(eng, lambda e, d=dst, s=src: e.dma_start(out=d, in_=s),
               reads=reads, writes=writes, dma=True, force_sig=True)

    def mm(out, lhsT, rhs, start, stop, reads, writes):
        p.emit("te", lambda e, o=out, a=lhsT, b=rhs, s1=start, s2=stop:
               e.matmul(o, a, b, start=s1, stop=s2),
               reads=reads, writes=writes)

    def trans(out, in_, ident, reads, writes):
        p.emit("te", lambda e, o=out, i=in_, d=ident: e.transpose(o, i, d),
               reads=reads, writes=writes)

    def cp(eng, out, in_, reads, writes):
        p.emit(eng, lambda e, o=out, i=in_: e.tensor_copy(out=o, in_=i),
               reads=reads, writes=writes)

    def act(out, in_, fn, reads, writes, bias=0.0, scale=1.0):
        p.emit("se", lambda e, o=out, i=in_, f=fn, b=bias, s=scale:
               e.activation(out=o, in_=i, func=f, bias=b, scale=s),
               reads=reads, writes=writes)

    def tt(eng, out, a, b, op, reads, writes):
        p.emit(eng, lambda e, o=out, x=a, y=b, f=op:
               e.tensor_tensor(out=o, in0=x, in1=y, op=f),
               reads=reads, writes=writes)

    def ts(eng, out, a, s1, s2, op0, op1, reads, writes):
        if s2 is None:
            p.emit(eng, lambda e, o=out, x=a, u=s1, f0=op0:
                   e.tensor_single_scalar(out=o, in_=x, scalar=u, op=f0),
                   reads=reads, writes=writes)
        else:
            p.emit(eng, lambda e, o=out, x=a, u=s1, v=s2, f0=op0, f1=op1:
                   e.tensor_scalar(out=o, in0=x, scalar1=u, scalar2=v,
                                   op0=f0, op1=f1),
                   reads=reads, writes=writes)

    def stt(eng, out, in0, scalar, in1, op0, op1, reads, writes):
        p.emit(eng, lambda e, o=out, x=in0, s=scalar, y=in1, f0=op0, f1=op1:
               e.scalar_tensor_tensor(out=o, in0=x, scalar=s, in1=y,
                                      op0=f0, op1=f1),
               reads=reads, writes=writes)

    def red(out, in_, op, absv, reads, writes):
        p.emit("ve", lambda e, o=out, i=in_, f=op, a=absv:
               e.tensor_reduce(o, i, AX, f, apply_absolute_value=a),
               reads=reads, writes=writes)

    def recip(out, in_, reads, writes):
        p.emit("ve", lambda e, o=out, i=in_: e.reciprocal(out=o, in_=i),
               reads=reads, writes=writes)

    def mset(eng, ap_, v, writes):
        p.emit(eng, lambda e, a=ap_, x=v: e.memset(a, x), writes=writes)

    def scan(out, d0, d1, reads, writes):
        p.emit("ve", lambda e, o=out, a=d0, b=d1:
               e.tensor_tensor_scan(out=o, data0=a, data1=b, initial=0.0,
                                    op0=ALU.mult, op1=ALU.add),
               reads=reads, writes=writes)

    def pzero(pb, pname):
        p.emit("ve", lambda e, a=pb: e.memset(a[:, 0:512], 0.0),
               reads=(), writes=(pname,))

    def tap(name, src_ap, reads):
        if name in tap_outs:
            dma("gp", tap_outs[name], src_ap, reads=reads,
                writes=(name + "_tap",), force_sig=True)

    def selbc(out, pos, rblk, rname, start, stop, wname, extra_reads=()):
        mm(out, t_selo[:, 128 * pos:128 * (pos + 1)], rblk[0:16, :],
           start, stop, ("selo", rname) + tuple(extra_reads), (wname,))

    for dst, src, nm in [
        (t_idf, idf, "idf"), (t_idb, idb, "idb"), (t_onesr, onesr, "onesr"),
        (t_onescb, onescb, "onescb"), (t_negbig, negbig, "negbig"),
        (t_maskY, maskY4, "maskY"), (t_maskM, maskM4, "maskM"),
        (t_hotq, hotq, "hotq"), (t_hotk, hotk, "hotk"), (t_hotv, hotv, "hotv"),
        (t_selo, selo_f, "selo"), (t_e127, e127, "e127"),
        (t_onespad, onespad, "onespad"),
        (t_g1c, g1c, "g1c"), (t_b1c, b1c, "b1c"), (t_rmc, rmc, "rmc"),
        (t_g2c, g2c, "g2c"), (t_b2wc, b2wc, "b2wc"), (t_fbc, fbc, "fbc"),
        (t_b2c, b2c, "b2c"),
        (t_onwc, onwc, "onwc"), (t_dtbp, dtbp, "dtbp"),
        (t_wabp, wabp, "wabp"), (t_convq, cqw, "convq"),
        (t_convk, ckw, "convk"), (t_convv, cvw, "convv"),
    ]:
        dma("sy", dst[:, :], src, (), (nm,))
    dma("sy", t_nega[:, :], alogp, (), ("nega",))
    act(t_nega[32:36, :], t_nega[32:36, :], AF.Exp, ("nega",), ("nega",))
    ts("ve", t_nega[32:36, :], t_nega[32:36, :], -1.0, None, ALU.mult,
       ALU.mult, ("nega",), ("nega",))
    mset("ve", t_space[:, :], 0.0, ("space",))
    mset("gp", t_space[:, :], 0.0, ("space",))
    for tns, v, nm in [(t_eps5, LN_EPS, "eps5"), (t_eps6, EPS, "eps6"),
                       (t_one1, 1.0, "one1")]:
        mset("ve", tns[:, :], v, (nm,))
    mset("ve", t_ones36[:, :], 1.0, ("ones36",))
    mset("ve", t_S[:, :], 0.0, ("S0", "S1", "S2", "S3"))
    mset("ve", t_carry[:, :], 0.0, ("carry",))
    mset("ve", t_g0[:, :], 0.0, ("g0",))
    mset("ve", t_statq[:, :], 0.0, ("statq",))
    mset("ve", t_statz[:, :], 0.0, ("statz",))
    mset("ve", t_am1[:, :], 0.0, ("am1",))
    mset("ve", t_am2[:, :], 0.0, ("am2",))
    for i_ in range(2):
        mset("ve", r_cols[i_][:, :], 0.0, (f"rcols{i_}",))
        mset("ve", r_rblk[i_][:, :], 0.0, (f"rrblk{i_}",))
    mset("ve", t_hrb[:, :], 0.0, ("hrb",))
    mset("ve", t_zrb[:, :], 0.0, ("zrb",))
    mset("ve", t_rstage[:, :], 0.0, ("rstage",))
    mset("ve", t_rstg2[:, :], 0.0, ("rstg2",))
    mset("ve", t_sbc[:, :], 0.0, ("osbc", "sbc"))

    for c in range(DCH):
        b0 = 10 * c
        cp("ve", t_statq[:, b0:b0 + 1], t_onescb[:, :], ("onescb",), ("statq",))
        tt("ve", t_statq[:, b0 + 1:b0 + 2], t_g1c[:, c:c + 1],
           t_g1c[:, c:c + 1], ALU.mult, ("g1c",), ("statq",))
        tt("ve", t_statq[:, b0 + 2:b0 + 3], t_g1c[:, c:c + 1],
           t_b1c[:, c:c + 1], ALU.mult, ("g1c", "b1c"), ("statq",))
        cp("ve", t_statq[:, b0 + 8:b0 + 9], t_onescb[:, :], ("onescb",),
           ("statq",))
        tt("ve", t_statq[:, b0 + 9:b0 + 10], t_g1c[:, c:c + 1],
           t_g1c[:, c:c + 1], ALU.mult, ("g1c",), ("statq",))
    cp("ve", t_statz[:, 17:18], t_onescb[:, :], ("onescb",), ("statz",))
    cp("ve", t_statz[:, 18 + 18:18 + 19], t_onescb[:, :], ("onescb",),
       ("statz",))
    t_staty = S("t_staty", [128, 42], BF16)
    mset("ve", t_staty[:, :], 0.0, ("staty",))
    cp("ve", t_staty[:, 19:20], t_onescb[:, :], ("onescb",), ("staty",))
    cp("ve", t_staty[:, 20 + 20:20 + 21], t_onescb[:, :], ("onescb",),
       ("staty",))

    tt("ve", t_ucol[:, :], t_g1c[:, :], t_rmc[:, :], ALU.mult,
       ("g1c", "rmc"), ("ucol",))
    tt("ve", t_wcol[:, :], t_b1c[:, :], t_rmc[:, :], ALU.mult,
       ("b1c", "rmc"), ("wcol",))
    t_bcolb = S("t_bcolb", [128, 3 * DCH], BF16)
    for c in range(DCH):
        tt("ve", t_bcolb[:, 3 * c:3 * c + 1], t_g1c[:, c:c + 1],
           t_g1c[:, c:c + 1], ALU.mult, ("g1c",), ("bcolb",))
        tt("ve", t_bcolb[:, 3 * c + 1:3 * c + 2], t_g1c[:, c:c + 1],
           t_b1c[:, c:c + 1], ALU.mult, ("g1c", "b1c"), ("bcolb",))
        tt("ve", t_bcolb[:, 3 * c + 2:3 * c + 3], t_b1c[:, c:c + 1],
           t_b1c[:, c:c + 1], ALU.mult, ("b1c",), ("bcolb",))
    for j in range(3):
        for c in range(DCH):
            mm(pb3[0:1, 128 * j:128 * j + 1], t_bcolb[:, 3 * c + j:3 * c + j + 1],
               t_onescb[:, :], c == 0, c == DCH - 1, ("bcolb", "onescb"),
               ("pb3",))
    cp("ve", t_cst[:, 0:1], pb3[0:1, 0:1], ("pb3",), ("cst",))
    cp("ve", t_cst[:, 1:2], pb3[0:1, 128:129], ("pb3",), ("cst",))
    cp("ve", t_cst[:, 2:3], pb3[0:1, 256:257], ("pb3",), ("cst",))
    ts("ve", t_cst[:, 0:3], t_cst[:, 0:3], 1.0 / D, None, ALU.mult, ALU.mult,
       ("cst",), ("cst",))
    for j_ in range(3):
        mm(pb4[:, 132 + j_:133 + j_], t_onesr[:, :], t_cst[0:1, j_:j_ + 1],
           True, True, ("onesr", "cst"), ("pb4",))
    cp("ve", t_cstb[:, 0:3], pb4[:, 132:135], ("pb4",), ("cstb",))


    p.emit("ve", lambda e: e.memset(pdel[:, 0:1024], 0.0), reads=(),
           writes=("pdelw",))
    specs = [(wq, wq_b, KEY, False), (wk, wk_b, KEY, False),
             (wv, wv_b, VAL, False), (wg, wg_b, VAL, False),
             (wo, wo_b, D, False), (w2, w2p_b, D, True)]
    for si, (src, dst, width, isw2) in enumerate(specs):
        nrow = src.shape[0] // 128
        for rc in range(nrow):
            for hf in range(width // 512):
                ii = (rc * (width // 512) + hf) % 2
                sl, bn = r_wf[ii], f"rwo2{ii}"
                bsl, bbn = r_wsl[ii], f"rwsl{ii}"
                dma("sy", sl[:, 0:512], src[128 * rc:128 * (rc + 1),
                                            512 * hf:512 * (hf + 1)],
                    (bn,), (bn,))
                if isw2:
                    mm(pdel[0:1, 512 * (hf % 2):512 * (hf % 2) + 512],
                       t_b2c[:, rc:rc + 1], sl[:, 0:512],
                       False, False, ("b2c", bn), ("pdelw",))
                    ts("ve", bsl[:, 0:512], sl[:, 0:512],
                       t_g2c[:, rc:rc + 1], None,
                       ALU.mult, ALU.mult, (bn, "g2c"), (bbn,))
                else:
                    cp("ve", bsl[:, 0:512], sl[:, 0:512], (bn,), (bbn,))
                dma("gp", dst[128 * rc:128 * (rc + 1),
                              512 * hf:512 * (hf + 1)], bsl[:, 0:512],
                    (bbn,), (f"wh{si}",))
    cp("ve", t_w2cs[:, :], pdel[0:1, 0:1024], ("pdelw",), ("w2cs",))
    for c in range(DCH):
        trans(pb4[:, 0:1], t_w2cs[0:1, 128 * c:128 * (c + 1)],
              t_idf[0:1, 0:1], ("w2cs", "idf"), ("pb4",))
        cp("ve", t_gelb[:, c:c + 1], pb4[:, 0:1], ("pb4",), ("gelb",))
    for c in range(DCH):
        tt("ve", t_gelb[:, c:c + 1], t_gelb[:, c:c + 1], t_b2wc[:, c:c + 1],
           ALU.add, ("gelb", "b2wc"), ("gelb",))
    for rc in range(DCH):
        sl, bn = r_wsl[rc % 2], f"rwsl{rc % 2}"
        dma("sy", sl[:, :], w2p_b[128 * rc:128 * (rc + 1), 0:1024],
            (bn, "wh5"), (bn,))
        mm(pdel[0:1, 0:512], t_onescb[:, :], sl[:, 0:512], rc == 0, False,
           ("onescb", bn), ("pdel",))
        mm(pdel[0:1, 512:1024], t_onescb[:, :], sl[:, 512:1024], False,
           rc == DCH - 1, ("onescb", bn), ("pdel",))
    cp("ve", t_w2cs[:, :], pdel[0:1, 0:1024], ("pdel",), ("w2cs",))
    for c in range(DCH):
        trans(pb4[:, 0:1], t_w2cs[0:1, 128 * c:128 * (c + 1)],
              t_idf[0:1, 0:1], ("w2cs", "idf"), ("pb4",))
        cp("ve", t_w2csc[:, c:c + 1], pb4[:, 0:1], ("pb4",), ("w2csc",))

    for c in range(DCH):
        mset("ve", t_stg16[:, :], 0.0, ("stg16",))
        cp("ve", t_stg16[:, P_ROWB:P_ROWB + 1], t_ucol[:, c:c + 1],
           ("ucol",), ("stg16",))
        cp("ve", t_stg16[:, P_RC:P_RC + 1], t_wcol[:, c:c + 1],
           ("wcol",), ("stg16",))
        trans(pb4[0:16, 0:128], t_stg16[:, :], t_idf[:, :],
              ("stg16", "idf"), ("pb4",))
        cp("ve", t_selP[:, 128 * c:128 * (c + 1)], pb4[0:16, 0:128],
           ("pb4",), ("selP",))
        mset("ve", t_stg16[:, :], 0.0, ("stg16",))
        cp("ve", t_stg16[:, P_X2:P_X2 + 1], t_w2csc[:, c:c + 1],
           ("w2csc",), ("stg16",))
        trans(pb4[0:16, 0:128], t_stg16[:, :], t_idf[:, :],
              ("stg16", "idf"), ("pb4",))
        cp("ve", t_selw2[:, 128 * c:128 * (c + 1)], pb4[0:16, 0:128],
           ("pb4",), ("selw2",))

    for rc in range(DCH):
        for hf in range(2):
            sl, bn = r_wf[(2 * rc + hf) % 2], f"rwo2{(2 * rc + hf) % 2}"
            dma("sy", sl[:, 0:512], wfT[128 * rc:128 * (rc + 1),
                                        512 * hf:512 * (hf + 1)],
                (bn,), (bn,))
            red(t_am1[:, 1 + 2 * rc + hf:2 + 2 * rc + hf], sl[:, 0:512],
                ALU.max, True, (bn,), ("am1",))
    red(t_am1[:, 0:1], t_am1[:, 1:1 + 2 * DCH], ALU.max, False, ("am1",),
        ("am1c",))
    trans(pb4[0:32, 0:128], t_am1[:, 0:32], t_idf[:, :], ("am1c", "idf"),
          ("pb4",))
    cp("ve", t_rowcc[:, :], pb4[0:1, 0:128], ("pb4",), ("rowcc",))
    red(t_s1x[:, 0:1], t_rowcc[:, :], ALU.max, False, ("rowcc",), ("s1x",))
    ts("ve", t_s1x[:, 1:2], t_s1x[:, 0:1], 1e-12, None, ALU.max, ALU.max,
       ("s1x",), ("s1x",))
    recip(t_s1x[:, 2:3], t_s1x[:, 1:2], ("s1x",), ("s1x",))
    ts("ve", t_s1x[:, 3:4], t_s1x[:, 2:3], 448.0, 448.0, ALU.mult, ALU.min,
       ("s1x",), ("s1x",))
    ts("ve", t_s1x[:, 4:5], t_s1x[:, 3:4], 0.5, None, ALU.mult, ALU.mult,
       ("s1x",), ("s1x",))
    recip(t_s1x[:, 5:6], t_s1x[:, 3:4], ("s1x",), ("s1x",))
    mm(pb4[:, 128:129], t_onesr[:, :], t_s1x[:, 4:5], True, True,
       ("onesr", "s1x"), ("pb4",))
    cp("ve", t_swh[:, :], pb4[:, 128:129], ("pb4",), ("swh",))
    for rc in range(DCH):
        for hf in range(2):
            sl, bn = r_wf[(2 * rc + hf) % 2], f"rwo2{(2 * rc + hf) % 2}"
            q8b, qn = r_f8[(2 * rc + hf) % 2], f"rf8{(2 * rc + hf) % 2}"
            dma("sy", sl[:, 0:512], wfT[128 * rc:128 * (rc + 1),
                                        512 * hf:512 * (hf + 1)],
                (bn,), (bn,))
            act(q8b[:, 0:512], sl[:, 0:512], AF.Copy, (bn, "swh"), (qn,),
                scale=t_swh[:, :])
            dma("gp", wf8sp[128 * rc:128 * (rc + 1),
                            512 * hf:512 * (hf + 1)], q8b[:, 0:512], (qn,),
                ("wf8",))

    if upto == "setup":
        for oc in range(DCH):
            dma("gp", out_d[128 * oc:128 * (oc + 1), 0:TILE],
                t_rstage[:, 0:TILE], ("rstage",), ("outdummy",))
        n = p.build(nc, sems, spacers, es)
        import sys
        print(f"[kernel] built {n} ops (setup)", file=sys.stderr)
        es.close()
        return nc, tap_outs, p

    def x_3d(hbm, t0):
        return hbm.rearrange("(c p) t -> p c t", p=128)[:, :, t0:t0 + TILE]

    def sb_3d(tns):
        return tns[:, :].rearrange("p (c t) -> p c t", c=DCH)

    def phaseA_tile(ti):
        t0 = ti * TILE
        dma("sy", sb_3d(t_x1), x_3d(x0, t0), ("x1",), ("x1",))
        pzero(pb3, "pb3")
        for c in range(DCH):
            cs = slice(TILE * c, TILE * (c + 1))
            sq, sqn = r_32[c % 2], f"r32{c % 2}"
            tt("ve", sq[:, :], t_x1[:, cs], t_x1[:, cs], ALU.mult, ("x1",),
               (sqn,))
            mm(pb3[0:1, 0:TILE], t_onespad[:, 0:1], t_x1[:, cs],
               False, False, ("onespad", "x1"), ("pb3",))
            mm(pb3[0:4, 0:TILE], t_onespad[:, 4:8], sq[:, :],
               False, c == DCH - 1, ("onespad", sqn), ("pb3",))
        cp("ve", t_rstage[0:5, :], pb3[0:5, 0:TILE], ("pb3",), ("rstage",))
        for ci in range(NCH):
            cur = ci % 2
            col = r_cols[cur]
            coln = f"rcols{cur}"
            rblk = r_rblk[cur]
            rbn = f"rrblk{cur}"
            ccs = slice(ci * C, (ci + 1) * C)
            trans(pb4[:, 0:128], t_rstage[:, ccs], t_idf[:, :],
                  ("rstage", "idf"), ("pb4",))
            cp("ve", col[:, 0:21], pb4[:, 0:21], ("pb4",), (coln,))
            ts("ve", col[:, C_M:C_M + 1], col[:, C_S1:C_S1 + 1], 1.0 / D,
               None, ALU.mult, ALU.mult, (coln,), (coln,))
            ts("ve", col[:, C_T1:C_T1 + 1], col[:, C_SX2:C_SX2 + 1], 1.0 / D,
               None, ALU.mult, ALU.mult, (coln,), (coln,))
            tt("ve", col[:, C_T2:C_T2 + 1], col[:, C_M:C_M + 1],
               col[:, C_M:C_M + 1], ALU.mult, (coln,), (coln,))
            tt("ve", col[:, C_V:C_V + 1], col[:, C_T1:C_T1 + 1],
               col[:, C_T2:C_T2 + 1], ALU.subtract, (coln,), (coln,))
            act(col[:, C_T3:C_T3 + 1], col[:, C_V:C_V + 1], AF.Ln,
                (coln, "eps5"), (coln,), bias=t_eps5[:, :])
            act(col[:, CS_S:CS_S + 1], col[:, C_T3:C_T3 + 1], AF.Exp,
                (coln,), (coln,), scale=-0.5)
            tt("ve", col[:, CS_ROWD:CS_ROWD + 1], col[:, C_M:C_M + 1],
               col[:, CS_S:CS_S + 1], ALU.mult, (coln,), (coln,))
            ts("ve", col[:, CS_ROWD:CS_ROWD + 1], col[:, CS_ROWD:CS_ROWD + 1],
               -1.0, None, ALU.mult, ALU.mult, (coln,), (coln,))
            trans(pb4[0:16, 128:256], col[:, SEL0:SEL0 + 16], t_idf[:, :],
                  (coln, "idf"), ("pb4",))
            cp("ve", rblk[0:16, :], pb4[0:16, 128:256], ("pb4",), (rbn,))
            selbc(pb0[:, 0:C], P_S, rblk, rbn, True, True, "pb0")
            selbc(pb1[:, 0:C], P_ROWD, rblk, rbn, True, True, "pb1")
            for c in range(DCH):
                cs2 = slice(TILE * c + ci * C, TILE * c + (ci + 1) * C)
                tt("ve", t_z[:, cs2], t_x1[:, cs2], pb0[:, 0:C], ALU.mult,
                   ("x1", "pb0"), ("z",))
                tt("ve", t_z[:, cs2], t_z[:, cs2], pb1[:, 0:C], ALU.add,
                   ("z", "pb1"), ("z",))
        red(t_am1[:, 17 + ti:18 + ti], t_z[:, :], ALU.max, True, ("z",),
            (f"amT{ti}",))
        dma("gp", x_3d(ln0sp, t0), sb_3d(t_z), ("z",), (f"ln0w{ti}",),
            force_sig=True)

    def amax_collective(which):
        tam, amn = (t_am1, "ax1") if which == 1 else (t_am2, "ax2")
        cci, cco = (cc1i, cc1o) if which == 1 else (cc2i, cc2o)
        sx, tf = (t_sxh, t_f1) if which == 1 else (t_sx2h, t_f2)
        rd = tuple(f"amT{ti}" if which == 1 else f"bmT{ti}"
                   for ti in range(NT))
        red(tam[:, 0:1], tam[:, 17:17 + NT] if which == 1
            else tam[:, 8:8 + NT], ALU.max, False,
            rd + (("am1",) if which == 1 else ("am2",)), (amn + "c",))
        trans(pb4[0:32, 0:128], tam[:, 0:32], t_idf[:, :],
              (amn + "c", "idf", "am1" if which == 1 else "am2"),
              ("pb4",))
        cp("ve", t_rowcc[:, :], pb4[0:1, 0:128], ("pb4",), (amn + "r",))
        red(t_s1x[:, 6:7], t_rowcc[:, :], ALU.max, False, (amn + "r",),
            (amn + "v",))
        mset("gp", t_gcc[:, :], 0.0, (amn + "g",))
        cp("gp", t_gcc[:, 0:1], t_s1x[:, 6:7], (amn + "v",), (amn + "g",))
        dma("gp", cci.ap()[:, :], t_gcc[:, :], (amn + "g",), (amn + "i",))
        p.emit("gp", lambda e, ci_=cci, co_=cco: e.collective_compute(
            "AllReduce", ALU.max, replica_groups=[list(range(8))],
            ins=[ci_.ap().opt()], outs=[co_.ap().opt()]),
            reads=(amn + "i",), writes=(amn + "o",), dma=True, force_sig=True,
            inc=1)
        dma("gp", t_gcc[:, :], cco.ap()[:, :], (amn + "o",), (amn + "gg",))
        ts("ve", t_s1x[:, 7:8], t_gcc[0:1, 0:1], 1e-12, None, ALU.max,
           ALU.max, (amn + "gg",), (amn + "a",))
        recip(t_s1x[:, 8:9], t_s1x[:, 7:8], (amn + "a",), (amn + "b",))
        ts("ve", t_s1x[:, 9:10], t_s1x[:, 8:9], 448.0, 448.0, ALU.mult,
           ALU.min, (amn + "b",), (amn + "sx",))
        ts("ve", t_s1x[:, 10:11], t_s1x[:, 9:10], 0.5, None, ALU.mult,
           ALU.mult, (amn + "sx",), (amn + "sxh",))
        recip(t_s1x[:, 11:12], t_s1x[:, 9:10], (amn + "sx",), (amn + "rx",))
        tt("ve", t_s1x[:, 12:13], t_s1x[:, 11:12], t_s1x[:, 5:6], ALU.mult,
           (amn + "rx", "s1x"), (amn + "ff",))
        ts("ve", t_s1x[:, 12:13], t_s1x[:, 12:13], 4.0, None, ALU.mult,
           ALU.mult, (amn + "ff",), (amn + "ff",))
        mm(pb4[:, 129:130], t_onesr[:, :], t_s1x[:, 10:11], True, True,
           ("onesr", amn + "sxh"), ("pb4",))
        cp("ve", sx[:, :], pb4[:, 129:130], ("pb4",), ("sxh%d" % which,))
        mm(pb4[:, 130:131], t_onesr[:, :], t_s1x[:, 12:13], True, True,
           ("onesr", amn + "ff"), ("pb4",))
        cp("ve", tf[:, :], pb4[:, 130:131], ("pb4",), ("fd%d" % which,))

    def phaseA2_tile(ti):
        t0 = ti * TILE
        dma("sy", sb_3d(t_z), x_3d(ln0sp, t0),
            ("z",) + tuple(f"ln0w{t}" for t in range(NT)),
            ("z",))
        act(t_q8[:, :], t_z[:, :], AF.Copy, ("z", "sxh1"), ("q8t",),
            scale=t_sxh[:, :])
        dma("gp", x_3d(q8sp, t0).bitcast(FP8), sb_3d(t_q8), ("q8t",),
            (f"q8w{ti}",), force_sig=True)

    for ti in range(NT):
        phaseA_tile(ti)
    if upto == "A1":
        for oc in range(DCH):
            dma("gp", out_d[128 * oc:128 * (oc + 1), 0:TILE],
                t_rstage[:, 0:TILE], ("rstage",), ("outdummy",))
        n = p.build(nc, sems, spacers, es)
        import sys
        print(f"[kernel] built {n} ops (A1)", file=sys.stderr)
        es.close()
        return nc, tap_outs, p
    if upto != "A2nc":
        amax_collective(1)
    else:
        mset("ve", t_sxh[:, :], 0.001, ("sxh1",))
    for ti in range(NT):
        phaseA2_tile(ti)
    if upto == "A2nc":
        for oc in range(DCH):
            dma("gp", out_d[128 * oc:128 * (oc + 1), 0:TILE],
                t_rstage[:, 0:TILE], ("rstage",), ("outdummy",))
        n = p.build(nc, sems, spacers, es)
        import sys
        print(f"[kernel] built {n} ops (A2nc)", file=sys.stderr)
        es.close()
        return nc, tap_outs, p

    if "sca" in taps:
        tap("sca", t_s1x[:, :], ("fd1", "sxh1", "s1x"))

    if upto == "A":
        for oc in range(DCH):
            dma("gp", out_d[128 * oc:128 * (oc + 1), 0:TILE], t_z[:, 0:TILE],
                ("z",), ("outdummy",))
        n = p.build(nc, sems, spacers, es)
        import sys
        print(f"[kernel] built {n} ops (upto=A)", file=sys.stderr)
        es.close()
        return nc, tap_outs, p

    def w_slab(dst, dstname, whbm, whname, oc, nic, wid):
        src = whbm.rearrange("(ic p) o -> p ic o", p=128)[
            :, :, 128 * oc:128 * (oc + 1)]
        dma("sy", dst[:, 0:nic * 128].rearrange("p (ic j) -> p ic j", ic=nic),
            src, (dstname, whname), (dstname,))

    def ffn_stage(ti, q8tile, q8name, resid_ring, residname, dq_col, dqn,
                  xout, xoutname, wsp, wspname):
        for oc in range(DCH):
            sl, sn = r_f8[oc % 3], f"rf8{oc % 3}"
            w_slab(sl, sn, wsp, wspname, oc, DCH, D)
            pb, pn = (pb0, "pb0") if oc % 2 == 0 else (pb1, "pb1")
            for ic in range(DCH):
                mm(pb[:, 0:TILE], sl[:, 128 * ic:128 * (ic + 1)],
                   q8tile[:, TILE * ic:TILE * (ic + 1)], ic == 0,
                   ic == DCH - 1, (sn, q8name), (pn,))
            ff, fn_ = r_ff[oc % 2], f"rff{oc % 2}"
            ts("ve", ff[:, :], pb[:, 0:TILE], dq_col[:, :],
               t_fbc[:, oc:oc + 1], ALU.mult, ALU.add,
               (pn, dqn, "fbc"), (fn_,))
            rx = resid_ring(oc)
            stt("ve", xout[:, TILE * oc:TILE * (oc + 1)], ff[:, :], 0.5,
                rx[0], ALU.mult, ALU.add, (fn_, rx[1]),
                (xoutname,))

    def stats_group(rhs_t, rhsname, kind):
        for c in range(DCH):
            cs = slice(TILE * c, TILE * (c + 1))
            xb, xbn = r_ff[c % 2], f"rff{c % 2}"
            sq, sqn = r_sq[c % 2], f"rsq{c % 2}"
            cp("ve", xb[:, :], rhs_t[:, cs], (rhsname,), (xbn,))
            tt("ve", sq[:, :], xb[:, :], xb[:, :], ALU.mult, (xbn,), (sqn,))
            if kind == "x1":
                mm(pb3[0:5, 0:TILE], t_statq[:, 10 * c:10 * c + 5], xb[:, :],
                   False, False, ("statq", xbn), ("pb3",))
                mm(pb3[0:5, 0:TILE], t_statq[:, 10 * c + 5:10 * c + 10],
                   sq[:, :], False, False, ("statq", sqn), ("pb3",))
            elif kind == "z":
                mm(pb3[0:18, 0:TILE], t_statz[:, 0:18], xb[:, :],
                   False, False, ("statz", xbn), ("pb3",))
                mm(pb3[0:19, 0:TILE], t_statz[:, 18:37], sq[:, :],
                   False, False, ("statz", sqn), ("pb3",))
            else:
                mm(pb3[0:20, 0:TILE], t_staty[:, 0:20], xb[:, :],
                   False, False, ("staty", xbn), ("pb3",))
                mm(pb3[0:21, 0:TILE], t_staty[:, 20:41], sq[:, :],
                   False, False, ("staty", sqn), ("pb3",))

    def stats_group_bf(rhs_t, rhsname, kind):
        for c in range(DCH):
            cs = slice(TILE * c, TILE * (c + 1))
            sq, sqn = r_sq[c % 2], f"rsq{c % 2}"
            tt("ve", sq[:, :], rhs_t[:, cs], rhs_t[:, cs], ALU.mult,
               (rhsname,), (sqn,))
            mm(pb3[0:18, 0:TILE], t_statz[:, 0:18], rhs_t[:, cs],
               False, False, ("statz", rhsname), ("pb3",))
            mm(pb3[0:19, 0:TILE], t_statz[:, 18:37], sq[:, :],
               False, False, ("statz", sqn), ("pb3",))

    def sweep1_tile(ti):
        t0 = ti * TILE
        TL = f"t{ti}_"
        dma("sy", sb_3d(t_q8), x_3d(q8sp, t0).bitcast(FP8),
            ("q8t",) + tuple(f"q8w{t}" for t in range(NT)), ("q8t",))
        def resid_x0(oc):
            rx, rxn = r_x0[oc % 2], f"rx0{oc % 2}"
            dma("sy", rx[:, :], x0[128 * oc:128 * (oc + 1), t0:t0 + TILE],
                (rxn,), (rxn,))
            return rx[:, :], rxn
        ffn_stage(ti, t_q8, "q8t", resid_x0, None, t_f1, "fd1", t_x1, "x1",
                  wf8sp, "wf8")
        if upto == "S1f":
            return
        if "x1" in tap_outs:
            dma("gp", x_3d(tap_outs["x1"], t0), sb_3d(t_x1), ("x1",),
                ("x1tap",), force_sig=True)
        pzero(pb3, "pb3")
        stats_group(t_x1, "x1", "x1")
        cp("ve", t_rstage[0:5, :], pb3[0:5, 0:TILE], ("pb3",), ("rstage",))
        hrblks = []
        for ci in range(NCH):
            cur = ci % 2
            col, coln = r_cols[cur], f"rcols{cur}"
            rblk, rbn = r_rblk[cur], f"rrblk{cur}"
            ccs = slice(ci * C, (ci + 1) * C)
            trans(pb4[:, 0:128], t_rstage[:, ccs], t_idf[:, :],
                  ("rstage", "idf"), ("pb4",))
            cp("ve", col[:, 0:21], pb4[:, 0:21], ("pb4",), (coln,))
            ts("ve", col[:, C_M:C_M + 1], col[:, C_S1:C_S1 + 1], 1.0 / D,
               None, ALU.mult, ALU.mult, (coln,), (coln,))
            ts("ve", col[:, C_T1:C_T1 + 1], col[:, C_SX2:C_SX2 + 1], 1.0 / D,
               None, ALU.mult, ALU.mult, (coln,), (coln,))
            tt("ve", col[:, C_T2:C_T2 + 1], col[:, C_M:C_M + 1],
               col[:, C_M:C_M + 1], ALU.mult, (coln,), (coln,))
            tt("ve", col[:, C_V:C_V + 1], col[:, C_T1:C_T1 + 1],
               col[:, C_T2:C_T2 + 1], ALU.subtract, (coln,), (coln,))
            act(col[:, C_T3:C_T3 + 1], col[:, C_V:C_V + 1], AF.Ln,
                (coln, "eps5"), (coln,), bias=t_eps5[:, :])
            act(col[:, CS_S:CS_S + 1], col[:, C_T3:C_T3 + 1], AF.Exp,
                (coln,), (coln,), scale=-0.5)
            tt("ve", col[:, C_T1:C_T1 + 1], col[:, CS_S:CS_S + 1],
               col[:, CS_S:CS_S + 1], ALU.mult, (coln,), (coln,))
            tt("ve", col[:, C_T2:C_T2 + 1], col[:, C_V:C_V + 1],
               col[:, C_T1:C_T1 + 1], ALU.mult, (coln,), (coln,))
            act(col[:, C_T3:C_T3 + 1], col[:, C_T2:C_T2 + 1], AF.Ln,
                (coln, "eps5"), (coln,), bias=t_eps5[:, :])
            act(col[:, C_SP:C_SP + 1], col[:, C_T3:C_T3 + 1], AF.Exp,
                (coln,), (coln,), scale=-0.5)
            ts("ve", col[:, C_T1:C_T1 + 1], col[:, C_SG2X:C_SG2X + 1],
               1.0 / D, None, ALU.mult, ALU.mult, (coln,), (coln,))
            tt("ve", col[:, C_T2:C_T2 + 1], col[:, C_M:C_M + 1],
               col[:, C_T1:C_T1 + 1], ALU.mult, (coln,), (coln,))
            ts("ve", col[:, C_T1:C_T1 + 1], col[:, C_SG2X2:C_SG2X2 + 1],
               1.0 / D, None, ALU.mult, ALU.mult, (coln,), (coln,))
            stt("ve", col[:, C_T2:C_T2 + 1], col[:, C_T2:C_T2 + 1], -2.0,
                col[:, C_T1:C_T1 + 1], ALU.mult, ALU.add, (coln,), (coln,))
            tt("ve", col[:, C_T1:C_T1 + 1], col[:, C_M:C_M + 1],
               col[:, C_M:C_M + 1], ALU.mult, (coln,), (coln,))
            stt("ve", col[:, C_T2:C_T2 + 1], col[:, C_T1:C_T1 + 1],
                t_cstb[:, 0:1], col[:, C_T2:C_T2 + 1], ALU.mult, ALU.add,
                (coln, "cstb"), (coln,))
            tt("ve", col[:, C_T1:C_T1 + 1], col[:, CS_S:CS_S + 1],
               col[:, CS_S:CS_S + 1], ALU.mult, (coln,), (coln,))
            tt("ve", col[:, C_T2:C_T2 + 1], col[:, C_T2:C_T2 + 1],
               col[:, C_T1:C_T1 + 1], ALU.mult, (coln,), (coln,))
            ts("ve", col[:, C_T1:C_T1 + 1], col[:, C_SGBX:C_SGBX + 1],
               1.0 / D, None, ALU.mult, ALU.mult, (coln,), (coln,))
            stt("ve", col[:, C_T3:C_T3 + 1], col[:, C_M:C_M + 1],
                t_cstb[:, 1:2], col[:, C_T1:C_T1 + 1], ALU.mult,
                ALU.subtract, (coln, "cstb"), (coln,))
            tt("ve", col[:, C_T3:C_T3 + 1], col[:, C_T3:C_T3 + 1],
               col[:, CS_S:CS_S + 1], ALU.mult, (coln,), (coln,))
            tt("ve", col[:, C_T1:C_T1 + 1], col[:, C_SP:C_SP + 1],
               col[:, C_SP:C_SP + 1], ALU.mult, (coln,), (coln,))
            tt("ve", col[:, C_T2:C_T2 + 1], col[:, C_T2:C_T2 + 1],
               col[:, C_T1:C_T1 + 1], ALU.mult, (coln,), (coln,))
            tt("ve", col[:, C_T3:C_T3 + 1], col[:, C_T3:C_T3 + 1],
               col[:, C_SP:C_SP + 1], ALU.mult, (coln,), (coln,))
            stt("ve", col[:, C_T2:C_T2 + 1], col[:, C_T3:C_T3 + 1], -2.0,
                col[:, C_T2:C_T2 + 1], ALU.mult, ALU.add, (coln,), (coln,))
            ts("ve", col[:, C_T2:C_T2 + 1], col[:, C_T2:C_T2 + 1], 1.0,
               t_cstb[:, 2:3], ALU.mult, ALU.add, (coln, "cstb"), (coln,))
            act(col[:, C_T3:C_T3 + 1], col[:, C_T2:C_T2 + 1], AF.Ln,
                (coln, "eps6"), (coln,), bias=t_eps6[:, :])
            act(col[:, C_R:C_R + 1], col[:, C_T3:C_T3 + 1], AF.Exp,
                (coln,), (coln,), scale=-0.5)
            cp("ve", col[:, CS_RC:CS_RC + 1], col[:, C_R:C_R + 1], (coln,),
               (coln,))
            tt("ve", col[:, C_T1:C_T1 + 1], col[:, CS_S:CS_S + 1],
               col[:, C_SP:C_SP + 1], ALU.mult, (coln,), (coln,))
            tt("ve", col[:, CS_ROWA:CS_ROWA + 1], col[:, C_T1:C_T1 + 1],
               col[:, C_R:C_R + 1], ALU.mult, (coln,), (coln,))
            tt("ve", col[:, CS_ROWB:CS_ROWB + 1], col[:, C_M:C_M + 1],
               col[:, CS_ROWA:CS_ROWA + 1], ALU.mult, (coln,), (coln,))
            ts("ve", col[:, CS_ROWB:CS_ROWB + 1], col[:, CS_ROWB:CS_ROWB + 1],
               -1.0, None, ALU.mult, ALU.mult, (coln,), (coln,))
            tt("ve", col[:, CS_ROWD:CS_ROWD + 1], col[:, C_M:C_M + 1],
               col[:, CS_S:CS_S + 1], ALU.mult, (coln,), (coln,))
            ts("ve", col[:, CS_ROWD:CS_ROWD + 1], col[:, CS_ROWD:CS_ROWD + 1],
               -1.0, None, ALU.mult, ALU.mult, (coln,), (coln,))
            trans(pb4[0:16, 128:256], col[:, SEL0:SEL0 + 16], t_idf[:, :],
                  (coln, "idf"), ("pb4",))
            cp("ve", rblk[0:16, :], pb4[0:16, 128:256], ("pb4",), (rbn,))
            cp("ve", t_hrb[0:16, 128 * ci:128 * (ci + 1)], rblk[0:16, :],
               (rbn,), ("hrb",))
            selbc(pb0[:, 0:C], P_ROWA, rblk, rbn, True, True, "pb0")
            for dc in range(DCH):
                mm(pb1[:, 0:C], t_selP[:, 128 * dc:128 * (dc + 1)],
                   rblk[0:16, :], True, True, ("selP", rbn), ("pb1",))
                cs2 = slice(TILE * dc + ci * C, TILE * dc + (ci + 1) * C)
                sqq, sqqn = r_sq[dc % 2], f"rsq{dc % 2}"
                tt("ve", sqq[:, 0:C], t_x1[:, cs2], pb0[:, 0:C], ALU.mult,
                   ("x1", "pb0"), (sqqn,))
                ts("ve", sqq[:, 0:C], sqq[:, 0:C], t_ucol[:, dc:dc + 1],
                   None, ALU.mult, ALU.mult, (sqqn, "ucol"), (sqqn,))
                tt("ve", t_h[:, cs2], sqq[:, 0:C], pb1[:, 0:C], ALU.add,
                   (sqqn, "pb1"), ("h",))
        if "h" in tap_outs:
            dma("gp", x_3d(tap_outs["h"], t0), sb_3d(t_h), ("h",),
                ("htap",), force_sig=True)

        if upto == "S1h":
            return
        def proj(dst, dstname, whbm, whname, dgt, dgname, convt, convname,
                 nout, carry_base, rebuild_v=False, ti=ti):
            for oc in range(nout):
                sl, sn = r_wsl[oc % 4], f"rwsl{oc % 4}"
                w_slab(sl, sn, whbm, whname, oc, DCH, nout * 128)
                pb, pn = (pb0, "pb0") if oc % 2 == 0 else (pb1, "pb1")
                for ic in range(DCH):
                    mm(pb[:, 0:TILE], sl[:, 128 * ic:128 * (ic + 1)],
                       t_h[:, TILE * ic:TILE * (ic + 1)], ic == 0,
                       ic == DCH - 1, (sn, "h"), (pn,))
                pre, pren = r_pre[oc % 2], f"rpre{oc % 2}"
                cb = 64 * (carry_base + oc)
                cp("ve", pre[:, 0:64], t_carry[:, cb:cb + 64], ("carry",),
                   (pren,))
                cp("ve", pre[:, 64:64 + TILE], pb[:, 0:TILE], (pn,), (pren,))
                cp("ve", t_carry[:, cb:cb + 64], pre[:, TILE:TILE + 64],
                   (pren,), ("carry",))
                for tp in range(4):
                    if rebuild_v:
                        dgt_, dgn = r_dgv[tp % 2], f"rdgv{tp % 2}"
                        ts("ve", dgt_[:, :], t_idb[:, :],
                           convt[:, 4 * oc + tp:4 * oc + tp + 1], None,
                           ALU.mult, ALU.mult, ("idb", convname), (dgn,))
                        dgap = dgt_[:, :]
                    else:
                        j = 4 * oc + tp
                        dgap, dgn = dgt[:, 128 * j:128 * (j + 1)], dgname
                    mm(pb2[:, 0:TILE], dgap,
                       pre[:, 61 + tp:61 + tp + TILE], tp == 0, tp == 3,
                       (dgn, pren), ("pb2",))
                act(dst[:, TILE * oc:TILE * (oc + 1)], pb2[:, 0:TILE],
                    AF.Silu, ("pb2",), (dstname,))

        if ti == 1 and "dCar" in tap_outs:
            cp("ve", t_sbc[:, 0:96], t_carry[:, :], ("carry",), ("sbc",))
            dma("gp", tap_outs["dCar"], t_sbc[:, 0:96], ("sbc",),
                ("dCar_tap",))
        proj(t_q, "q", wq_b, "wh0", None, "dgq", t_convq, "convq", DCH,
             0, rebuild_v=True)
        proj(t_k, "k", wk_b, "wh1", None, "dgk", t_convk, "convk", DCH,
             DCH, rebuild_v=True)
        proj(t_v, "v", wv_b, "wh2", None, "dgv", t_convv, "convv", VCH,
             2 * DCH, rebuild_v=True)
        if "q" in tap_outs:
            dma("gp", tap_outs["q"].rearrange("(c p) t -> p c t", p=128)
                [:, :, t0:t0 + TILE],
                t_q[:, :].rearrange("p (c t) -> p c t", c=DCH), ("q",),
                ("qtap",), force_sig=True)
        if "k" in tap_outs:
            dma("gp", tap_outs["k"].rearrange("(c p) t -> p c t", p=128)
                [:, :, t0:t0 + TILE],
                t_k[:, :].rearrange("p (c t) -> p c t", c=DCH), ("k",),
                ("ktap",), force_sig=True)
        if "v" in tap_outs:
            dma("gp", tap_outs["v"].rearrange("(c p) t -> p c t", p=128)
                [:, :, t0:t0 + TILE],
                t_v[:, :].rearrange("p (c t) -> p c t", c=VCH), ("v",),
                ("vtap",), force_sig=True)

        if upto == "S1q":
            return
        for c in range(DCH):
            mm(pb3[0:68, 0:TILE], t_wabp[:, 68 * c:68 * (c + 1)],
               t_h[:, TILE * c:TILE * (c + 1)], False, False,
               ("wabp", "h"), ("pb3",))
        for c in range(DCH):
            cs = slice(TILE * c, TILE * (c + 1))
            sq, sqn = r_sq[c % 2], f"rsq{c % 2}"
            tt("ve", sq[:, :], t_q[:, cs], t_q[:, cs], ALU.mult, ("q",),
               (sqn,))
            mm(pb3[0:9, 0:TILE], t_hotq[:, 9 * c:9 * (c + 1)], sq[:, :],
               False, False, ("hotq", sqn), ("pb3",))
        for c in range(DCH):
            cs = slice(TILE * c, TILE * (c + 1))
            sq, sqn = r_sq[c % 2], f"rsq{c % 2}"
            tt("ve", sq[:, :], t_k[:, cs], t_k[:, cs], ALU.mult, ("k",),
               (sqn,))
            mm(pb3[0:13, 0:TILE], t_hotk[:, 13 * c:13 * (c + 1)], sq[:, :],
               False, False, ("hotk", sqn), ("pb3",))
        cp("ve", t_rstage[0:32, :], pb3[0:32, 0:TILE], ("pb3",), ("rstage",))
        cp("ve", t_rstage[32:64, :], pb3[32:64, 0:TILE], ("pb3",),
           ("rstage",))
        cp("ve", t_rstage[64:96, :], pb3[64:96, 0:TILE], ("pb3",),
           ("rstage",))
        if ti == 0 and "dRst0" in tap_outs:
            dma("gp", tap_outs["dRst0"], t_rstage[:, :], ("rstage",),
                ("dRst0_tap",))
        act(t_rstg2[32:36, :], t_rstage[32:36, :], AF.Exp,
            ("rstage", "dtbp"), ("rstg2",), bias=t_dtbp[32:36, :])
        act(t_rstage[32:36, :], t_rstg2[32:36, :], AF.Ln,
            ("rstg2", "one1"), ("rstage",), bias=t_one1[32:36, :])
        ts("ve", t_rstg2[32:36, :], t_rstage[32:36, :],
           t_nega[32:36, :], None, ALU.mult, ALU.mult, ("rstage", "nega"),
           ("rstg2",))
        act(t_rstg2[64:68, :], t_rstage[64:68, :], AF.Exp, ("rstage",),
            ("rstg2",), scale=-1.0)
        ts("ve", t_rstg2[64:68, :], t_rstg2[64:68, :], 1.0, None, ALU.add,
           ALU.add, ("rstg2",), ("rstg2",))
        recip(t_rstage[64:68, :], t_rstg2[64:68, :], ("rstg2",), ("rstage",))
        for ci in range(NCH):
            ccs = slice(ci * C, (ci + 1) * C)
            scan(t_rstage[32:36, ccs], t_ones36[32:36, ccs],
                 t_rstg2[32:36, ccs], ("ones36", "rstg2"), ("rstage",))

        if ti == 0 and "dRst1" in tap_outs:
            dma("gp", tap_outs["dRst1"], t_rstage[:, :], ("rstage",),
                ("dRst1_tap",))
        if ti == 0 and "dRs2" in tap_outs:
            dma("gp", tap_outs["dRs2"], t_rstg2[:, :], ("rstg2",),
                ("dRs2_tap",))
        if upto == "S1g":
            return
        for ci in range(NCH):
            delta_chunk(ti, ci)
        if upto == "S1d":
            return

        if "o" in tap_outs:
            dma("gp", tap_outs["o"].rearrange("(c p) t -> p c t", p=128)
                [:, :, t0:t0 + TILE],
                t_o[:, :].rearrange("p (c t) -> p c t", c=VCH), ("o",),
                ("otap",), force_sig=True)

        o_gate_enc_z(ti)
        if upto == "S1o":
            return
        w2_stage(ti)

    def delta_chunk(ti, ci):
        cur = ci % 2
        col, coln = r_cols[cur], f"rcols{cur}"
        rblk, rbn = r_rblk[cur], f"rrblk{cur}"
        ccs = slice(ci * C, (ci + 1) * C)
        trans(pb4[:, 0:128], t_rstage[:, ccs], t_idf[:, :],
              ("rstage", "idf"), ("pb4",))
        cp("ve", col[:, 0:21], pb4[:, 0:21], ("pb4",), (coln,))
        cp("ve", col[:, C_C:C_C + 4], pb4[:, 32:36], ("pb4",), (coln,))
        cp("ve", col[:, C_BETA:C_BETA + 4], pb4[:, 64:68], ("pb4",), (coln,))
        act(col[:, C_T1:C_T1 + 4], col[:, C_SSK:C_SSK + 4], AF.Ln,
            (coln, "eps6"), (coln,), bias=t_eps6[:, :])
        ts("ve", col[:, C_LNRK:C_LNRK + 4], col[:, C_T1:C_T1 + 4], -0.5,
           None, ALU.mult, ALU.mult, (coln,), (coln,))
        act(col[:, C_T1:C_T1 + 4], col[:, C_SSQ:C_SSQ + 4], AF.Ln,
            (coln, "eps6"), (coln,), bias=t_eps6[:, :])
        ts("ve", col[:, C_LNRQ:C_LNRQ + 4], col[:, C_T1:C_T1 + 4], -0.5,
           -0.5 * float(np.log(Kd)), ALU.mult, ALU.add, (coln,), (coln,))
        act(col[:, C_LNB:C_LNB + 4], col[:, C_BETA:C_BETA + 4], AF.Ln,
            (coln,), (coln,))
        tt("ve", col[:, C_LA:C_LA + 4], col[:, C_LNRK:C_LNRK + 4],
           col[:, C_C:C_C + 4], ALU.subtract, (coln,), (coln,))
        tt("ve", col[:, C_T1:C_T1 + 4], col[:, C_C:C_C + 4],
           col[:, C_LNRK:C_LNRK + 4], ALU.add, (coln,), (coln,))
        tt("ve", col[:, CS_B:CS_B + 4], col[:, C_T1:C_T1 + 4],
           col[:, C_LNB:C_LNB + 4], ALU.add, (coln,), (coln,))
        tt("ve", col[:, CS_E:CS_E + 4], col[:, C_C:C_C + 4],
           col[:, C_LNRQ:C_LNRQ + 4], ALU.add, (coln,), (coln,))
        act(col[:, C_EXPB:C_EXPB + 4], col[:, CS_B:CS_B + 4], AF.Exp,
            (coln,), (coln,))
        mm(pb4[0:1, 200:204], t_e127[:, :], col[:, C_C:C_C + 4], True, True,
           ("e127", coln), ("pb4",))
        cp("ve", t_cc4[:, :], pb4[0:1, 200:204], ("pb4",), ("cc4",))
        for hh in range(H):
            mm(pb4[:, 204 + hh:205 + hh], t_onesr[:, :],
               t_cc4[0:1, hh:hh + 1], True, True, ("onesr", "cc4"), ("pb4",))
        cp("ve", t_cCb[:, :], pb4[:, 204:208], ("pb4",), ("cCb",))
        act(t_gCc[:, :], t_cCb[:, :], AF.Exp, ("cCb",), ("gCc",))
        for hh in range(H):
            act(col[:, C_EXPD + hh:C_EXPD + hh + 1],
                col[:, C_LA + hh:C_LA + hh + 1], AF.Exp, (coln, "cCb"),
                (coln,), bias=t_cCb[:, hh:hh + 1])
        trans(pb4[0:16, 128:256], col[:, SEL0:SEL0 + 16], t_idf[:, :],
              (coln, "idf"), ("pb4",))
        cp("ve", rblk[0:16, :], pb4[0:16, 128:256], ("pb4",), (rbn,))

        for hh in range(H):
            for j2 in range(2):
                ks = slice(TILE * (2 * hh + j2) + ci * C,
                           TILE * (2 * hh + j2) + (ci + 1) * C)
                mm(pb0[:, 128 * hh:128 * (hh + 1)], t_k[:, ks], t_k[:, ks],
                   j2 == 0, j2 == 1, ("k",), ("pb0",))
                mm(pb1[:, 128 * hh:128 * (hh + 1)], t_k[:, ks], t_q[:, ks],
                   j2 == 0, j2 == 1, ("k", "q"), ("pb1",))
        mm(pb3[:, 0:512], t_negbig[:, :], t_maskY[:, :], True, True,
           ("negbig", "maskY"), ("pb3",))
        for hh in range(H):
            mm(pb3[:, 128 * hh:128 * (hh + 1)],
               t_selo[:, 128 * (P_B + hh):128 * (P_B + hh + 1)],
               rblk[0:16, :], False, True, ("selo", rbn), ("pb3",))
        for hh in range(H):
            act(t_Eh[:, 128 * hh:128 * (hh + 1)],
                pb3[:, 128 * hh:128 * (hh + 1)], AF.Exp, ("pb3", coln),
                ("Eh",), bias=col[:, C_LA + hh:C_LA + hh + 1])
        if ti == 0 and ci == 0 and "dEhY" in tap_outs:
            cp("ve", t_sbc[:, 0:512], t_Eh[:, :], ("Eh",), ("sbc",))
            dma("gp", tap_outs["dEhY"], t_sbc[:, 0:512], ("sbc",),
                ("dEhY_tap",))
            cp("ve", t_sbc[:, 0:512], pb3[:, 0:512], ("pb3",), ("sbc",))
            dma("gp", tap_outs["dEhM"], t_sbc[:, 0:512], ("sbc",),
                ("dEhM_tap",))
        stt("ve", r_Y[0][:, :], t_Eh[:, :], -1.0, pb0[:, 0:512], ALU.mult,
            ALU.mult, ("Eh", "pb0"), ("rY0",))
        mm(pb3[:, 0:512], t_negbig[:, :], t_maskM[:, :], True, True,
           ("negbig", "maskM"), ("pb3",))
        for hh in range(H):
            mm(pb3[:, 128 * hh:128 * (hh + 1)],
               t_selo[:, 128 * (P_E + hh):128 * (P_E + hh + 1)],
               rblk[0:16, :], False, True, ("selo", rbn), ("pb3",))
        for hh in range(H):
            act(t_Eh[:, 128 * hh:128 * (hh + 1)],
                pb3[:, 128 * hh:128 * (hh + 1)], AF.Exp, ("pb3", coln),
                ("Eh",), bias=col[:, C_LA + hh:C_LA + hh + 1])
        tt("ve", t_MT[:, :], t_Eh[:, :], pb1[:, 0:512], ALU.mult,
           ("Eh", "pb1"), ("MT",))
        for hh in range(H):
            trans(pbt[:, 128 * hh:128 * (hh + 1)],
                  r_Y[0][:, 128 * hh:128 * (hh + 1)], t_idb[:, :],
                  ("rY0", "idb"), ("pbt",))
        cp("ve", r_X[0][:, :], pbt[:, 0:512], ("pbt",), ("rX0",))
        pzero(pb4, "pb4")
        for hh in range(H):
            mm(pb4[:, 128 * hh:128 * (hh + 1)], t_idb[:, :], t_idb[:, :],
               False, False, ("idb",), ("pb4",))
            mm(pb4[:, 128 * hh:128 * (hh + 1)],
               r_X[0][:, 128 * hh:128 * (hh + 1)], t_idb[:, :], False, True,
               ("rX0", "idb"), ("pb4",))
        for j in range(6):
            xj, xjn = r_X[j % 2], f"rX{j % 2}"
            yj, yjn = r_Y[j % 2], f"rY{j % 2}"
            xn, xnn = r_X[(j + 1) % 2], f"rX{(j + 1) % 2}"
            yn, ynn = r_Y[(j + 1) % 2], f"rY{(j + 1) % 2}"
            for hh in range(H):
                mm(pb2[:, 128 * hh:128 * (hh + 1)],
                   yj[:, 128 * hh:128 * (hh + 1)],
                   xj[:, 128 * hh:128 * (hh + 1)], True, True, (yjn, xjn),
                   ("pb2",))
            cp("ve", xn[:, :], pb2[:, 0:512], ("pb2",), (xnn,))
            if j < 5:
                for hh in range(H):
                    mm(pb2[:, 128 * hh:128 * (hh + 1)],
                       xj[:, 128 * hh:128 * (hh + 1)],
                       yj[:, 128 * hh:128 * (hh + 1)], True, True,
                       (xjn, yjn), ("pb2",))
                cp("ve", yn[:, :], pb2[:, 0:512], ("pb2",), (ynn,))
            zs, zsn = r_Zs[j % 2], f"rZs{j % 2}"
            cp("ve", zs[:, :], pb4[:, 0:512], ("pb4",), (zsn,))
            for hh in range(H):
                mm(pb4[:, 128 * hh:128 * (hh + 1)],
                   xn[:, 128 * hh:128 * (hh + 1)],
                   zs[:, 128 * hh:128 * (hh + 1)], False, True, (xnn, zsn),
                   ("pb4",))
        cp("ve", t_TT[:, :], pb4[:, 0:512], ("pb4",), ("TT",))

        if ti == 0 and ci == 0:
            def dtap(nm, srcap, rdname, ncols):
                if nm in tap_outs:
                    for c0 in range(0, ncols, 512):
                        w = min(512, ncols - c0)
                        cp("ve", t_sbc[:, 0:w], srcap[:, c0:c0 + w],
                           (rdname,), ("sbc",))
                        dma("gp", tap_outs[nm][:, c0:c0 + w], t_sbc[:, 0:w],
                            ("sbc",), (nm + "_tap",))
            dtap("dY0", r_Y[0], "rY0", 512)
            dtap("dX0", r_X[0], "rX0", 512)
            dtap("dTT", t_TT, "TT", 512)
            dtap("dMT", t_MT, "MT", 512)
            if "dRblk" in tap_outs:
                dma("gp", tap_outs["dRblk"], rblk[0:16, :], (rbn,),
                    ("dRblk_tap",))
            if "dCols" in tap_outs:
                dma("gp", tap_outs["dCols"], col[:, :], (coln,),
                    ("dCols_tap",))
        for hh in range(H):
            hc = hh % 2
            rp, rpn = r_Rp[hc], f"rRp{hc}"
            dl, dln = r_Dl[hc], f"rDl{hc}"
            for jv in range(4):
                vs = slice(TILE * (4 * hh + jv) + ci * C,
                           TILE * (4 * hh + jv) + (ci + 1) * C)
                trans(pbt[:, 128 * jv:128 * (jv + 1)], t_v[:, vs],
                      t_idb[:, :], ("v", "idb"), ("pbt",))
            ts("ve", rp[:, 0:512], pbt[:, 0:512],
               col[:, C_BETA + hh:C_BETA + hh + 1], None, ALU.mult, ALU.mult,
               ("pbt", coln), (rpn,))
            for j2 in range(2):
                ks = slice(TILE * (2 * hh + j2) + ci * C,
                           TILE * (2 * hh + j2) + (ci + 1) * C)
                trans(pbt[:, 512 + 128 * j2:512 + 128 * (j2 + 1)],
                      t_k[:, ks], t_idb[:, :], ("k", "idb"), ("pbt",))
            ts("ve", rp[:, 512:768], pbt[:, 512:768],
               col[:, C_EXPB + hh:C_EXPB + hh + 1], None, ALU.mult, ALU.mult,
               ("pbt", coln), (rpn,))
            ts("ve", t_Kh[:, Kd * hh:Kd * (hh + 1)], pbt[:, 512:768],
               col[:, C_EXPD + hh:C_EXPD + hh + 1], None, ALU.mult, ALU.mult,
               ("pbt", coln), ("Kh",))
            mm(pdel[:, 0:512], t_TT[:, 128 * hh:128 * (hh + 1)],
               rp[:, 0:512], True, True, ("TT", rpn), ("pdel",))
            mm(pdel[:, 512:768], t_TT[:, 128 * hh:128 * (hh + 1)],
               rp[:, 512:768], True, True, ("TT", rpn), ("pdel",))
            cp("ve", dl[:, :], pdel[:, 0:768], ("pdel",), (dln,))
            mm(pb3[:, 0:128], t_selo[:, 128 * (P_E + hh):128 * (P_E + hh + 1)],
               rblk[0:16, :], True, True, ("selo", rbn), ("pb3",))
            act(t_eqb[:, :], pb3[:, 0:128], AF.Exp, ("pb3",), ("eqb",))
            for j2 in range(2):
                ks = slice(TILE * (2 * hh + j2) + ci * C,
                           TILE * (2 * hh + j2) + (ci + 1) * C)
                tt("ve", r_Qt[j2][:, :], t_q[:, ks], t_eqb[:, :], ALU.mult,
                   ("q", "eqb"), (f"rQt{j2}",))
            for jv in range(4):
                for j2 in range(2):
                    mm(pb2[:, 128 * jv:128 * (jv + 1)],
                       t_S[:, 1024 * hh + 512 * j2 + 128 * jv:
                           1024 * hh + 512 * j2 + 128 * (jv + 1)],
                       r_Qt[j2][:, :], j2 == 0, False,
                       (f"S{hh}", f"rQt{j2}"), ("pb2",))
                mm(pb2[:, 128 * jv:128 * (jv + 1)],
                   dl[:, 128 * jv:128 * (jv + 1)],
                   t_MT[:, 128 * hh:128 * (hh + 1)], False, True,
                   (dln, "MT"), ("pb2",))
            p.emit("ve", lambda e, hh=hh, ci=ci: e.tensor_copy(
                out=t_o[:, :].rearrange("p (c t) -> p c t", c=VCH)
                [:, 4 * hh:4 * hh + 4, ci * C:(ci + 1) * C],
                in_=pb2[:, 0:512].rearrange("p (c t) -> p c t", c=4)),
                reads=("pb2",), writes=("o",))
            for m_ in range(2):
                mm(pdel[:, 256 * m_:256 * (m_ + 1)],
                   dl[:, 512 + 128 * m_:512 + 128 * (m_ + 1)],
                   t_Kh[:, Kd * hh:Kd * (hh + 1)], True, True,
                   (dln, "Kh"), ("pdel",))
            ts("ve", t_g0[:, 0:128], t_idb[:, :], t_gCc[:, hh:hh + 1], None,
               ALU.mult, ALU.mult, ("idb", "gCc"), ("g0",))
            ts("ve", t_g0[:, 384:512], t_idb[:, :], t_gCc[:, hh:hh + 1],
               None, ALU.mult, ALU.mult, ("idb", "gCc"), ("g0",))
            wp, wpn = r_Wp[hc], f"rWp{hc}"
            tt("ve", wp[:, 0:256], t_g0[:, 0:256], pdel[:, 0:256],
               ALU.subtract, ("g0", "pdel"), (wpn,))
            tt("ve", wp[:, 256:512], t_g0[:, 256:512], pdel[:, 256:512],
               ALU.subtract, ("g0", "pdel"), (wpn,))
            for a_ in range(2):
                for b_ in range(2):
                    mm(pdel[:, 512 * a_:512 * (a_ + 1)],
                       wp[:, 256 * b_ + 128 * a_:256 * b_ + 128 * (a_ + 1)],
                       t_S[:, 1024 * hh + 512 * b_:1024 * hh + 512 * (b_ + 1)],
                       b_ == 0, False, (wpn, f"S{hh}"), ("pdel",))
                mm(pdel[:, 512 * a_:512 * (a_ + 1)],
                   t_Kh[:, Kd * hh + 128 * a_:Kd * hh + 128 * (a_ + 1)],
                   dl[:, 0:512], False, True, ("Kh", dln), ("pdel",))
            for a_ in range(2):
                cp("ve", t_S[:, 1024 * hh + 512 * a_:1024 * hh + 512 * (a_ + 1)],
                   pdel[:, 512 * a_:512 * (a_ + 1)], ("pdel",), (f"S{hh}",))
            if ti == 0 and ci == 0 and hh == 0:
                def dtap2(nm, srcap, rdname, ncols):
                    if nm in tap_outs:
                        for c0 in range(0, ncols, 512):
                            w = min(512, ncols - c0)
                            cp("ve", t_sbc[:, 0:w], srcap[:, c0:c0 + w],
                               (rdname,), ("sbc",))
                            dma("gp", tap_outs[nm][:, c0:c0 + w],
                                t_sbc[:, 0:w], ("sbc",), (nm + "_tap",))
                dtap2("dRp", rp, rpn, 768)
                dtap2("dDl", dl, dln, 768)
                dtap2("dKh", t_Kh, "Kh", 1024)
                dtap2("dWp", wp, wpn, 512)
                dtap2("dS", t_S, "S0", 4096)

    def o_gate_enc_z(ti):
        t0 = ti * TILE
        pzero(pb3, "pb3")
        for vc in range(VCH):
            cs = slice(TILE * vc, TILE * (vc + 1))
            sq, sqn = r_sq[vc % 2], f"rsq{vc % 2}"
            tt("ve", sq[:, :], t_o[:, cs], t_o[:, cs], ALU.mult, ("o",),
               (sqn,))
            mm(pb3[0:17, 0:TILE], t_hotv[:, 17 * vc:17 * (vc + 1)], sq[:, :],
               False, False, ("hotv", sqn), ("pb3",))
        cp("ve", t_rstage[0:17, :], pb3[0:17, 0:TILE], ("pb3",), ("rstage",))
        for ci in range(NCH):
            cur = ci % 2
            col, coln = r_cols[cur], f"rcols{cur}"
            rblk, rbn = r_rblk[cur], f"rrblk{cur}"
            ccs = slice(ci * C, (ci + 1) * C)
            trans(pb4[:, 0:128], t_rstage[:, ccs], t_idf[:, :],
                  ("rstage", "idf"), ("pb4",))
            cp("ve", col[:, 0:21], pb4[:, 0:21], ("pb4",), (coln,))
            act(col[:, C_T1:C_T1 + 4], col[:, C_SSO:C_SSO + 4], AF.Ln,
                (coln, "eps6"), (coln,), bias=t_eps6[:, :], scale=1.0 / Vd)
            act(col[:, CS_B:CS_B + 4], col[:, C_T1:C_T1 + 4], AF.Exp,
                (coln,), (coln,), scale=-0.5)
            trans(pb4[0:16, 128:256], col[:, SEL0:SEL0 + 16], t_idf[:, :],
                  (coln, "idf"), ("pb4",))
            cp("ve", rblk[0:16, :], pb4[0:16, 128:256], ("pb4",),
               (f"orb{ci}",))
            cp("ve", t_sbc[0:16, 128 * ci:128 * (ci + 1)], rblk[0:16, :],
               (f"orb{ci}",), ("osbc",))
        for hh in range(H):
            for ci in range(NCH):
                mm(pb3[:, 128 * ci:128 * (ci + 1)],
                   t_selo[:, 128 * (P_B + hh):128 * (P_B + hh + 1)],
                   t_sbc[0:16, 128 * ci:128 * (ci + 1)], True, True,
                   ("selo", "osbc"), ("pb3",))
            for jv in range(4):
                vc = 4 * hh + jv
                cs = slice(TILE * vc, TILE * (vc + 1))
                sl, sn = r_wsl[vc % 4], f"rwsl{vc % 4}"
                w_slab(sl, sn, wg_b, "wh3", vc, DCH, VAL)
                pb, pn = (pb0, "pb0") if vc % 2 == 0 else (pb1, "pb1")
                for ic in range(DCH):
                    mm(pb[:, 0:TILE], sl[:, 128 * ic:128 * (ic + 1)],
                       t_h[:, TILE * ic:TILE * (ic + 1)], ic == 0,
                       ic == DCH - 1, (sn, "h"), (pn,))
                ga, gan = r_ga[vc % 2], f"rga{vc % 2}"
                act(ga[:, :], pb[:, 0:TILE], AF.Silu, (pn,), (gan,))
                sq, sqn = r_sq[vc % 2], f"rsq{vc % 2}"
                tt("ve", sq[:, :], t_o[:, cs], pb3[:, 0:TILE], ALU.mult,
                   ("o", "pb3"), (sqn,))
                stt("ve", t_o[:, cs], sq[:, :], t_onwc[:, jv:jv + 1],
                    ga[:, :], ALU.mult, ALU.mult, (sqn, "onwc", gan), ("o",))
        if "gt" in tap_outs:
            dma("gp", tap_outs["gt"].rearrange("(c p) t -> p c t", p=128)
                [:, :, t0:t0 + TILE],
                t_o[:, :].rearrange("p (c t) -> p c t", c=VCH), ("o",),
                ("gttap",), force_sig=True)
        for q_ in range(2):
            pbs = [(pb0, "pb0"), (pb1, "pb1"), (pb2, "pb2"), (pb4, "pb4")]
            for vc in range(VCH):
                sl, sn = r_wo2[vc % 2], f"rwo2{vc % 2}"
                src = wo_b.rearrange("(vc p) o -> p vc o", p=128)[
                    :, vc:vc + 1, 512 * q_:512 * (q_ + 1)]
                dma("sy", sl[:, 0:512].rearrange("p (a j) -> p a j", a=1),
                    src, (sn, "wh4"), (sn,))
                for jo in range(4):
                    oc = 4 * q_ + jo
                    pb, pn = pbs[jo]
                    mm(pb[:, 0:TILE], sl[:, 128 * jo:128 * (jo + 1)],
                       t_o[:, TILE * vc:TILE * (vc + 1)], vc == 0, False,
                       (sn, "o"), (pn,))
            for jo in range(4):
                oc = 4 * q_ + jo
                pb, pn = pbs[jo]
                for ci in range(NCH):
                    mm(pb[:, 128 * ci:128 * (ci + 1)],
                       t_selo[:, 128 * P_ROWD:128 * (P_ROWD + 1)],
                       t_hrb[0:16, 128 * ci:128 * (ci + 1)], False,
                       ci == NCH - 1, ("selo", "hrb"), (pn,))
            for jo in range(4):
                oc = 4 * q_ + jo
                pb, pn = pbs[jo]
                cs = slice(TILE * oc, TILE * (oc + 1))
                sq, sqn = r_sq[jo % 2], f"rsq{jo % 2}"
                for ci in range(NCH):
                    mm(pb3[:, 128 * ci:128 * (ci + 1)],
                       t_selo[:, 128 * P_S:128 * (P_S + 1)],
                       t_hrb[0:16, 128 * ci:128 * (ci + 1)], True, True,
                       ("selo", "hrb"), ("pb3",))
                tt("ve", r_32[jo % 2][:, :], t_x1[:, cs], pb3[:, 0:TILE],
                   ALU.mult, ("x1", "pb3"), (f"r32{jo % 2}",))
                tt("ve", t_z[:, cs], r_32[jo % 2][:, :], pb[:, 0:TILE],
                   ALU.add, (f"r32{jo % 2}", pn), ("z",))
                cp("ve", t_zb[:, cs], t_z[:, cs], ("z",), ("zb",))
        if "z" in tap_outs:
            dma("gp", x_3d(tap_outs["z"], t0), sb_3d(t_z), ("z",),
                ("ztap",), force_sig=True)

    def w2_stage(ti):
        t0 = ti * TILE
        pzero(pb3, "pb3")
        stats_group_bf(t_zb, "zb", "z")
        cp("ve", t_rstage[0:19, :], pb3[0:19, 0:TILE], ("pb3",), ("rstage",))
        for ci in range(NCH):
            cur = ci % 2
            col, coln = r_cols[cur], f"rcols{cur}"
            rblk, rbn = r_rblk[cur], f"rrblk{cur}"
            ccs = slice(ci * C, (ci + 1) * C)
            trans(pb4[:, 0:128], t_rstage[:, ccs], t_idf[:, :],
                  ("rstage", "idf"), ("pb4",))
            cp("ve", col[:, 0:21], pb4[:, 0:21], ("pb4",), (coln,))
            ts("ve", col[:, C_M:C_M + 1], col[:, C_SZ:C_SZ + 1], 1.0 / D,
               None, ALU.mult, ALU.mult, (coln,), (coln,))
            ts("ve", col[:, C_T1:C_T1 + 1], col[:, C_SZ2:C_SZ2 + 1], 1.0 / D,
               None, ALU.mult, ALU.mult, (coln,), (coln,))
            tt("ve", col[:, C_T2:C_T2 + 1], col[:, C_M:C_M + 1],
               col[:, C_M:C_M + 1], ALU.mult, (coln,), (coln,))
            tt("ve", col[:, C_V:C_V + 1], col[:, C_T1:C_T1 + 1],
               col[:, C_T2:C_T2 + 1], ALU.subtract, (coln,), (coln,))
            act(col[:, C_T3:C_T3 + 1], col[:, C_V:C_V + 1], AF.Ln,
                (coln, "eps5"), (coln,), bias=t_eps5[:, :])
            act(col[:, CS_X1:CS_X1 + 1], col[:, C_T3:C_T3 + 1], AF.Exp,
                (coln,), (coln,), scale=-0.5)
            tt("ve", col[:, CS_X2:CS_X2 + 1], col[:, C_M:C_M + 1],
               col[:, CS_X1:CS_X1 + 1], ALU.mult, (coln,), (coln,))
            ts("ve", col[:, CS_X2:CS_X2 + 1], col[:, CS_X2:CS_X2 + 1], -1.0,
               None, ALU.mult, ALU.mult, (coln,), (coln,))
            trans(pb4[0:16, 128:256], col[:, SEL0:SEL0 + 16], t_idf[:, :],
                  (coln, "idf"), ("pb4",))
            cp("ve", t_zrb[0:16, 128 * ci:128 * (ci + 1)],
               pb4[0:16, 128:256], ("pb4",), ("zrb",))
        for ci in range(NCH):
            mm(pb4[:, 128 * ci:128 * (ci + 1)],
               t_selo[:, 128 * P_X1:128 * (P_X1 + 1)],
               t_zrb[0:16, 128 * ci:128 * (ci + 1)], True, True,
               ("selo", "zrb"), ("pb4",))
        cp("ve", t_sbc[:, 0:TILE], pb4[:, 0:TILE], ("pb4",), ("sbc",))
        if upto == "S1v":
            return
        for oc in range(DCH):
            sl, sn = r_wsl[oc % 4], f"rwsl{oc % 4}"
            w_slab(sl, sn, w2p_b, "wh5", oc, DCH, D)
            pb, pn = (pb0, "pb0") if oc % 2 == 0 else (pb1, "pb1")
            for ic in range(DCH):
                mm(pb[:, 0:TILE], sl[:, 128 * ic:128 * (ic + 1)],
                   t_zb[:, TILE * ic:TILE * (ic + 1)], ic == 0,
                   ic == DCH - 1, (sn, "zb"), (pn,))
            for ci in range(NCH):
                mm(pb2[:, 128 * ci:128 * (ci + 1)],
                   t_selw2[:, 128 * oc:128 * (oc + 1)],
                   t_zrb[0:16, 128 * ci:128 * (ci + 1)], True, True,
                   ("selw2", "zrb"), ("pb2",))
            cs = slice(TILE * oc, TILE * (oc + 1))
            ff, fn_ = r_ff[oc % 2], f"rff{oc % 2}"
            tt("ve", ff[:, :], pb[:, 0:TILE], t_sbc[:, 0:TILE], ALU.mult,
               (pn, "sbc"), (fn_,))
            gl, gln = r_ga[oc % 2], f"rga{oc % 2}"
            sq2, sq2n = r_sq[oc % 2], f"rsq{oc % 2}"
            tt("ve", sq2[:, :], ff[:, :], pb2[:, 0:TILE], ALU.add,
               (fn_, "pb2"), (sq2n,))
            act(gl[:, :], sq2[:, :], AF.Gelu, (sq2n, "gelb"), (gln,),
                bias=t_gelb[:, oc:oc + 1])
            tt("ve", t_x1[:, cs], t_x1[:, cs], t_z[:, cs], ALU.add,
               ("x1", "z"), ("x1",))
            tt("ve", t_z[:, cs], t_x1[:, cs], gl[:, :], ALU.add,
               ("x1", gln), ("z",))
            if upto != "S1w1":
                dma("gp", x2sp[128 * oc:128 * (oc + 1), t0:t0 + TILE],
                    t_z[:, cs], ("z",), (f"x2w{ti}_{oc}",), force_sig=True)
        if upto == "S1w1":
            return
        if "x2f" in tap_outs:
            dma("gp", x_3d(tap_outs["x2f"], t0), sb_3d(t_z), ("z",),
                ("x2tap",), force_sig=True)
        if upto == "S1w":
            return
        pzero(pb3, "pb3")
        for c in range(DCH):
            cs = slice(TILE * c, TILE * (c + 1))
            sq, sqn = r_32[c % 2], f"r32{c % 2}"
            tt("ve", sq[:, :], t_z[:, cs], t_z[:, cs], ALU.mult, ("z",),
               (sqn,))
            mm(pb3[0:20, 0:TILE], t_onespad[:, 8:28], t_z[:, cs],
               False, False, ("onespad", "z"), ("pb3",))
            mm(pb3[0:21, 0:TILE], t_onespad[:, 28:49], sq[:, :],
               False, c == DCH - 1, ("onespad", sqn), ("pb3",))
        cp("ve", t_rstage[0:21, :], pb3[0:21, 0:TILE], ("pb3",), ("rstage",))
        if upto == "S1x":
            return
        for ci in range(NCH):
            cur = ci % 2
            col, coln = r_cols[cur], f"rcols{cur}"
            ccs = slice(ci * C, (ci + 1) * C)
            trans(pb4[:, 0:128], t_rstage[:, ccs], t_idf[:, :],
                  ("rstage", "idf"), ("pb4",))
            cp("ve", col[:, 0:21], pb4[:, 0:21], ("pb4",), (coln,))
            ts("ve", col[:, C_M:C_M + 1], col[:, C_SY:C_SY + 1], 1.0 / D,
               None, ALU.mult, ALU.mult, (coln,), (coln,))
            ts("ve", col[:, C_T1:C_T1 + 1], col[:, C_SY2:C_SY2 + 1], 1.0 / D,
               None, ALU.mult, ALU.mult, (coln,), (coln,))
            tt("ve", col[:, C_T2:C_T2 + 1], col[:, C_M:C_M + 1],
               col[:, C_M:C_M + 1], ALU.mult, (coln,), (coln,))
            tt("ve", col[:, C_V:C_V + 1], col[:, C_T1:C_T1 + 1],
               col[:, C_T2:C_T2 + 1], ALU.subtract, (coln,), (coln,))
            act(col[:, C_T3:C_T3 + 1], col[:, C_V:C_V + 1], AF.Ln,
                (coln, "eps5"), (coln,), bias=t_eps5[:, :])
            act(col[:, CS_X1:CS_X1 + 1], col[:, C_T3:C_T3 + 1], AF.Exp,
                (coln,), (coln,), scale=-0.5)
            tt("ve", col[:, CS_X2:CS_X2 + 1], col[:, C_M:C_M + 1],
               col[:, CS_X1:CS_X1 + 1], ALU.mult, (coln,), (coln,))
            ts("ve", col[:, CS_X2:CS_X2 + 1], col[:, CS_X2:CS_X2 + 1], -1.0,
               None, ALU.mult, ALU.mult, (coln,), (coln,))
            trans(pb4[0:16, 128:256], col[:, SEL0:SEL0 + 16], t_idf[:, :],
                  (coln, "idf"), ("pb4",))
            cp("ve", t_zrb[0:16, 128 * ci:128 * (ci + 1)],
               pb4[0:16, 128:256], ("pb4",), ("zrb",))
            mm(pb4[:, 256:384], t_selo[:, 128 * P_X1:128 * (P_X1 + 1)],
               t_zrb[0:16, 128 * ci:128 * (ci + 1)], True, True,
               ("selo", "zrb"), ("pb4",))
            mm(pb4[:, 384:512], t_selo[:, 128 * P_X2:128 * (P_X2 + 1)],
               t_zrb[0:16, 128 * ci:128 * (ci + 1)], True, True,
               ("selo", "zrb"), ("pb4",))
            for dc in range(DCH):
                cs2 = slice(TILE * dc + ci * C, TILE * dc + (ci + 1) * C)
                jj = dc % 2
                tt("ve", r_32[jj][:, 0:C], t_z[:, cs2], pb4[:, 256:384],
                   ALU.mult, ("z", "pb4"), (f"r32{jj}",))
                tt("ve", r_32[jj][:, 0:C], r_32[jj][:, 0:C], pb4[:, 384:512],
                   ALU.add, (f"r32{jj}", "pb4"), (f"r32{jj}",))
                red(t_am2[:, 16 + ci * DCH + dc:17 + ci * DCH + dc],
                    r_32[jj][:, 0:C], ALU.max,
                    True, (f"r32{jj}",), (f"bm_{ti}_{ci}_{dc}",))
                dma("gp", lnx2sp[128 * dc:128 * (dc + 1),
                                 t0 + ci * C:t0 + (ci + 1) * C],
                    r_32[jj][:, 0:C], (f"r32{jj}",),
                    (f"lxw{ti}_{ci}_{dc}",), force_sig=True)
        red(t_am2[:, 8 + ti:9 + ti], t_am2[:, 16:16 + NCH * DCH], ALU.max,
            False,
            tuple(f"bm_{ti}_{ci}_{dc}" for ci in range(NCH)
                  for dc in range(DCH)), (f"bmT{ti}",))

    for ti in range(NT):
        sweep1_tile(ti)
        if upto.startswith("S1") and upto != "S1" and ti == 0:
            break
    if upto.startswith("S1"):
        for oc in range(DCH):
            dma("gp", out_d[128 * oc:128 * (oc + 1), 0:TILE],
                t_rstage[:, 0:TILE], ("rstage",), ("outdummy",))
        n = p.build(nc, sems, spacers, es)
        import sys
        print(f"[kernel] built {n} ops ({upto})", file=sys.stderr)
        es.close()
        return nc, tap_outs, p
    amax_collective(2)

    def sweep2_tile(ti):
        t0 = ti * TILE
        dma("sy", sb_3d(t_z), x_3d(lnx2sp, t0),
            ("z",) + tuple(f"lxw{t}_{ci}_{dc}" for t in range(NT)
                           for ci in range(NCH)
                           for dc in range(DCH)), ("z",))
        act(t_q8[:, :], t_z[:, :], AF.Copy, ("z", "sxh2"), ("q8t",),
            scale=t_sx2h[:, :])
        def resid_x2(oc):
            rx, rxn = r_x0[oc % 2], f"rx0{oc % 2}"
            dma("sy", rx[:, :], x2sp[128 * oc:128 * (oc + 1), t0:t0 + TILE],
                (rxn,) + tuple(f"x2w{t}_{k}" for t in range(NT)
                               for k in range(DCH)), (rxn,))
            return rx[:, :], rxn
        for oc in range(DCH):
            sl, sn = r_f8[oc % 3], f"rf8{oc % 3}"
            w_slab(sl, sn, wf8sp, "wf8", oc, DCH, D)
            pb, pn = (pb0, "pb0") if oc % 2 == 0 else (pb1, "pb1")
            for ic in range(DCH):
                mm(pb[:, 0:TILE], sl[:, 128 * ic:128 * (ic + 1)],
                   t_q8[:, TILE * ic:TILE * (ic + 1)], ic == 0,
                   ic == DCH - 1, (sn, "q8t"), (pn,))
            ff, fn_ = r_ff[oc % 2], f"rff{oc % 2}"
            ts("ve", ff[:, :], pb[:, 0:TILE], t_f2[:, :],
               t_fbc[:, oc:oc + 1], ALU.mult, ALU.add, (pn, "fd2", "fbc"),
               (fn_,))
            rx, rxn = resid_x2(oc)
            jj = oc % 2
            stt("ve", r_32[jj][:, :], ff[:, :], 0.5, rx, ALU.mult, ALU.add,
                (fn_, rxn), (f"r32{jj}",))
            dma("gp", out_d[128 * oc:128 * (oc + 1), t0:t0 + TILE],
                r_32[jj][:, :], (f"r32{jj}",), (f"outw{ti}_{oc}",),
                force_sig=True)

    for ti in range(NT):
        sweep2_tile(ti)

    n = p.build(nc, sems, spacers, es)
    import sys
    print(f"[kernel] built {n} ops", file=sys.stderr)
    es.close()
    return nc, tap_outs, p


def _prep_inputs(inputs, core):
    bf = ml_dtypes.bfloat16
    x = np.asarray(inputs["L0_attn_x"], np.float32)
    d = {}
    d["x0"] = np.ascontiguousarray(x[core].T)
    d["wq"] = np.asarray(inputs["L0_attn_Wq"], np.float32)
    d["wk"] = np.asarray(inputs["L0_attn_Wk"], np.float32)
    d["wv"] = np.asarray(inputs["L0_attn_Wv"], np.float32)
    d["wg"] = np.asarray(inputs["L0_attn_Wg"], np.float32)
    d["wo"] = np.asarray(inputs["L0_attn_Wo"], np.float32)
    d["w2"] = np.asarray(inputs["L0_attn_W2"], np.float32)
    d["wfT"] = np.ascontiguousarray(
        np.asarray(inputs["L0_ffn_weight"], np.float32).T)
    wa = np.asarray(inputs["L0_attn_Wa"], np.float32)
    wb = np.asarray(inputs["L0_attn_Wb"], np.float32)
    wabp = np.zeros((DCH, 128, 68), np.float32)
    wabp[:, :, 32:36] = wa.reshape(DCH, 128, 4)
    wabp[:, :, 64:68] = wb.reshape(DCH, 128, 4)
    d["wabp"] = np.ascontiguousarray(
        wabp.transpose(1, 0, 2).reshape(128, 68 * DCH)).astype(bf)
    for nm, key, w in (("cqw", "L0_attn_conv_q_w", KEY),
                       ("ckw", "L0_attn_conv_k_w", KEY),
                       ("cvw", "L0_attn_conv_v_w", VAL)):
        cwt = np.asarray(inputs[key], np.float32)
        nch = w // 128
        d[nm] = np.ascontiguousarray(
            cwt.reshape(nch, 128, 4).transpose(1, 0, 2).reshape(128, 4 * nch))
    def cols(v):
        v = np.asarray(v, np.float32)
        n = v.shape[0] // 128
        return np.ascontiguousarray(v.reshape(n, 128).T)
    d["g1c"] = cols(inputs["L0_attn_ln1_g"])
    d["b1c"] = cols(inputs["L0_attn_ln1_b"])
    d["rmc"] = cols(inputs["L0_attn_rms_in_w"])
    d["g2c"] = cols(inputs["L0_attn_ln2_g"])
    d["b2wc"] = cols(inputs["L0_attn_b2"])
    d["b2c"] = cols(inputs["L0_attn_ln2_b"])
    d["fbc"] = cols(inputs["L0_ffn_bias"])
    d["onwc"] = cols(inputs["L0_attn_o_norm_w"])
    dtb = np.zeros((128, 1), np.float32)
    dtb[32:36, 0] = np.asarray(inputs["L0_attn_dt_bias"], np.float32)
    d["dtbp"] = dtb
    alo = np.zeros((128, 1), np.float32)
    alo[32:36, 0] = np.asarray(inputs["L0_attn_A_log"], np.float32)
    d["alogp"] = alo
    d["idf"] = np.eye(128, dtype=np.float32)
    d["idb"] = np.eye(128).astype(bf)
    d["onesr"] = np.ones((1, 128), np.float32)
    d["onescb"] = np.ones((128, 1)).astype(bf)
    d["negbig"] = (np.eye(128) * -1e30).astype(bf)
    killY = np.tril(np.ones((128, 128), np.float32))
    killM = np.tril(np.ones((128, 128), np.float32), k=-1)
    d["maskY4"] = np.tile(killY, (1, 4)).astype(bf)
    d["maskM4"] = np.tile(killM, (1, 4)).astype(bf)
    hq = np.zeros((128, 9 * DCH), np.float32)
    for c in range(DCH):
        hq[:, 9 * c + RW_SSQ + c // 2] = 1.0
    d["hotq"] = hq.astype(bf)
    hk = np.zeros((128, 13 * DCH), np.float32)
    for c in range(DCH):
        hk[:, 13 * c + RW_SSK + c // 2] = 1.0
    d["hotk"] = hk.astype(bf)
    hv = np.zeros((128, 17 * VCH), np.float32)
    for c in range(VCH):
        hv[:, 17 * c + RW_SSO + c // 4] = 1.0
    d["hotv"] = hv.astype(bf)
    selo = np.zeros((16, 16, 128), np.float32)
    for j in range(16):
        selo[j, j, :] = 1.0
    d["selo_f"] = np.ascontiguousarray(
        selo.transpose(1, 0, 2).reshape(16, 16 * 128))
    onespad = np.zeros((128, 49), np.float32)
    onespad[:, 0] = 1.0; onespad[:, 7] = 1.0
    onespad[:, 27] = 1.0; onespad[:, 48] = 1.0
    d["onespad"] = onespad
    e127 = np.zeros((128, 1), np.float32)
    e127[127, 0] = 1.0
    d["e127"] = e127
    return d


def kernel(**inputs):
    nc, _, _ = build_program("full")
    in_maps = [_prep_inputs(inputs, c) for c in range(B)]
    res = run_bass_kernel_spmd(nc, in_maps, core_ids=list(range(B)))
    outs = [np.asarray(r["out"]).T for r in res.results]
    return np.ascontiguousarray(np.stack(outs, axis=0))
